# version 02
# baseline (speedup 1.0000x reference)
import contextlib
import math
import numpy as np

import concourse.bass as bass
import concourse.mybir as mybir
from concourse.bass_utils import run_bass_kernel_spmd

F32 = mybir.dt.float32
BF16 = mybir.dt.bfloat16
I32 = mybir.dt.int32
AF = mybir.ActivationFunctionType
ALU = mybir.AluOpType

T = 2048
D = 1024
NTT = T // 128
EPS = 1e-5


class P:

    def __init__(self):
        self.nc = bass.Bass("TRN2", target_bir_lowering=False)
        self._sems = {}
        self._uid = 0
        self.stack = contextlib.ExitStack()

    def uid(self, pfx):
        self._uid += 1
        return "%s_%d" % (pfx, self._uid)

    def _sem(self, name):
        if name not in self._sems:
            self._sems[name] = [self.nc.alloc_semaphore(name), 0]
        return self._sems[name]

    def inc(self, name, instr, k):
        s = self._sem(name)
        instr.then_inc(s[0], k)
        s[1] += k
        return s[1]

    def val(self, name):
        return self._sem(name)[1]

    def wait(self, eng, name, v=None):
        s = self._sem(name)
        eng.wait_ge(s[0], s[1] if v is None else v)

    def sb(self, pfx, shape, dtype):
        return self.stack.enter_context(
            self.nc.sbuf_tensor(self.uid(pfx), shape, dtype)
        ).ap()

    def ps(self, pfx, shape, dtype=F32):
        return self.stack.enter_context(
            self.nc.psum_tensor(self.uid(pfx), shape, dtype)
        ).ap()

    def dram(self, pfx, shape, dtype=F32, kind="Internal"):
        return self.nc.dram_tensor(self.uid(pfx), shape, dtype, kind=kind).ap()

    def vop(self, fn, *args, **kw):
        self.wait(self.vector, "V")
        return self.inc("V", fn(*args, **kw), 1)

    def aop(self, fn, *args, **kw):
        self.wait(self.scalar, "A")
        return self.inc("A", fn(*args, **kw), 1)

    def gop(self, fn, *args, **kw):
        self.wait(self.gpsimd, "G")
        return self.inc("G", fn(*args, **kw), 1)

    @property
    def tensor(self):
        return self.nc.tensor

    @property
    def vector(self):
        return self.nc.vector

    @property
    def scalar(self):
        return self.nc.scalar

    @property
    def gpsimd(self):
        return self.nc.gpsimd

    @property
    def sync(self):
        return self.nc.sync


def load_weight_bf16(p, dram_ap, sb_ap, sem="W"):
    instr = p.gpsimd.dma_start(out=sb_ap, in_=dram_ap)
    return p.inc(sem, instr, 16)


def emit_ln_fm(p, x_dram, out_fm, ident, nch=8, out_tm=None, gw=None, dbg=None,
               hook=None, wrep2=None, brep2=None):
    with contextlib.ExitStack() as _st:
        _old = p.stack
        p.stack = _st
        r = _emit_ln_fm_inner(p, x_dram, out_fm, ident, nch, out_tm, gw, dbg, hook,
                              wrep2, brep2)
        p.stack = _old
    return r


def _emit_ln_fm_inner(p, x_dram, out_fm, ident, nch, out_tm, gw, dbg=None, hook=None,
                      wrep2=None, brep2=None):
    Dp = nch * 128
    xt = [p.sb("lnx", [128, Dp], F32) for _ in range(2)]
    sq = p.sb("lnsq", [128, Dp], BF16)
    s1 = p.sb("lns1", [128, 1], F32)
    s2 = p.sb("lns2", [128, 1], F32)
    mean = p.sb("lnmean", [128, 1], F32)
    nm2 = p.sb("lnnm2", [128, 1], F32)
    var = p.sb("lnvar", [128, 1], F32)
    rstd = p.sb("lnrstd", [128, 1], F32)
    xn = p.sb("lnxn", [128, Dp], BF16)
    pt = [p.ps("lnps", [128, Dp], BF16) for _ in range(2)]
    last_consume = [[("V", 0)], [("V", 0)]]
    last_evac = [0, 0]
    s2_cons = 0
    nt = T // 128
    for tt in range(nt):
        b = tt % 2
        for (snm, scnt) in last_consume[b]:
            p.wait(p.sync, snm, scnt)
        ld = p.inc("L", p.sync.dma_start(out=xt[b], in_=x_dram[tt * 128:(tt + 1) * 128, :]), 16)
        p.wait(p.vector, "L", ld)
        p.vop(p.vector.reduce_sum, s1, xt[b], axis=mybir.AxisListType.X)
        p.wait(p.scalar, "L", ld)
        p.wait(p.scalar, "V", s2_cons)
        a1 = p.aop(p.scalar.activation, out=sq, in_=xt[b], func=AF.Square, accum_out=s2)
        p.vop(p.vector.tensor_scalar_mul, mean, s1, 1.0 / Dp)
        p.wait(p.vector, "A", a1)
        p.vop(p.vector.tensor_scalar, nm2, s1, mean, -1.0 / Dp, ALU.mult, ALU.mult)
        s2_cons = p.vop(p.vector.tensor_scalar, var, s2, 1.0 / Dp, nm2, ALU.mult, ALU.add)
        v1 = p.vop(p.vector.tensor_scalar_add, var, var, EPS)
        p.wait(p.scalar, "V", v1)
        p.aop(p.scalar.activation, out=rstd, in_=var, func=AF.Ln)
        a2 = p.aop(p.scalar.activation, out=rstd, in_=rstd, func=AF.Exp, scale=-0.5)
        if dbg is not None:
            p.wait(p.vector, "A", a2)
            for k, src in enumerate((s1, s2, mean, nm2, var, rstd)):
                p.vop(p.vector.tensor_copy, dbg[:, tt, k:k + 1], src)
        p.wait(p.vector, "A", a2)
        vap = p.vop(p.vector.tensor_scalar, xn, xt[b], mean, rstd, ALU.subtract, ALU.mult)
        if wrep2 is not None:
            vap = p.vop(p.vector.tensor_mul, xn, xn, wrep2)
        if brep2 is not None:
            vap = p.vop(p.vector.tensor_add, xn, xn, brep2)
        last_consume[b] = [("V", vap)]
        if hook is not None:
            last_consume[b] = last_consume[b] + list(hook(tt, xt[b], xn, mean, p.val("L")))
        if gw is not None:
            vap = p.vop(p.vector.tensor_tensor,
                        xn, xn, gw.broadcast_to((128, Dp)), ALU.mult)
        if out_tm is not None:
            vap = p.vop(p.vector.tensor_copy, out_tm[:, tt, :], xn)
        p.wait(p.tensor, "V", vap)
        p.wait(p.tensor, "V", last_evac[b])
        for k in range(nch):
            ins = p.tensor.transpose(pt[b][:, k * 128:(k + 1) * 128],
                                     xn[:, k * 128:(k + 1) * 128], ident)
        pe = p.inc("P", ins, 1)
        p.wait(p.vector, "P", pe)
        ev = p.vop(p.vector.tensor_copy,
                   out_fm[:, :, tt * 128:(tt + 1) * 128],
                   pt[b].rearrange("p (c f) -> p c f", f=128))
        last_evac[b] = ev
    return p.val("V")


def emit_mm_fm(p, w_sb, x_fm, kchunks, mchunks, evac, n_tile=512):
    with contextlib.ExitStack() as _st:
        _o = p.stack
        p.stack = _st
        nps = 2
        pst = [p.ps("mmps", [128, n_tile], F32) for _ in range(nps)]
        entry_v = p.val("V")
        last_read = [entry_v] * nps
        which = ["V"] * nps
        i = 0
        for mc in range(mchunks):
            for n0 in range(0, T, n_tile):
                b = i % nps
                i += 1
                p.wait(p.tensor, which[b], last_read[b])
                for kc in range(kchunks):
                    ins = p.tensor.matmul(
                        pst[b][:, 0:n_tile],
                        w_sb[:, kc, mc * 128:(mc + 1) * 128],
                        x_fm[:, kc, n0:n0 + n_tile],
                        start=(kc == 0), stop=(kc == kchunks - 1))
                pe = p.inc("P", ins, 1)
                sem_name, cnt = evac(mc, n0, pst[b], pe)
                which[b] = sem_name
                last_read[b] = cnt
        p.stack = _o


def emit_mm_tm(p, x_fm, w_sb, kchunks, dout, evac, psum_banks=2):
    with contextlib.ExitStack() as _st:
        _o = p.stack
        p.stack = _st
        n_tile = 512
        nps = psum_banks
        pst = [p.ps("mtps", [128, n_tile], F32) for _ in range(nps)]
        entry_v = p.val("V")
        last_read = [entry_v] * nps
        which = ["V"] * nps
        i = 0
        for tt in range(NTT):
            for n0 in range(0, dout, n_tile):
                b = i % nps
                i += 1
                p.wait(p.tensor, which[b], last_read[b])
                for kc in range(kchunks):
                    ins = p.tensor.matmul(
                        pst[b][:, 0:n_tile],
                        x_fm[:, kc, tt * 128:(tt + 1) * 128],
                        w_sb[:, kc, n0:n0 + n_tile],
                        start=(kc == 0), stop=(kc == kchunks - 1))
                pe = p.inc("P", ins, 1)
                sem_name, cnt = evac(tt, n0, pst[b], pe)
                which[b] = sem_name
                last_read[b] = cnt
        p.stack = _o


def emit_l1ffn(p, x_dram, out_dram, w_gate, w_up, w_down, ident):
    HID = 3072
    HC = HID // 128
    S127 = 1.0 / 127.0
    with contextlib.ExitStack() as st:
        old = p.stack
        p.stack = st

        h_fm = p.sb("hfm", [128, HC, T], BF16)

        with contextlib.ExitStack() as st1:
            p.stack = st1
            xn_fm = p.sb("xnfm", [128, 8, T], BF16)
            ln_done = emit_ln_fm(p, x_dram, xn_fm, ident)
            gtmp = p.sb("gtmp", [128, 512], F32)

            wg_r = w_gate.rearrange("(kc kp) n -> kp kc n", kp=128)
            wu_r = w_up.rearrange("(kc kp) n -> kp kc n", kp=128)
            wgi = [p.sb("wgi", [128, 8, 128], I32) for _ in range(2)]
            wui = [p.sb("wui", [128, 8, 128], I32) for _ in range(2)]
            wgb = [p.sb("wgb", [128, 8, 128], BF16) for _ in range(2)]
            wub = [p.sb("wub", [128, 8, 128], BF16) for _ in range(2)]
            wld = [0, 0]
            wcast = [ln_done, ln_done]
            wused = [0, 0]
            gtmp_cons = ln_done

            n_tile = 512
            pg = [p.ps("pg", [128, n_tile], F32) for _ in range(2)]
            pu = [p.ps("pu", [128, n_tile], F32) for _ in range(2)]
            last_read = [0, 0]

            def load_w(mc):
                wb = mc % 2
                p.wait(p.sync, "V", wcast[wb])
                p.inc("L", p.sync.dma_start(
                    out=wgi[wb], in_=wg_r[:, :, mc * 128:(mc + 1) * 128]), 16)
                wld[wb] = p.inc("L", p.sync.dma_start(
                    out=wui[wb], in_=wu_r[:, :, mc * 128:(mc + 1) * 128]), 16)

            def cast_w(mc):
                wb = mc % 2
                p.wait(p.vector, "L", wld[wb])
                p.wait(p.vector, "P", wused[wb])
                p.vop(p.vector.tensor_copy,
                      wgb[wb].rearrange("p a b -> p (a b)"),
                      wgi[wb].rearrange("p a b -> p (a b)"))
                wcast[wb] = p.vop(p.vector.tensor_copy,
                                  wub[wb].rearrange("p a b -> p (a b)"),
                                  wui[wb].rearrange("p a b -> p (a b)"))

            load_w(0)
            cast_w(0)
            i = 0
            for mc in range(HC):
                wb = mc % 2
                if mc + 1 < HC:
                    load_w(mc + 1)
                    cast_w(mc + 1)
                p.wait(p.tensor, "V", wcast[wb])
                for n0 in range(0, T, n_tile):
                    b = i % 2
                    i += 1
                    p.wait(p.tensor, "V", last_read[b])
                    for kc in range(8):
                        p.tensor.matmul(pg[b], wgb[wb][:, kc, :],
                                        xn_fm[:, kc, n0:n0 + n_tile],
                                        start=(kc == 0), stop=(kc == 7))
                    for kc in range(8):
                        ins = p.tensor.matmul(pu[b], wub[wb][:, kc, :],
                                              xn_fm[:, kc, n0:n0 + n_tile],
                                              start=(kc == 0), stop=(kc == 7))
                    pe = p.inc("P", ins, 1)
                    if n0 + n_tile == T:
                        wused[wb] = pe
                    p.wait(p.scalar, "P", pe)
                    p.wait(p.scalar, "V", gtmp_cons)
                    ac = p.aop(p.scalar.activation, out=gtmp, in_=pg[b],
                               func=AF.Silu, scale=S127)
                    p.wait(p.vector, "A", ac)
                    p.vop(p.vector.tensor_scalar_mul, gtmp, gtmp, S127 * S127)
                    cnt = p.vop(p.vector.tensor_mul,
                                h_fm[:, mc, n0:n0 + n_tile], gtmp, pu[b])
                    gtmp_cons = cnt
                    last_read[b] = cnt
            p.stack = st
        gate_done = p.val("V")

        wd = p.sb("wd", [128, HC, D], BF16)
        with contextlib.ExitStack() as st2:
            p.stack = st2
            wdi = p.sb("wdi", [128, 4, D], I32)
            wd_conv = gate_done
            for g4 in range(HC // 4):
                p.wait(p.sync, "V", wd_conv)
                ld = p.inc("L", p.sync.dma_start(
                    out=wdi,
                    in_=w_down.rearrange("(kc kp) n -> kp kc n", kp=128)[
                        :, g4 * 4:(g4 + 1) * 4, :]), 16)
                p.wait(p.vector, "L", ld)
                wd_conv = p.vop(p.vector.tensor_copy,
                                wd[:, g4 * 4:(g4 + 1) * 4, :], wdi)
            p.stack = st

        xres = [p.sb("xres", [128, D], F32) for _ in range(2)]
        ot = [p.sb("ot", [128, D], F32) for _ in range(2)]
        res_ld = [0] * NTT
        res_consumed = [wd_conv, wd_conv]
        st_done = [0, 0]

        def load_res(tt):
            b = tt % 2
            p.wait(p.sync, "V", res_consumed[b])
            res_ld[tt] = p.inc("L", p.sync.dma_start(
                out=xres[b], in_=x_dram[tt * 128:(tt + 1) * 128, :]), 16)

        load_res(0)
        load_res(1)

        def evac(tt, n0, ps_ap, pe_cnt):
            b = tt % 2
            p.wait(p.vector, "P", pe_cnt)
            p.wait(p.vector, "L", res_ld[tt])
            p.wait(p.vector, "S", st_done[b])
            cnt = p.vop(p.vector.tensor_add,
                        ot[b][:, n0:n0 + 512], xres[b][:, n0:n0 + 512], ps_ap)
            if n0 + 512 == D:
                res_consumed[b] = cnt
                p.wait(p.sync, "V", cnt)
                s = p.inc("S", p.sync.dma_start(
                    out=out_dram[tt * 128:(tt + 1) * 128, :], in_=ot[b]), 16)
                st_done[b] = s
                if tt + 2 < NTT:
                    load_res(tt + 2)
            return ("V", cnt)

        p.wait(p.tensor, "V", wd_conv)
        emit_mm_tm(p, h_fm, wd, HC, D, evac)
        p.wait(p.sync, "S")
        p.stack = old


def _dummy_out(p, x_dram, out_dram):
    tmp = p.sb("dmy", [128, D], F32)
    for tt in range(NTT):
        p.wait(p.sync, "S")
        p.inc("L", p.sync.dma_start(out=tmp, in_=x_dram[tt * 128:(tt + 1) * 128, :]), 16)
        p.wait(p.sync, "L")
        p.inc("S", p.sync.dma_start(out=out_dram[tt * 128:(tt + 1) * 128, :], in_=tmp), 16)
    p.wait(p.sync, "S")


def emit_attn(p, x_dram, out_dram, w, ident, ident_f32, dbg_phase=99):
    H = 16
    with contextlib.ExitStack() as st:
        old = p.stack
        p.stack = st
        xn_fm = p.sb("xnfm", [128, 8, T], BF16)
        q_fm = p.sb("qfm", [128, 8, T], BF16)
        k_fm = p.sb("kfm", [128, 8, T], BF16)
        v_tm = p.sb("vtm", [128, NTT, 1024], BF16)
        o_fm = p.sb("ofm", [128, 8, T], BF16)
        cext = p.sb("cext", [128, T], BF16)
        sext = p.sb("sext", [128, T], BF16)
        cmask = p.sb("cmask", [128, 128], F32)
        p.inc("W", p.gpsimd.dma_start(out=cext, in_=w["cext"]), 16)
        p.inc("W", p.gpsimd.dma_start(out=sext, in_=w["sext"]), 16)
        cw = p.inc("W", p.gpsimd.dma_start(out=cmask, in_=w["cmask"]), 16)

        ln_done = emit_ln_fm(p, x_dram, xn_fm, ident)
        if dbg_phase < 1:
            _dummy_out(p, x_dram, out_dram)
            p.stack = old
            return

        def build_eff(downT, upc, upe, eff, rot, gate_v):
            with contextlib.ExitStack() as s2:
                o2 = p.stack
                p.stack = s2
                dT = p.sb("dT", [128, 8, 1024], BF16)
                up2 = p.sb("up2", [128, 128], BF16)
                rtmp = p.sb("rtmp", [128, 32], BF16)
                pw = p.ps("pw", [128, 1024], F32)
                p.wait(p.gpsimd, "V", gate_v)
                p.inc("W", p.gpsimd.dma_start(
                    out=dT, in_=downT.rearrange("(kc kp) c -> kp kc c", kp=128)), 16)
                vz = p.vop(p.vector.memset, up2, 0.0)
                p.wait(p.gpsimd, "V", vz)
                if upc is None:
                    for half in (0, 64):
                        p.inc("W", p.gpsimd.dma_start(
                            out=up2[half:half + 64, half:half + 64], in_=upe), 16)
                else:
                    for half in (0, 64):
                        p.inc("W", p.gpsimd.dma_start(
                            out=up2[half:half + 64, half:half + 32], in_=upc), 16)
                        p.inc("W", p.gpsimd.dma_start(
                            out=rtmp[half:half + 64, :], in_=upe), 16)
                wle = p.val("W")
                p.wait(p.vector, "W", wle)
                if upc is not None:
                    for half in (0, 64):
                        ec = half + 32
                        ev2 = up2[half:half + 64, ec:ec + 32].rearrange(
                            "p (i two) -> p i two", two=2)
                        rv = rtmp[half:half + 64, :].rearrange(
                            "p (i two) -> p i two", two=2)
                        if rot:
                            p.vop(p.vector.tensor_scalar_mul,
                                  ev2[:, :, 0], rv[:, :, 1], -1.0)
                            p.vop(p.vector.tensor_copy, ev2[:, :, 1], rv[:, :, 0])
                        else:
                            p.vop(p.vector.tensor_copy,
                                  up2[half:half + 64, ec:ec + 32],
                                  rtmp[half:half + 64, :])
                wv = p.val("V")
                p.wait(p.tensor, "W", wle)
                p.wait(p.tensor, "V", wv)
                ev = 0
                for kcc in range(8):
                    p.wait(p.tensor, "V", ev)
                    for hp in range(H // 2):
                        ins = p.tensor.matmul(
                            pw[:, hp * 128:(hp + 1) * 128],
                            dT[:, hp, kcc * 128:(kcc + 1) * 128],
                            up2, start=True, stop=True)
                    pe = p.inc("P", ins, 1)
                    p.wait(p.vector, "P", pe)
                    ev = p.vop(p.vector.tensor_copy, eff[:, kcc, :], pw)
                p.stack = o2
            return ev

        def project_fm(eff, dst):
            def evac(mc, n0, ps_ap, pe_cnt):
                p.wait(p.vector, "P", pe_cnt)
                cnt = p.vop(p.vector.tensor_copy, dst[:, mc, n0:n0 + 512], ps_ap)
                return ("V", cnt)
            emit_mm_fm(p, eff, xn_fm, 8, 8, evac)

        with contextlib.ExitStack() as s3:
            o3 = p.stack
            p.stack = s3
            eff = p.sb("eff", [128, 8, 1024], BF16)
            rot_fm = o_fm
            gate = ln_done
            for (fam, dst) in (("q", q_fm), ("k", k_fm)):
                build_eff(w[fam + "dT"], w[fam + "_up_c"], w[fam + "_up_e"],
                          eff, rot=False, gate_v=gate)
                p.wait(p.tensor, "V")
                project_fm(eff, dst)
                build_eff(w[fam + "dT"], w[fam + "_up_c"], w[fam + "_up_e"],
                          eff, rot=True, gate_v=p.val("V"))
                p.wait(p.tensor, "V")
                project_fm(eff, rot_fm)
                p.wait(p.vector, "W", cw)
                for c in range(8):
                    p.vop(p.vector.tensor_tensor, rot_fm[:, c, :], rot_fm[:, c, :],
                          sext, ALU.mult)
                    p.vop(p.vector.tensor_tensor, dst[:, c, :], dst[:, c, :],
                          cext, ALU.mult)
                    p.vop(p.vector.tensor_add, dst[:, c, :], dst[:, c, :],
                          rot_fm[:, c, :])
                gate = p.val("V")
            build_eff(w["vdT"], None, w["v_up"], eff, rot=False, gate_v=gate)
            p.wait(p.tensor, "V")

            def evac_v(tt, n0, ps_ap, pe_cnt):
                p.wait(p.vector, "P", pe_cnt)
                cnt = p.vop(p.vector.tensor_copy, v_tm[:, tt, n0:n0 + 512], ps_ap)
                return ("V", cnt)
            emit_mm_tm(p, xn_fm, eff.rearrange("p a b -> p a b"), 8, 1024, evac_v)
            p.stack = o3
        qkv_done = p.val("V")
        if dbg_phase < 2:
            _dummy_out(p, x_dram, out_dram)
            p.stack = old
            return

        with contextlib.ExitStack() as s4:
            o4 = p.stack
            p.stack = s4
            ps_s = p.ps("pss", [128, T], F32)
            ps_pt = [p.ps("pspt", [128, 512], BF16) for _ in range(2)]
            ps_o = p.ps("pso", [64, 128], F32)
            p_sb = p.sb("psb", [128, T], BF16)
            pt_sb = [p.sb("ptsb", [128, 512], BF16) for _ in range(2)]
            rmax = p.sb("rmax", [128, 1], F32)
            nsm = p.sb("nsm", [128, 1], F32)
            lsum = p.sb("lsum", [128, 1], F32)
            rl = p.sb("rl", [128, 1], F32)
            s_free = ("V", qkv_done)
            o_free = ("V", 0)
            pt_free = [("V", 0), ("V", 0)]
            first = True
            pb = [0]

            def pe_base(b):
                if pb[0] != b:
                    p.wait(p.tensor, "P", p.val("P"))
                    pb[0] = b
            for h in range(min(H, getattr(emit_attn, "h_limit", H))):
                ch, base = h // 2, (h % 2) * 64
                for qt in range(min(NTT, getattr(emit_attn, "qt_limit", NTT))):
                    nk = (qt + 1) * 128
                    p.wait(p.tensor, *s_free)
                    if first:
                        p.wait(p.tensor, "V", qkv_done)
                        first = False
                    pe_base(base)
                    for n0 in range(0, nk, 512):
                        n1 = min(n0 + 512, nk)
                        ins = p.tensor.matmul(
                            ps_s[:, n0:n1],
                            q_fm[base:base + 64, ch, qt * 128:(qt + 1) * 128],
                            k_fm[base:base + 64, ch, n0:n1],
                            start=True, stop=True)
                    pe_s = p.inc("P", ins, 1)
                    p.wait(p.vector, "P", pe_s)
                    p.vop(p.vector.tensor_add, ps_s[:, nk - 128:nk],
                          ps_s[:, nk - 128:nk], cmask)
                    p.vop(p.vector.reduce_max, rmax, ps_s[:, 0:nk],
                          axis=mybir.AxisListType.X)
                    vm = p.vop(p.vector.tensor_scalar_mul, nsm, rmax, -0.125)
                    p.wait(p.scalar, "V", vm)
                    ae = p.aop(p.scalar.activation, out=p_sb[:, 0:nk],
                               in_=ps_s[:, 0:nk], func=AF.Exp, scale=0.125,
                               bias=nsm, accum_out=lsum)
                    s_free = ("A", ae)
                    p.wait(p.vector, "A", ae)
                    p.vop(p.vector.reciprocal, rl, lsum)
                    vp = p.vop(p.vector.tensor_scalar_mul, p_sb[:, 0:nk],
                               p_sb[:, 0:nk], rl)
                    nch = nk // 128
                    for g0 in range(0, nch, 4):
                        g1 = min(g0 + 4, nch)
                        bb = (g0 // 4) % 2
                        p.wait(p.tensor, *pt_free[bb])
                        p.wait(p.tensor, "V", vp)
                        pe_base(0)
                        for c in range(g0, g1):
                            ins = p.tensor.transpose(
                                ps_pt[bb][:, (c - g0) * 128:(c - g0 + 1) * 128],
                                p_sb[:, c * 128:(c + 1) * 128], ident)
                        pe_t = p.inc("P", ins, 1)
                        p.wait(p.vector, "P", pe_t)
                        ev = p.vop(p.vector.tensor_copy,
                                   pt_sb[bb][:, 0:(g1 - g0) * 128],
                                   ps_pt[bb][:, 0:(g1 - g0) * 128])
                        pt_free[bb] = ("V", ev)
                        p.wait(p.tensor, "V", ev)
                        if g0 == 0:
                            p.wait(p.tensor, *o_free)
                        for c in range(g0, g1):
                            ins = p.tensor.matmul(
                                ps_o,
                                v_tm[:, c, h * 64:(h + 1) * 64],
                                pt_sb[bb][:, (c - g0) * 128:(c - g0 + 1) * 128],
                                start=(c == 0), stop=(c == nch - 1))
                        pe_o = p.inc("P", ins, 1)
                    p.wait(p.vector, "P", pe_o)
                    vo = p.vop(p.vector.tensor_copy,
                               o_fm[base:base + 64, ch, qt * 128:(qt + 1) * 128],
                               ps_o)
                    o_free = ("V", vo)
            p.stack = o4
        attn_done = p.val("V")
        if dbg_phase < 3:
            _dummy_out(p, x_dram, out_dram)
            p.stack = old
            return

        with contextlib.ExitStack() as s5:
            o5 = p.stack
            p.stack = s5
            wc = p.sb("wc", [128, 8, 1024], BF16)
            p.wait(p.gpsimd, "V", attn_done)
            wcl = p.inc("W", p.gpsimd.dma_start(
                out=wc, in_=w["wc"].rearrange("(kc kp) n -> kp kc n", kp=128)), 16)
            p.wait(p.tensor, "W", wcl)
            xres = [p.sb("xres", [128, D], F32) for _ in range(2)]
            ot = [p.sb("ot", [128, D], F32) for _ in range(2)]
            res_ld = [0] * NTT
            res_consumed = [attn_done, attn_done]
            st_done = [0, 0]

            def load_res(tt):
                b = tt % 2
                p.wait(p.sync, "V", res_consumed[b])
                res_ld[tt] = p.inc("L", p.sync.dma_start(
                    out=xres[b], in_=x_dram[tt * 128:(tt + 1) * 128, :]), 16)

            load_res(0)
            load_res(1)

            def evac(tt, n0, ps_ap, pe_cnt):
                b = tt % 2
                p.wait(p.vector, "P", pe_cnt)
                p.wait(p.vector, "L", res_ld[tt])
                p.wait(p.vector, "S", st_done[b])
                cnt = p.vop(p.vector.tensor_add,
                            ot[b][:, n0:n0 + 512], xres[b][:, n0:n0 + 512], ps_ap)
                if n0 + 512 == D:
                    res_consumed[b] = cnt
                    p.wait(p.sync, "V", cnt)
                    s = p.inc("S", p.sync.dma_start(
                        out=out_dram[tt * 128:(tt + 1) * 128, :], in_=ot[b]), 16)
                    st_done[b] = s
                    if tt + 2 < NTT:
                        load_res(tt + 2)
                return ("V", cnt)

            p.wait(p.tensor, "V", attn_done)
            emit_mm_tm(p, o_fm, wc, 8, D, evac)
            p.wait(p.sync, "S")
            p.stack = o5
        p.stack = old


def emit_moe(p, x_dram, out_dram, gate_wt, expert_wt, ident, ident_f32):
    with contextlib.ExitStack() as st:
        old = p.stack
        p.stack = st
        xn_fm = p.sb("xnfm", [128, 8, T], BF16)
        acc = p.sb("acc", [128, NTT, D], F32)
        scores = p.sb("scores", [128, NTT, 8], F32)
        onehot = p.sb("onehot", [128, NTT, 8], F32)
        gwt = p.sb("gwt", [128, 8, 8], F32)
        srep = p.sb("srep", [128, 8], F32)
        onesk = p.sb("onesk", [128, 1], F32)
        onesr = p.sb("onesr", [1, 128], F32)
        srow = p.sb("srow", [1, 8], F32)
        xfs = p.sb("xfs", [128, 8, 128], F32)
        negmean = p.sb("negmean", [128, 1], F32)
        rm = p.sb("rm", [128, 1], F32)
        ptx = p.ps("ptx", [128, 1024], F32)
        psc = p.ps("psc", [128, 8], F32)
        psr = p.ps("psr", [128, 8], F32)

        gld = p.inc("L", p.sync.dma_start(
            out=gwt, in_=gate_wt.rearrange("(kc kp) e -> kp kc e", kp=128)), 16)
        p.vop(p.vector.memset, onesk, 1.0)
        p.vop(p.vector.memset, onesr, 1.0)
        p.wait(p.tensor, "L", gld)
        p.wait(p.tensor, "V")
        for kc in range(8):
            ins = p.tensor.matmul(psr[0:1, :], onesk, gwt[:, kc, :],
                                  start=(kc == 0), stop=(kc == 7))
        pe = p.inc("P", ins, 1)
        p.wait(p.vector, "P", pe)
        vs = p.vop(p.vector.tensor_copy, srow, psr[0:1, :])
        p.wait(p.tensor, "V", vs)
        ins = p.tensor.matmul(psr[:, :], onesr, srow, start=True, stop=True)
        pe = p.inc("P", ins, 1)
        p.wait(p.vector, "P", pe)
        p.vop(p.vector.tensor_copy, srep, psr)

        hook_state = {"ptx_free": ("V", 0), "pscore_free": ("V", 0)}

        def hook(tt, xt, xn, mean, ld):
            p.vop(p.vector.tensor_scalar_mul, negmean, mean, -1.0)
            p.wait(p.tensor, *hook_state["ptx_free"])
            for k in range(8):
                ins = p.tensor.transpose(ptx[:, k * 128:(k + 1) * 128],
                                         xt[:, k * 128:(k + 1) * 128], ident_f32)
            pe1 = p.inc("P", ins, 1)
            p.wait(p.vector, "P", pe1)
            ev = p.vop(p.vector.tensor_copy, xfs,
                       ptx.rearrange("p (c f) -> p c f", f=128))
            hook_state["ptx_free"] = ("V", ev)
            p.wait(p.tensor, "V", ev)
            for kc in range(8):
                ins = p.tensor.matmul(psc, xfs[:, kc, :], gwt[:, kc, :],
                                      start=(kc == 0), stop=(kc == 7))
            pe2 = p.inc("P", ins, 1)
            p.wait(p.vector, "P", pe2)
            p.vop(p.vector.scalar_tensor_tensor, scores[:, tt, :], srep, negmean,
                  psc, ALU.mult, ALU.add)
            p.vop(p.vector.reduce_max, rm, scores[:, tt, :], axis=mybir.AxisListType.X)
            p.vop(p.vector.tensor_single_scalar, onehot[:, tt, :], scores[:, tt, :],
                  rm, ALU.is_equal)
            p.vop(p.vector.tensor_copy, acc[:, tt, :], xn)
            va = p.vop(p.vector.tensor_add, acc[:, tt, :], acc[:, tt, :], xt)
            return [("V", va), ("P", pe1)]

        ln_done = emit_ln_fm(p, x_dram, xn_fm, ident, hook=hook)
        hook_done_v = p.val("V")

        with contextlib.ExitStack() as st2:
            p.stack = st2
            ew = [p.sb("ew", [128, 8, D], BF16) for _ in range(2)]
            pex = [p.ps("pex", [128, 512], F32) for _ in range(2)]
            ewr = expert_wt.rearrange("e (kc kp) f -> e kp kc f", kp=128)
            wld = [0, 0]
            wused = [0, 0]
            last_read = [0, 0]
            first_gate = [hook_done_v, hook_done_v]

            def load_e(e):
                wb = e % 2
                p.wait(p.gpsimd, "P", wused[wb])
                p.wait(p.gpsimd, "V", first_gate[wb])
                first_gate[wb] = 0
                wld[wb] = p.inc("W", p.gpsimd.dma_start(out=ew[wb], in_=ewr[e]), 16)

            load_e(0)
            load_e(1)
            i = 0
            for e in range(8):
                wb = e % 2
                p.wait(p.tensor, "W", wld[wb])
                for tt in range(NTT):
                    for n0 in (0, 512):
                        b = i % 2
                        i += 1
                        p.wait(p.tensor, "V", last_read[b])
                        for kc in range(8):
                            ins = p.tensor.matmul(
                                pex[b], xn_fm[:, kc, tt * 128:(tt + 1) * 128],
                                ew[wb][:, kc, n0:n0 + 512],
                                start=(kc == 0), stop=(kc == 7))
                        pe = p.inc("P", ins, 1)
                        if tt == NTT - 1 and n0 == 512:
                            wused[wb] = pe
                        p.wait(p.vector, "P", pe)
                        cnt = p.vop(p.vector.scalar_tensor_tensor,
                                    acc[:, tt, n0:n0 + 512], pex[b],
                                    onehot[:, tt, e:e + 1],
                                    acc[:, tt, n0:n0 + 512], ALU.mult, ALU.add)
                        last_read[b] = cnt
                if e + 2 < 8:
                    load_e(e + 2)
            p.stack = st

        acc_done = p.val("V")
        p.wait(p.sync, "V", acc_done)
        for tt in range(NTT):
            p.inc("S", p.sync.dma_start(
                out=out_dram[tt * 128:(tt + 1) * 128, :], in_=acc[:, tt, :]), 16)
        p.wait(p.sync, "S")
        p.stack = old


def emit_mamba(p, xn_fm, w, scr, evac_out, gate_v=0):
    EC = 16
    with contextlib.ExitStack() as st:
        old = p.stack
        p.stack = st
        convw = p.sb("convw", [128, EC, 4], F32)
        convb = p.sb("convb", [128, EC], F32)
        negdtb = p.sb("negdtb", [128, EC], F32)
        apos = p.sb("apos", [128, EC, 16], F32)
        dpar = p.sb("dpar", [128, EC], F32)
        selp = p.sb("selp", [16, T], BF16)
        seln = p.sb("seln", [16, T], BF16)
        p.wait(p.gpsimd, "V", gate_v)
        p.inc("W", p.gpsimd.dma_start(out=convw, in_=w["convw"]), 16)
        p.inc("W", p.gpsimd.dma_start(out=convb, in_=w["convb"]), 16)
        p.inc("W", p.gpsimd.dma_start(out=negdtb, in_=w["dtpb"]), 16)
        p.inc("W", p.gpsimd.dma_start(out=apos, in_=w["alog"]), 16)
        p.inc("W", p.gpsimd.dma_start(out=dpar, in_=w["dpar"]), 16)
        p.inc("W", p.gpsimd.dma_start(out=selp, in_=w["selp"]), 16)
        wsm = p.inc("W", p.gpsimd.dma_start(out=seln, in_=w["seln"]), 16)
        p.wait(p.vector, "W", wsm)
        p.vop(p.vector.tensor_scalar_mul, negdtb, negdtb, -1.0)
        p.wait(p.scalar, "W", wsm)
        p.aop(p.scalar.activation, out=apos.rearrange("p a b -> p (a b)"),
              in_=apos.rearrange("p a b -> p (a b)"), func=AF.Exp)

        stbuf = [p.sb("stg", [128, 512], BF16) for _ in range(2)]
        st_state = {"free": [0, 0], "i": 0}

        def stage_store(src_ap, dst_ap, pe_cnt=None):
            b = st_state["i"] % 2
            st_state["i"] += 1
            if pe_cnt is not None:
                p.wait(p.vector, "P", pe_cnt)
            p.wait(p.vector, "S", st_state["free"][b])
            cnt = p.vop(p.vector.tensor_copy, stbuf[b], src_ap)
            p.wait(p.sync, "V", cnt)
            s = p.inc("S", p.sync.dma_start(out=dst_ap, in_=stbuf[b]), 16)
            st_state["free"][b] = s
            return ("V", cnt)

        stx = contextlib.ExitStack()
        p.stack = stx
        xh = p.sb("xh", [128, EC, T], BF16)
        with contextlib.ExitStack() as s1:
            p.stack = s1
            inw = p.sb("inw", [128, 8, 4096], BF16)
            p.wait(p.gpsimd, "V")
            wl = p.inc("W", p.gpsimd.dma_start(
                out=inw, in_=w["inwT"].rearrange("(kc kp) n -> kp kc n", kp=128)), 16)
            p.wait(p.tensor, "W", wl)

            def evac1(mc, n0, ps_ap, pe_cnt):
                if mc < EC:
                    p.wait(p.vector, "P", pe_cnt)
                    cnt = p.vop(p.vector.tensor_copy, xh[:, mc, n0:n0 + 512], ps_ap)
                    return ("V", cnt)
                return stage_store(ps_ap, scr["z"][:, mc - EC, n0 // 128:n0 // 128 + 4, :]
                                   .rearrange("p a b -> p (a b)"), pe_cnt)

            emit_mm_fm(p, inw, xn_fm, 8, 32, evac1)
            p.stack = stx

        dbg_xh = getattr(emit_mamba, "dbg_xh", None)
        if dbg_xh is not None:
            p.wait(p.sync, "V")
            p.inc("S", p.sync.dma_start(out=dbg_xh, in_=xh), 16)
            p.wait(p.vector, "S")
            emit_mamba.dbg_xh = None
        acc = p.sb("cacc", [128, T], F32)
        sil2d = p.sb("sil2d", [128, T], BF16)
        for dc in range(EC):
            p.vop(p.vector.tensor_scalar_mul, acc, xh[:, dc, :],
                  convw[:, dc, 3:4])
            for s in (1, 2, 3):
                p.vop(p.vector.scalar_tensor_tensor, acc[:, s:T],
                      xh[:, dc, 0:T - s], convw[:, dc, 3 - s:4 - s],
                      acc[:, s:T], ALU.mult, ALU.add)
            vc = p.val("V")
            p.wait(p.scalar, "V", vc)
            av = p.aop(p.scalar.activation, out=sil2d, in_=acc, func=AF.Silu,
                       bias=convb[:, dc:dc + 1])
            p.wait(p.vector, "A", av)
            p.vop(p.vector.tensor_copy, xh[:, dc, :], sil2d)
        u_done = p.val("A")
        p.wait(p.vector, "A", u_done)
        dbg_u = getattr(emit_mamba, "dbg_u", None)
        if dbg_u is not None:
            p.wait(p.sync, "A", u_done)
            p.inc("S", p.sync.dma_start(out=dbg_u, in_=xh), 16)
            emit_mamba.dbg_u = None

        with contextlib.ExitStack() as s3:
            p.stack = s3
            xpw = p.sb("xpw", [128, EC, 96], BF16)
            dtpw = p.sb("dtpw", [64, T], BF16)
            dtr = p.sb("dtr", [64, T], BF16)
            bm = p.sb("bm", [16, T], BF16)
            cm = p.sb("cm", [16, T], BF16)
            sgt = p.sb("sgt", [128, 512], F32)
            pj = [p.ps("pj", [128, 512], F32) for _ in range(2)]
            p.wait(p.gpsimd, "V")
            p.inc("W", p.gpsimd.dma_start(
                out=xpw, in_=w["xpwT"].rearrange("(kc kp) n -> kp kc n", kp=128)), 16)
            wl = p.inc("W", p.gpsimd.dma_start(out=dtpw, in_=w["dtpwT"]), 16)
            p.wait(p.tensor, "W", wl)
            p.wait(p.tensor, "A", u_done)
            entry_v = p.val("V")
            bank_free = [("V", entry_v), ("V", entry_v)]
            ib = [0]

            def mm_group(lhsT_slices, rhs_slices, n0, nsz, M):
                b = ib[0] % 2
                ib[0] += 1
                p.wait(p.tensor, *bank_free[b])
                for ki, (lh, rh) in enumerate(zip(lhsT_slices, rhs_slices)):
                    ins = p.tensor.matmul(pj[b][0:M, 0:nsz], lh, rh,
                                          start=(ki == 0),
                                          stop=(ki == len(lhsT_slices) - 1))
                return b, p.inc("P", ins, 1)

            for n0 in range(0, T, 512):
                for (dst, c0, c1) in ((dtr, 0, 64), (bm, 64, 80), (cm, 80, 96)):
                    M = c1 - c0
                    b, pe = mm_group(
                        [xpw[:, kc, c0:c1] for kc in range(EC)],
                        [xh[:, kc, n0:n0 + 512] for kc in range(EC)],
                        n0, 512, M)
                    p.wait(p.vector, "P", pe)
                    cnt = p.vop(p.vector.tensor_copy, dst[0:M, n0:n0 + 512],
                                pj[b][0:M, 0:512])
                    bank_free[b] = ("V", cnt)
            for mc in range(EC):
                for n0 in range(0, T, 512):
                    b, pe = mm_group(
                        [dtpw[:, mc * 128:(mc + 1) * 128]],
                        [dtr[:, n0:n0 + 512]], n0, 512, 128)
                    p.wait(p.scalar, "P", pe)
                    ac = p.aop(p.scalar.activation, out=sgt, in_=pj[b][:, 0:512],
                               func=AF.Sigmoid, scale=-1.0,
                               bias=negdtb[:, mc:mc + 1])
                    bank_free[b] = ("A", ac)
                    b2 = st_state["i"] % 2
                    st_state["i"] += 1
                    p.wait(p.scalar, "S", st_state["free"][b2])
                    ac2 = p.aop(p.scalar.activation, out=stbuf[b2], in_=sgt,
                                func=AF.Ln)
                    p.wait(p.sync, "A", ac2)
                    s = p.inc("S", p.sync.dma_start(
                        out=scr["nd"][:, mc, n0 // 128:n0 // 128 + 4, :]
                        .rearrange("p a b -> p (a b)"), in_=stbuf[b2]), 16)
                    st_state["free"][b2] = s
            for n in range(16):
                for n0 in range(0, T, 512):
                    for (src, sel, dst) in ((bm, seln, scr["brep"]),
                                            (cm, selp, scr["crep"])):
                        b, pe = mm_group(
                            [sel[:, n * 128:(n + 1) * 128]],
                            [src[:, n0:n0 + 512]], n0, 512, 128)
                        bank_free[b] = stage_store(
                            pj[b][:, 0:512],
                            dst[:, n, n0 // 128:n0 // 128 + 4, :]
                            .rearrange("p a b -> p (a b)"), pe)
            p.stack = stx
        p3_done = p.val("S")

        with contextlib.ExitStack() as s4:
            p.stack = s4
            du = p.sb("du", [128, T], BF16)
            ndt = p.sb("ndt", [128, T], BF16)
            dA = p.sb("dA", [128, T], BF16)
            dBu = p.sb("dBu", [128, T], BF16)
            hh = p.sb("hh", [128, T], BF16)
            yac = p.sb("yac", [128, T], F32)
            zs = p.sb("zs", [128, T], BF16)
            bc = [p.sb("bcb", [128, T], BF16) for _ in range(2)]
            cc = [p.sb("ccb", [128, T], BF16) for _ in range(2)]
            bc_ld = [0, 0]
            bc_used = [0, 0]
            p.wait(p.sync, "S", p3_done)
            nd_ld = 0
            nd_used = 0

            def load_bc(dc, n):
                b = n % 2
                p.wait(p.sync, "V", bc_used[b])
                p.inc("L", p.sync.dma_start(
                    out=bc[b], in_=scr["brep"][:, n, :, :].rearrange("p a b -> p (a b)")), 16)
                bc_ld[b] = p.inc("L", p.sync.dma_start(
                    out=cc[b], in_=scr["crep"][:, n, :, :].rearrange("p a b -> p (a b)")), 16)

            for dc in range(EC):
                p.wait(p.sync, "V", nd_used)
                ndl = p.inc("L", p.sync.dma_start(
                    out=ndt, in_=scr["nd"][:, dc, :, :].rearrange("p a b -> p (a b)")), 16)
                p.wait(p.vector, "L", ndl)
                p.vop(p.vector.tensor_mul, du, xh[:, dc, :], ndt)
                load_bc(dc, 0)
                load_bc(dc, 1)
                for n in range(16):
                    b = n % 2
                    p.wait(p.scalar, "L", ndl)
                    p.wait(p.scalar, "V")
                    av = p.aop(p.scalar.activation, out=dA, in_=ndt, func=AF.Exp,
                               scale=apos[:, dc, n:n + 1])
                    p.wait(p.vector, "L", bc_ld[b])
                    p.vop(p.vector.tensor_mul, dBu, du, bc[b])
                    p.wait(p.vector, "A", av)
                    p.vop(p.vector.tensor_tensor_scan, hh, dA, dBu, 0.0,
                          ALU.mult, ALU.add)
                    p.vop(p.vector.tensor_mul, hh, hh, cc[b])
                    if n == 0:
                        cnt = p.vop(p.vector.tensor_copy, yac, hh)
                    else:
                        cnt = p.vop(p.vector.tensor_add, yac, yac, hh)
                    bc_used[b] = cnt
                    if n + 2 < 16:
                        load_bc(dc, n + 2)
                nd_used = p.val("V")
                p.vop(p.vector.scalar_tensor_tensor, yac, xh[:, dc, :],
                      dpar[:, dc:dc + 1], yac, ALU.mult, ALU.add)
                zl = p.inc("L", p.sync.dma_start(
                    out=zs, in_=scr["z"][:, dc, :, :].rearrange("p a b -> p (a b)")), 16)
                p.wait(p.scalar, "L", zl)
                p.wait(p.scalar, "V")
                az = p.aop(p.scalar.activation, out=zs, in_=zs, func=AF.Silu)
                p.wait(p.vector, "A", az)
                yv = p.vop(p.vector.tensor_mul, hh, yac, zs)
                p.wait(p.sync, "V", yv)
                p.inc("S", p.sync.dma_start(out=scr["y"][:, dc, :], in_=hh), 16)
            p.stack = stx
        stx.close()
        p.stack = st
        y_done = p.val("S")

        with contextlib.ExitStack() as s5:
            p.stack = s5
            opw = p.sb("opw", [128, EC, 1024], BF16)
            p.wait(p.gpsimd, "V")
            wl = p.inc("W", p.gpsimd.dma_start(
                out=opw, in_=w["opwT"].rearrange("(kc kp) n -> kp kc n", kp=128)), 16)
            yt = p.sb("yt", [128, EC, T], BF16)
            p.wait(p.sync, "S", y_done)
            p.wait(p.sync, "V")
            yl = p.inc("L", p.sync.dma_start(out=yt, in_=scr["y"]), 16)
            p.wait(p.tensor, "W", wl)
            p.wait(p.tensor, "L", yl)
            with contextlib.ExitStack() as s5b:
                p.stack = s5b
                n_tile = 512
                pst = [p.ps("mt", [128, n_tile], F32) for _ in range(2)]
                entry_v = p.val("V")
                last_read = [("V", entry_v), ("V", entry_v)]
                i = 0
                for tt in range(NTT):
                    for n0 in range(0, 1024, n_tile):
                        b = i % 2
                        i += 1
                        p.wait(p.tensor, *last_read[b])
                        for kc in range(EC):
                            ins = p.tensor.matmul(
                                pst[b], yt[:, kc, tt * 128:(tt + 1) * 128],
                                opw[:, kc, n0:n0 + n_tile],
                                start=(kc == 0), stop=(kc == EC - 1))
                        pe = p.inc("P", ins, 1)
                        last_read[b] = evac_out(tt, n0, pst[b], pe)
                p.stack = s5
            p.stack = st
        p.stack = old


def make_res_evac(p, x_dram, out_dram, gate_v):
    xres = [p.sb("xres", [128, D], F32) for _ in range(2)]
    ot = [p.sb("ot", [128, D], F32) for _ in range(2)]
    res_ld = [0] * NTT
    res_consumed = [gate_v, gate_v]
    st_done = [0, 0]

    def load_res(tt):
        b = tt % 2
        p.wait(p.sync, "V", res_consumed[b])
        res_ld[tt] = p.inc("L", p.sync.dma_start(
            out=xres[b], in_=x_dram[tt * 128:(tt + 1) * 128, :]), 16)

    load_res(0)
    load_res(1)

    def evac(tt, n0, ps_ap, pe_cnt):
        b = tt % 2
        p.wait(p.vector, "P", pe_cnt)
        p.wait(p.vector, "L", res_ld[tt])
        p.wait(p.vector, "S", st_done[b])
        cnt = p.vop(p.vector.tensor_add,
                    ot[b][:, n0:n0 + 512], xres[b][:, n0:n0 + 512], ps_ap)
        if n0 + 512 == D:
            res_consumed[b] = cnt
            p.wait(p.sync, "V", cnt)
            s = p.inc("S", p.sync.dma_start(
                out=out_dram[tt * 128:(tt + 1) * 128, :], in_=ot[b]), 16)
            st_done[b] = s
            if tt + 2 < NTT:
                load_res(tt + 2)
        return ("V", cnt)

    return evac


def emit_rowrep(p, row_sb, dst_sb, onesr, pswr):
    p.wait(p.tensor, "V")
    for n0 in range(0, D, 512):
        ins = p.tensor.matmul(pswr[:, n0 % 1024:n0 % 1024 + 512],
                              onesr, row_sb[0:1, n0:n0 + 512],
                              start=True, stop=True)
        pe = p.inc("P", ins, 1)
        p.wait(p.vector, "P", pe)
        p.vop(p.vector.tensor_copy, dst_sb[:, n0:n0 + 512],
              pswr[:, n0 % 1024:n0 % 1024 + 512])
    return p.val("V")


def emit_ln_tm(p, x_dram, out_dram, wrep=None, brep=None, add_dram=None):
    with contextlib.ExitStack() as _st:
        _old = p.stack
        p.stack = _st
        xt = [p.sb("lpx", [128, D], F32) for _ in range(2)]
        ad = [p.sb("lpa", [128, D], F32) for _ in range(2)]
        sq = p.sb("lpsq", [128, D], BF16)
        s1 = p.sb("lps1", [128, 1], F32)
        s2 = p.sb("lps2", [128, 1], F32)
        mean = p.sb("lpmean", [128, 1], F32)
        nm2 = p.sb("lpnm2", [128, 1], F32)
        var = p.sb("lpvar", [128, 1], F32)
        rstd = p.sb("lprstd", [128, 1], F32)
        ot = [p.sb("lpot", [128, D], F32) for _ in range(2)]
        cons = [0, 0]
        st_done = [0, 0]
        s2_cons = 0
        for tt in range(NTT):
            b = tt % 2
            p.wait(p.sync, "V", cons[b])
            ld = p.inc("L", p.sync.dma_start(
                out=xt[b], in_=x_dram[tt * 128:(tt + 1) * 128, :]), 16)
            if add_dram is not None:
                ld = p.inc("L", p.sync.dma_start(
                    out=ad[b], in_=add_dram[tt * 128:(tt + 1) * 128, :]), 16)
            p.wait(p.vector, "L", ld)
            p.vop(p.vector.reduce_sum, s1, xt[b], axis=mybir.AxisListType.X)
            p.wait(p.scalar, "L", ld)
            p.wait(p.scalar, "V", s2_cons)
            a1 = p.aop(p.scalar.activation, out=sq, in_=xt[b], func=AF.Square,
                       accum_out=s2)
            p.vop(p.vector.tensor_scalar_mul, mean, s1, 1.0 / D)
            p.wait(p.vector, "A", a1)
            p.vop(p.vector.tensor_scalar, nm2, s1, mean, -1.0 / D, ALU.mult, ALU.mult)
            s2_cons = p.vop(p.vector.tensor_scalar, var, s2, 1.0 / D, nm2,
                            ALU.mult, ALU.add)
            v1 = p.vop(p.vector.tensor_scalar_add, var, var, EPS)
            p.wait(p.scalar, "V", v1)
            p.aop(p.scalar.activation, out=rstd, in_=var, func=AF.Ln)
            a2 = p.aop(p.scalar.activation, out=rstd, in_=rstd, func=AF.Exp,
                       scale=-0.5)
            p.wait(p.vector, "A", a2)
            p.wait(p.vector, "S", st_done[b])
            cnt = p.vop(p.vector.tensor_scalar, ot[b], xt[b], mean, rstd,
                        ALU.subtract, ALU.mult)
            if wrep is not None:
                cnt = p.vop(p.vector.tensor_mul, ot[b], ot[b], wrep)
            if brep is not None:
                cnt = p.vop(p.vector.tensor_add, ot[b], ot[b], brep)
            if add_dram is not None:
                cnt = p.vop(p.vector.tensor_add, ot[b], ot[b], ad[b])
            cons[b] = cnt
            p.wait(p.sync, "V", cnt)
            st_done[b] = p.inc("S", p.sync.dma_start(
                out=out_dram[tt * 128:(tt + 1) * 128, :], in_=ot[b]), 16)
        p.wait(p.sync, "S")
        p.stack = _old


def emit_l1_attn_a(p, x_dram, out_dram, wl, wn, scr, ident):
    with contextlib.ExitStack() as st:
        old = p.stack
        p.stack = st
        onesr = p.sb("onesr", [1, 128], F32)
        wrow = p.sb("wrow", [1, D], F32)
        wrep = p.sb("wrep", [128, D], F32)
        brep = p.sb("brep2", [128, D], F32)
        pswr = p.ps("pswr", [128, 1024], F32)
        p.vop(p.vector.memset, onesr, 1.0)

        def load_reps(w_ap, b_ap):
            p.wait(p.sync, "V")
            lw = p.inc("L", p.sync.dma_start(out=wrow, in_=w_ap), 16)
            p.wait(p.tensor, "L", lw)
            v1 = emit_rowrep(p, wrow, wrep, onesr, pswr)
            p.wait(p.sync, "V", v1)
            lb = p.inc("L", p.sync.dma_start(out=wrow, in_=b_ap), 16)
            p.wait(p.tensor, "L", lb)
            return emit_rowrep(p, wrow, brep, onesr, pswr)

        emit_ln_tm(p, x_dram, scr["res0"])
        res = [scr["res0"], scr["res1"]]
        nlayers = getattr(emit_l1_attn_a, "nlayers", 4)
        for i in range(nlayers):
            rin, rout = res[i % 2], res[(i + 1) % 2]
            gv = load_reps(wn["norm_w"][i:i + 1, :], wn["norm_b"][i:i + 1, :])
            with contextlib.ExitStack() as sl:
                p.stack = sl
                xn_fm = p.sb("xnfm", [128, 8, T], BF16)
                p.wait(p.sync, "S")
                ln_done = emit_ln_fm(p, rin, xn_fm, ident, gw=None,
                                     wrep2=wrep, brep2=brep)
                evac = make_res_evac(p, rin, rout, ln_done)
                emit_mamba(p, xn_fm, wl[i], scr, evac, gate_v=ln_done)
                p.wait(p.sync, "S")
                p.stack = st
        gv = load_reps(wn["normf_w"], wn["normf_b"])
        p.wait(p.sync, "S")
        emit_ln_tm(p, res[nlayers % 2], out_dram, wrep=wrep, brep=brep,
                   add_dram=x_dram)
        p.stack = old


def emit_l1_attn_b(p, x_dram, out_dram, w, scr, ident):
    with contextlib.ExitStack() as st:
        old = p.stack
        p.stack = st
        xn_fm = p.sb("xnfm", [128, 8, T], BF16)
        ln_done = emit_ln_fm(p, x_dram, xn_fm, ident)
        evac = make_res_evac(p, x_dram, out_dram, ln_done)
        emit_mamba(p, xn_fm, w, scr, evac, gate_v=ln_done)
        p.wait(p.sync, "S")
        p.stack = old


def emit_slstm_sublayer(p, l, xa, xb, xnext, w, scr, ident):
    H = 4
    with contextlib.ExitStack() as st:
        old = p.stack
        p.stack = st
        onesr = p.sb("onesr", [1, 128], F32)
        wrow = p.sb("wrow", [1, D], F32)
        wrep = p.sb("wrep", [128, D], F32)
        pswr = p.ps("pswr", [128, 1024], F32)
        p.vop(p.vector.memset, onesr, 1.0)

        def load_rep(row_ap):
            p.wait(p.sync, "V")
            lw = p.inc("L", p.sync.dma_start(out=wrow, in_=row_ap), 16)
            p.wait(p.tensor, "L", lw)
            return emit_rowrep(p, wrow, wrep, onesr, pswr)

        y_fm = p.sb("yfm", [128, 8, T], BF16)

        with contextlib.ExitStack() as sa:
            p.stack = sa
            gv = load_rep(w["ln1w"][l:l + 1, :])
            xn_fm = p.sb("xnfm", [128, 8, T], BF16)
            ln_done = emit_ln_fm(p, xa, xn_fm, ident, wrep2=wrep)
            xc_fm = p.sb("xcfm", [128, 8, T], BF16)
            convw = p.sb("convw", [128, 8, 4], F32)
            convb = p.sb("convb", [128, 8], F32)
            p.wait(p.gpsimd, "V", ln_done)
            p.inc("W", p.gpsimd.dma_start(out=convw, in_=w["convw"][l]), 16)
            cwl = p.inc("W", p.gpsimd.dma_start(out=convb, in_=w["convb"][l]), 16)
            acc = p.sb("cacc", [128, T], F32)
            sil2d = p.sb("sil2d", [128, T], BF16)
            p.wait(p.vector, "W", cwl)
            p.wait(p.scalar, "W", cwl)
            for dc in range(8):
                p.vop(p.vector.tensor_scalar_mul, acc, xn_fm[:, dc, :],
                      convw[:, dc, 3:4])
                for s in (1, 2, 3):
                    p.vop(p.vector.scalar_tensor_tensor, acc[:, s:T],
                          xn_fm[:, dc, 0:T - s], convw[:, dc, 3 - s:4 - s],
                          acc[:, s:T], ALU.mult, ALU.add)
                vc = p.val("V")
                p.wait(p.scalar, "V", vc)
                av = p.aop(p.scalar.activation, out=sil2d, in_=acc, func=AF.Silu,
                           bias=convb[:, dc:dc + 1])
                p.wait(p.vector, "A", av)
                p.vop(p.vector.tensor_copy, xc_fm[:, dc, :], sil2d)
            wg_sb = p.sb("wg", [128, 8, 1024], BF16)
            brow = p.sb("brow", [1, 4096], F32)
            b_rep = p.sb("brep32", [32, 4096], BF16)
            p.wait(p.gpsimd, "V")
            p.inc("W", p.gpsimd.dma_start(
                out=wg_sb, in_=w["wg"][l].rearrange("h (kc kp) n -> kp (h kc) n", kp=128)), 16)
            bwl = p.inc("W", p.gpsimd.dma_start(out=brow, in_=w["cb"][l]), 16)
            ones32 = p.sb("ones32", [1, 32], F32)
            p.vop(p.vector.memset, ones32, 1.0)
            p.wait(p.tensor, "W", bwl)
            for n0 in range(0, 4096, 512):
                p.wait(p.tensor, "V")
                ins = p.tensor.matmul(pswr[0:32, n0 % 1024:n0 % 1024 + 512],
                                      ones32, brow[0:1, n0:n0 + 512],
                                      start=True, stop=True)
                pe = p.inc("P", ins, 1)
                p.wait(p.vector, "P", pe)
                p.vop(p.vector.tensor_copy, b_rep[:, n0:n0 + 512],
                      pswr[0:32, n0 % 1024:n0 % 1024 + 512])
            stg = [p.sb("stg", [32, 1024], BF16) for _ in range(2)]
            stg_free = [0, 0]
            pgt = [p.ps("pgt", [32, 1024], F32) for _ in range(2)]
            pgt_free = ["V", 0, "V", 0]
            gmm = p.val("V")
            p.wait(p.tensor, "V", gmm)
            si = 0
            for t0 in range(0, T, 32):
                tc, tof = t0 // 128, t0 % 128
                for h in range(H):
                    b = si % 2
                    p.wait(p.tensor, pgt_free[2 * b], pgt_free[2 * b + 1])
                    for g, src in ((0, xc_fm), (1, xn_fm), (2, xc_fm), (3, xn_fm)):
                        for kc in range(2):
                            ins = p.tensor.matmul(
                                pgt[b][:, g * 256:(g + 1) * 256],
                                src[:, 2 * h + kc, t0:t0 + 32],
                                wg_sb[:, 2 * h + kc, g * 256:(g + 1) * 256],
                                start=(kc == 0), stop=(kc == 1))
                    pe = p.inc("P", ins, 1)
                    p.wait(p.vector, "P", pe)
                    p.wait(p.vector, "S", stg_free[b])
                    cnt = p.vop(p.vector.tensor_add, stg[b], pgt[b],
                                b_rep[:, h * 1024:(h + 1) * 1024])
                    pgt_free[2 * b], pgt_free[2 * b + 1] = "V", cnt
                    p.wait(p.sync, "V", cnt)
                    s = p.inc("S", p.sync.dma_start(
                        out=scr["wx"][t0:t0 + 32, h * 1024:(h + 1) * 1024],
                        in_=stg[b]), 16)
                    stg_free[b] = s
                    si += 1
            p.stack = st
        ga_done_s = p.val("S")
        ga_done_v = p.val("V")

        with contextlib.ExitStack() as sb:
            p.stack = sb
            R_sb = p.sb("rk", [128, 8, 1024], BF16)
            p.wait(p.gpsimd, "V", ga_done_v)
            rl = p.inc("W", p.gpsimd.dma_start(
                out=R_sb, in_=w["rk"][l].rearrange("h (kc kp) n -> kp (h kc) n", kp=128)), 16)
            SL = 16
            slab = [p.sb("wxsl", [128, SL, 1024], BF16) for _ in range(2)]
            slab_ld = [0, 0]
            slab_free = [ga_done_v, ga_done_v]
            rawb = p.sb("rawb", [128, 1024], F32)
            sfo = p.sb("sfo", [128, 512], F32)
            ei = p.sb("ei", [128, 256], F32)
            tz = p.sb("tz", [128, 256], F32)
            t1 = p.sb("t1", [128, 256], F32)
            cst = p.sb("cst", [128, 256], F32)
            nst = p.sb("nst", [128, 256], F32)
            rn = p.sb("rn", [128, 256], F32)
            h_lm = p.sb("hlm", [128, 256], BF16)
            h0 = p.sb("h0", [128, 8], BF16)
            praw = p.ps("praw", [128, 1024], F32)
            ptr = p.ps("ptr", [128, 256], BF16)
            p.vop(p.vector.memset, cst, 0.0)
            p.vop(p.vector.memset, nst, 0.0)
            p.vop(p.vector.memset, h0, 0.0)
            wxr = scr["wx"].rearrange("(nt sl) (hh n) -> hh nt sl n", sl=SL, hh=4)

            def load_slab(si_):
                b = si_ % 2
                p.wait(p.sync, "V", slab_free[b])
                p.wait(p.sync, "S", ga_done_s)
                slab_ld[b] = p.inc("L", p.sync.dma_start(
                    out=slab[b][::32, :, :],
                    in_=wxr[:, si_, :, :]), 16)

            load_slab(0)
            load_slab(1)
            p.wait(p.tensor, "W", rl)
            prev_v = p.val("V")
            a_sfo = 0
            for t in range(T):
                sidx = t // SL
                s = t % SL
                b = sidx % 2
                p.wait(p.tensor, "V", prev_v)
                for h in range(H):
                    for nh in (0, 512):
                        for kc in range(2):
                            lh = h0[:, 2 * h + kc:2 * h + kc + 1] if t == 0 else \
                                y_fm[:, 2 * h + kc, t - 1:t]
                            ins = p.tensor.matmul(
                                praw[32 * h:32 * h + 1, nh:nh + 512], lh,
                                R_sb[:, 2 * h + kc, nh:nh + 512],
                                start=(kc == 0), stop=(kc == 1),
                                tile_position=(0, 32 * h))
                pe = p.inc("P", ins, 1)
                p.wait(p.vector, "P", pe)
                p.wait(p.vector, "L", slab_ld[b])
                nc_ = p.nc
                p.inc("V", nc_.vector.tensor_add(rawb, praw, slab[b][:, s, :]), 1)
                raw_v = p.val("V")
                if s == SL - 1:
                    slab_free[b] = raw_v
                    if (sidx + 2) * SL < T:
                        load_slab(sidx + 2)
                p.wait(p.scalar, "V", raw_v)
                p.inc("A", nc_.scalar.activation(out=ei, in_=rawb[:, 0:256], func=AF.Exp), 1)
                p.inc("A", nc_.scalar.activation(out=tz, in_=rawb[:, 256:512], func=AF.Tanh), 1)
                a_sfo = p.inc("A", nc_.scalar.activation(out=sfo, in_=rawb[:, 512:1024], func=AF.Sigmoid), 1)
                p.wait(p.vector, "A", a_sfo)
                nc_.vector.tensor_mul(t1, ei, tz)
                nc_.vector.tensor_mul(cst, cst, sfo[:, 0:256])
                nc_.vector.tensor_add(cst, cst, t1)
                nc_.vector.tensor_mul(nst, nst, sfo[:, 0:256])
                p.inc("V", nc_.vector.tensor_add(nst, nst, ei), 1)
                n_v = p.val("V")
                p.wait(p.scalar, "V", n_v)
                a_ln = p.inc("A", nc_.scalar.activation(out=rn, in_=nst, func=AF.Ln), 1)
                p.wait(p.scalar, "A", a_ln)
                a_rn = p.inc("A", nc_.scalar.activation(out=rn, in_=rn, func=AF.Exp, scale=-1.0), 1)
                p.wait(p.vector, "A", a_rn)
                nc_.vector.tensor_mul(t1, cst, rn)
                p.inc("V", nc_.vector.tensor_mul(h_lm, t1, sfo[:, 256:512]), 1)
                h_v = p.val("V")
                p.wait(p.tensor, "V", h_v)
                nc_.tensor.transpose(ptr[:, 0:128], h_lm[:, 0:128], ident)
                ins = nc_.tensor.transpose(ptr[:, 128:256], h_lm[:, 128:256], ident)
                pt_ = p.inc("P", ins, 1)
                p.wait(p.vector, "P", pt_)
                nc_.vector.tensor_copy(
                    y_fm[:, 0:8:2, t].unsqueeze(2),
                    ptr[:, 0:128].rearrange("p (l q) -> p l q", q=32)[:, :, 0:1])
                p.inc("V", nc_.vector.tensor_copy(
                    y_fm[:, 1:8:2, t].unsqueeze(2),
                    ptr[:, 128:256].rearrange("p (l q) -> p l q", q=32)[:, :, 0:1]), 1)
                prev_v = p.val("V")
            p.stack = st
        rec_done = p.val("V")

        with contextlib.ExitStack() as sc:
            p.stack = sc
            gnv = load_rep(w["gnw"][l:l + 1, :])
            ytile = p.sb("ytile", [128, 1024], BF16)
            sq = p.sb("gnsq", [128, 256], BF16)
            s1 = p.sb("gns1", [128, 1], F32)
            s2 = p.sb("gns2", [128, 1], F32)
            mean = p.sb("gnmean", [128, 1], F32)
            nm2 = p.sb("gnnm2", [128, 1], F32)
            var = p.sb("gnvar", [128, 1], F32)
            rstd = p.sb("gnrstd", [128, 1], F32)
            xat = [p.sb("xat", [128, D], F32) for _ in range(2)]
            ot = [p.sb("got", [128, D], F32) for _ in range(2)]
            ptm = p.ps("ptm", [128, 1024], BF16)
            xa_ld = [0, 0]
            st_done = [0, 0]
            cons = [rec_done, rec_done]
            ptm_free = rec_done
            for tt in range(NTT):
                bb = tt % 2
                p.wait(p.sync, "V", cons[bb])
                xa_ld[bb] = p.inc("L", p.sync.dma_start(
                    out=xat[bb], in_=xa[tt * 128:(tt + 1) * 128, :]), 16)
                p.wait(p.tensor, "V", ptm_free)
                p.wait(p.tensor, "V", rec_done)
                for k in range(8):
                    ins = p.tensor.transpose(ptm[:, k * 128:(k + 1) * 128],
                                             y_fm[:, k, tt * 128:(tt + 1) * 128],
                                             ident)
                pe = p.inc("P", ins, 1)
                p.wait(p.vector, "P", pe)
                ptm_free = p.vop(p.vector.tensor_copy, ytile, ptm)
                for h in range(H):
                    sl_ = ytile[:, h * 256:(h + 1) * 256]
                    p.vop(p.vector.reduce_sum, s1, sl_, axis=mybir.AxisListType.X)
                    p.wait(p.scalar, "V")
                    a1 = p.aop(p.scalar.activation, out=sq, in_=sl_, func=AF.Square,
                               accum_out=s2)
                    p.vop(p.vector.tensor_scalar_mul, mean, s1, 1.0 / 256)
                    p.wait(p.vector, "A", a1)
                    p.vop(p.vector.tensor_scalar, nm2, s1, mean, -1.0 / 256,
                          ALU.mult, ALU.mult)
                    p.vop(p.vector.tensor_scalar, var, s2, 1.0 / 256, nm2,
                          ALU.mult, ALU.add)
                    v1 = p.vop(p.vector.tensor_scalar_add, var, var, EPS)
                    p.wait(p.scalar, "V", v1)
                    p.aop(p.scalar.activation, out=rstd, in_=var, func=AF.Ln)
                    a2 = p.aop(p.scalar.activation, out=rstd, in_=rstd, func=AF.Exp,
                               scale=-0.5)
                    p.wait(p.vector, "A", a2)
                    p.vop(p.vector.tensor_scalar_sub, sl_, sl_, mean)
                    p.vop(p.vector.tensor_scalar_mul, sl_, sl_, rstd)
                p.wait(p.vector, "L", xa_ld[bb])
                p.wait(p.vector, "S", st_done[bb])
                p.vop(p.vector.tensor_mul, ot[bb], ytile, wrep)
                cnt = p.vop(p.vector.tensor_add, ot[bb], ot[bb], xat[bb])
                cons[bb] = cnt
                p.wait(p.sync, "V", cnt)
                st_done[bb] = p.inc("S", p.sync.dma_start(
                    out=xb[tt * 128:(tt + 1) * 128, :], in_=ot[bb]), 16)
            p.wait(p.sync, "S")
            p.stack = st
        gn_done = p.val("V")

        with contextlib.ExitStack() as sd:
            p.stack = sd
            lv = load_rep(w["ln2w"][l:l + 1, :])
            xn2 = p.sb("xn2", [128, 8, T], BF16)
            p.wait(p.sync, "S")
            ln2_done = emit_ln_fm(p, xb, xn2, ident, wrep2=wrep)
            ffup = p.sb("ffup", [128, 8, 2688], BF16)
            ffdn = p.sb("ffdn", [128, 11, 1024], BF16)
            p.wait(p.gpsimd, "V", ln2_done)
            p.inc("W", p.gpsimd.dma_start(
                out=ffup, in_=w["ffup"][l].rearrange("(kc kp) n -> kp kc n", kp=128)), 16)
            p.inc("W", p.gpsimd.dma_start(
                out=ffdn[:, 0:10, :],
                in_=w["ffdn"][l][0:1280, :].rearrange("(kc kp) n -> kp kc n", kp=128)), 16)
            fwl = p.inc("W", p.gpsimd.dma_start(
                out=ffdn[0:64, 10, :], in_=w["ffdn"][l][1280:1344, :]), 16)
            u2 = p.sb("u2", [128, 2688], F32)
            g2 = p.sb("g2", [128, 1344], F32)
            h2 = p.sb("h2", [128, 1344], BF16)
            h2f = [p.sb("h2f", [128, 11, 128], BF16) for _ in range(2)]
            pup = [p.ps("pup", [128, 512], F32) for _ in range(2)]
            ptf = p.ps("ptf", [128, 1408], BF16)
            evac = make_res_evac(p, xb, xnext, ln2_done)
            p.wait(p.tensor, "W", fwl)
            pup_free = ["V", ln2_done, "V", ln2_done]
            ptf_free = ln2_done
            h2f_used = [0, 0]
            pst = [p.ps("pdn", [128, 512], F32) for _ in range(2)]
            dn_free = ["V", ln2_done, "V", ln2_done]
            si = 0
            for tt in range(NTT):
                bb = tt % 2
                for j, n0 in enumerate(range(0, 2688, 512)):
                    nsz = min(512, 2688 - n0)
                    b = si % 2
                    si += 1
                    p.wait(p.tensor, pup_free[2 * b], pup_free[2 * b + 1])
                    for kc in range(8):
                        ins = p.tensor.matmul(
                            pup[b][:, 0:nsz],
                            xn2[:, kc, tt * 128:(tt + 1) * 128],
                            ffup[:, kc, n0:n0 + nsz],
                            start=(kc == 0), stop=(kc == 7))
                    pe = p.inc("P", ins, 1)
                    p.wait(p.vector, "P", pe)
                    cnt = p.vop(p.vector.tensor_copy, u2[:, n0:n0 + nsz],
                                pup[b][:, 0:nsz])
                    pup_free[2 * b], pup_free[2 * b + 1] = "V", cnt
                uv = p.val("V")
                p.wait(p.scalar, "V", uv)
                ag = p.aop(p.scalar.activation, out=g2, in_=u2[:, 0:1344],
                           func=AF.Gelu)
                p.wait(p.vector, "A", ag)
                hv = p.vop(p.vector.tensor_mul, h2, g2, u2[:, 1344:2688])
                p.wait(p.tensor, "V", hv)
                p.wait(p.tensor, "V", ptf_free)
                for c in range(11):
                    if c < 10:
                        ins = p.tensor.transpose(ptf[:, c * 128:(c + 1) * 128],
                                                 h2[:, c * 128:(c + 1) * 128], ident)
                    else:
                        ins = p.tensor.transpose(ptf[0:64, 1280:1408],
                                                 h2[:, 1280:1344], ident)
                pe = p.inc("P", ins, 1)
                p.wait(p.vector, "P", pe)
                p.wait(p.vector, "P", h2f_used[bb])
                ptf_free = p.vop(p.vector.tensor_copy,
                                 h2f[bb].rearrange("p c n -> p (c n)")[:, 0:1408],
                                 ptf.rearrange("p n -> p n"))
                p.wait(p.tensor, "V", ptf_free)
                for n0 in (0, 512):
                    b = si % 2
                    si += 1
                    p.wait(p.tensor, dn_free[2 * b], dn_free[2 * b + 1])
                    for kc in range(11):
                        kk = 128 if kc < 10 else 64
                        ins = p.tensor.matmul(
                            pst[b], h2f[bb][0:kk, kc, :],
                            ffdn[0:kk, kc, n0:n0 + 512],
                            start=(kc == 0), stop=(kc == 10))
                    pe = p.inc("P", ins, 1)
                    if n0 == 512:
                        h2f_used[bb] = pe
                    sem, cnt = evac(tt, n0, pst[b], pe)
                    dn_free[2 * b], dn_free[2 * b + 1] = sem, cnt
            p.wait(p.sync, "S")
            p.stack = st
        p.stack = old


def emit_xlstm(p, x_dram, out_dram, w, scr, ident, subs=(0, 1, 2, 3),
               do_pre=True, do_post=True):
    bufs = [scr["xa"], scr["xb"], scr["xc"]]
    if do_pre:
        emit_ln_tm(p, x_dram, bufs[0])
    cur = 0
    for l in subs:
        emit_slstm_sublayer(p, l, bufs[cur % 3], bufs[(cur + 1) % 3],
                            bufs[(cur + 2) % 3], w, scr, ident)
        cur += 2
    if do_post:
        with contextlib.ExitStack() as st:
            old = p.stack
            p.stack = st
            onesr = p.sb("onesr", [1, 128], F32)
            wrow = p.sb("wrow", [1, D], F32)
            wrep = p.sb("wrep", [128, D], F32)
            pswr = p.ps("pswr", [128, 1024], F32)
            p.vop(p.vector.memset, onesr, 1.0)
            p.wait(p.sync, "V")
            lw = p.inc("L", p.sync.dma_start(out=wrow, in_=w["postw"]), 16)
            p.wait(p.tensor, "L", lw)
            emit_rowrep(p, wrow, wrep, onesr, pswr)
            p.wait(p.sync, "S")
            emit_ln_tm(p, bufs[cur % 3], out_dram, wrep=wrep, brep=None,
                       add_dram=x_dram)
            p.stack = old


def _stage(inputs):
    import ml_dtypes
    g = lambda k: np.asarray(inputs[k])
    S = {}
    S["identb"] = np.eye(128, dtype=ml_dtypes.bfloat16)
    S["identf"] = np.eye(128, dtype=np.float32)
    DHE = 32
    inv = 1.0 / (10000.0 ** (np.arange(0, DHE, 2) / DHE))
    freqs = np.arange(T)[:, None] * inv[None, :]
    emb = np.concatenate([freqs, freqs], axis=1)
    cos, sin = np.cos(emb), np.sin(emb)
    C = np.ones((128, T), np.float32)
    Sx = np.zeros((128, T), np.float32)
    C[32:64] = cos.T; C[96:128] = cos.T
    Sx[32:64] = sin.T; Sx[96:128] = sin.T
    S["cext"] = C; S["sext"] = Sx
    S["cmask"] = np.triu(np.full((128, 128), -1e30, np.float32), 1)
    selp = np.zeros((16, T), np.float32)
    seln = np.zeros((16, T), np.float32)
    for n in range(16):
        selp[n, n * 128:(n + 1) * 128] = 1.0
        seln[n, n * 128:(n + 1) * 128] = -1.0
    S["selp"] = selp; S["seln"] = seln
    S["qdT"] = np.ascontiguousarray(g("L0_attn_a_Wq_down").T)
    S["kdT"] = np.ascontiguousarray(g("L0_attn_a_Wk_down").T)
    S["vdT"] = np.ascontiguousarray(g("L0_attn_a_Wv_down").T)
    for nm in ("q_up_c", "q_up_e", "k_up_c", "k_up_e", "v_up"):
        S[nm] = np.ascontiguousarray(g("L0_attn_a_W" + nm.replace("_up", "_up")))
    S["wc"] = g("L0_attn_a_Wc")
    S["moe_gwT"] = np.ascontiguousarray(g("L0_ffn_gate_w").T)
    S["moe_ewT"] = np.ascontiguousarray(np.transpose(g("L0_ffn_expert_w"), (0, 2, 1)))
    gperm = [0, 2, 1, 3]
    wi, wf, wz, wo = (g("L0_attn_b_wi"), g("L0_attn_b_wf"),
                      g("L0_attn_b_wz"), g("L0_attn_b_wo"))
    wgs = np.stack([wi, wz, wf, wo], axis=2)
    S["xl_wg"] = np.ascontiguousarray(wgs.transpose(0, 1, 4, 2, 3).reshape(4, 4, 256, 1024))
    S["xl_rk"] = np.ascontiguousarray(
        g("L0_attn_b_rker")[:, :, :, gperm, :].reshape(4, 4, 256, 1024))
    S["xl_cb"] = np.ascontiguousarray(
        g("L0_attn_b_cell_b")[:, :, gperm, :].reshape(4, 1, 4096))
    S["xl_convw"] = np.ascontiguousarray(
        g("L0_attn_b_conv_w").reshape(4, 8, 128, 4).transpose(0, 2, 1, 3))
    S["xl_convb"] = np.ascontiguousarray(
        g("L0_attn_b_conv_b").reshape(4, 8, 128).transpose(0, 2, 1))
    S["xl_ln1w"] = g("L0_attn_b_ln1_w")
    S["xl_ln2w"] = g("L0_attn_b_ln2_w")
    S["xl_gnw"] = g("L0_attn_b_gn_w")
    S["xl_postw"] = g("L0_attn_b_post_w")[None, :]
    S["xl_ffup"] = g("L0_attn_b_ff_up")
    S["xl_ffdn"] = g("L0_attn_b_ff_down")
    def fm2(v):
        return np.ascontiguousarray(v.reshape(16, 128).T)
    for i in range(4):
        S["ma%d_inwT" % i] = np.ascontiguousarray(g("L1_attn_a_in_proj_w")[i].T)
        S["ma%d_convw" % i] = np.ascontiguousarray(
            g("L1_attn_a_conv_w")[i].reshape(16, 128, 4).transpose(1, 0, 2))
        S["ma%d_convb" % i] = fm2(g("L1_attn_a_conv_b")[i])
        S["ma%d_xpwT" % i] = np.ascontiguousarray(g("L1_attn_a_x_proj_w")[i].T)
        S["ma%d_dtpwT" % i] = np.ascontiguousarray(g("L1_attn_a_dt_proj_w")[i].T)
        S["ma%d_dtpb" % i] = fm2(g("L1_attn_a_dt_proj_b")[i])
        S["ma%d_alog" % i] = np.ascontiguousarray(
            g("L1_attn_a_A_log")[i].reshape(16, 128, 16).transpose(1, 0, 2))
        S["ma%d_dpar" % i] = fm2(g("L1_attn_a_D")[i])
        S["ma%d_opwT" % i] = np.ascontiguousarray(g("L1_attn_a_out_proj_w")[i].T)
    S["ma_norm_w"] = g("L1_attn_a_norm_w")
    S["ma_norm_b"] = g("L1_attn_a_norm_b")
    S["ma_normf_w"] = g("L1_attn_a_normf_w")[None, :]
    S["ma_normf_b"] = g("L1_attn_a_normf_b")[None, :]
    S["mb_inwT"] = np.ascontiguousarray(g("L1_attn_b_in_proj_w").T)
    S["mb_convw"] = np.ascontiguousarray(
        g("L1_attn_b_conv_w")[:, 0, :].reshape(16, 128, 4).transpose(1, 0, 2))
    S["mb_convb"] = fm2(g("L1_attn_b_conv_b"))
    S["mb_xpwT"] = np.ascontiguousarray(g("L1_attn_b_x_proj_w").T)
    S["mb_dtpwT"] = np.ascontiguousarray(g("L1_attn_b_dt_proj_w").T)
    S["mb_dtpb"] = fm2(g("L1_attn_b_dt_proj_b"))
    S["mb_alog"] = np.ascontiguousarray(
        g("L1_attn_b_A_log").reshape(16, 128, 16).transpose(1, 0, 2))
    S["mb_dpar"] = fm2(g("L1_attn_b_D_param"))
    S["mb_opwT"] = np.ascontiguousarray(g("L1_attn_b_out_proj_w").T)
    S["f_wg"] = np.ascontiguousarray(g("L1_ffn_w_gate").T)
    S["f_wu"] = np.ascontiguousarray(g("L1_ffn_w_up").T)
    S["f_wd"] = np.ascontiguousarray(g("L1_ffn_w_down").T)
    return S


_NPDT = {np.dtype(np.float32): F32, np.dtype(np.int32): I32}


class Launch:

    def __init__(self, S):
        self.p = P()
        self.S = S
        self.ext_in = {}
        self.ext_out = {}
        self.cache = {}

    def W(self, name):
        if name not in self.cache:
            a = self.S[name]
            import ml_dtypes
            dt = _NPDT.get(a.dtype, BF16 if a.dtype == np.dtype(ml_dtypes.bfloat16) else F32)
            self.cache[name] = self.p.nc.dram_tensor(
                "w_" + name, list(a.shape), dt, kind="ExternalInput").ap()
            self.ext_in[name] = a
        return self.cache[name]

    def buf_in(self, name):
        if name not in self.cache:
            self.cache[name] = self.p.nc.dram_tensor(
                name, [T, D], F32, kind="ExternalInput").ap()
            self.ext_in[name] = None
        return self.cache[name]

    def buf_out(self, name):
        if name not in self.cache:
            self.cache[name] = self.p.nc.dram_tensor(
                name, [T, D], F32, kind="ExternalOutput").ap()
            self.ext_out[name] = True
        return self.cache[name]

    def buf_int(self, name):
        if name not in self.cache:
            self.cache[name] = self.p.dram(name, [T, D], F32)
        return self.cache[name]

    def idents(self):
        p = self.p
        identb = p.sb("identb", [128, 128], BF16)
        identf = p.sb("identf", [128, 128], F32)
        p.inc("L", p.sync.dma_start(out=identb, in_=self.W("identb")), 16)
        li = p.inc("L", p.sync.dma_start(out=identf, in_=self.W("identf")), 16)
        for e in (p.tensor, p.vector, p.scalar, p.gpsimd):
            p.wait(e, "L", li)
        return identb, identf


def _blk_attn(L, xin, xout):
    w = {nm: L.W(nm) for nm in ("qdT", "kdT", "vdT", "q_up_c", "q_up_e",
                                "k_up_c", "k_up_e", "v_up", "wc", "cext",
                                "sext", "cmask")}
    emit_attn(L.p, xin, xout, w, L.identb, L.identf)


def _blk_moe(L, xin, xout):
    emit_moe(L.p, xin, xout, L.W("moe_gwT"), L.W("moe_ewT"), L.identb, L.identf)


def _blk_xlstm(L, xin, xout):
    p = L.p
    w = {"ln1w": L.W("xl_ln1w"), "ln2w": L.W("xl_ln2w"), "gnw": L.W("xl_gnw"),
         "postw": L.W("xl_postw"), "convw": L.W("xl_convw"),
         "convb": L.W("xl_convb"), "wg": L.W("xl_wg"), "rk": L.W("xl_rk"),
         "cb": L.W("xl_cb"), "ffup": L.W("xl_ffup"), "ffdn": L.W("xl_ffdn")}
    scr = {"wx": p.dram("xl_wx", [T, 4096], BF16),
           "xa": L.buf_int("xl_xa"), "xb": L.buf_int("xl_xb"),
           "xc": L.buf_int("xl_xc")}
    emit_xlstm(p, xin, xout, w, scr, L.identb)


def _mamba_scr(p):
    return {
        "z": p.dram("scr_z", [128, 16, NTT, 128], BF16),
        "nd": p.dram("scr_nd", [128, 16, NTT, 128], BF16),
        "brep": p.dram("scr_br", [128, 16, NTT, 128], BF16),
        "crep": p.dram("scr_cr", [128, 16, NTT, 128], BF16),
        "y": p.dram("scr_y", [128, 16, T], BF16),
    }


def _blk_mamba_a(L, xin, xout):
    p = L.p
    wl = []
    for i in range(4):
        d = {k: L.W("ma%d_%s" % (i, k)) for k in
             ("inwT", "convw", "convb", "xpwT", "dtpwT", "dtpb", "alog",
              "dpar", "opwT")}
        d["selp"] = L.W("selp")
        d["seln"] = L.W("seln")
        wl.append(d)
    wn = {"norm_w": L.W("ma_norm_w"), "norm_b": L.W("ma_norm_b"),
          "normf_w": L.W("ma_normf_w"), "normf_b": L.W("ma_normf_b")}
    scr = _mamba_scr(p)
    scr["res0"] = L.buf_int("ma_res0")
    scr["res1"] = L.buf_int("ma_res1")
    emit_l1_attn_a(p, xin, xout, wl, wn, scr, L.identb)


def _blk_mamba_b(L, xin, xout):
    p = L.p
    w = {k: L.W("mb_" + k) for k in ("inwT", "convw", "convb", "xpwT",
                                     "dtpwT", "dtpb", "alog", "dpar", "opwT")}
    w["selp"] = L.W("selp")
    w["seln"] = L.W("seln")
    emit_l1_attn_b(p, xin, xout, w, _mamba_scr(p), L.identb)


def _blk_l1ffn(L, xin, xout):
    emit_l1ffn(L.p, xin, xout, L.W("f_wg"), L.W("f_wu"), L.W("f_wd"), L.identb)


_CHAIN = [
    ("x0", _blk_attn, "x1"),
    ("x1", _blk_moe, "x2"),
    ("x2", _blk_xlstm, "x3"),
    ("x3", _blk_moe, "x4"),
    ("x4", _blk_mamba_a, "x5"),
    ("x5", _blk_l1ffn, "x6"),
    ("x6", _blk_mamba_b, "x7"),
    ("x7", _blk_l1ffn, "x8"),
]

GROUPS = [[0], [1], [2], [3], [4], [5], [6], [7]]


def kernel(**inputs):
    S = _stage(inputs)
    x_full = np.asarray(inputs["L0_attn_a_x"], np.float32)
    ncore = 8
    bufs = {"x0": [np.ascontiguousarray(x_full[c]) for c in range(ncore)]}
    for group in GROUPS:
        L = Launch(S)
        names_in = _CHAIN[group[0]][0]
        names_out = _CHAIN[group[-1]][2]
        xin_ap = L.buf_in(names_in)
        L.identb, L.identf = L.idents()
        cur_ap = xin_ap
        for bi in group:
            _, fn, outname = _CHAIN[bi]
            is_last = (bi == group[-1])
            out_ap = L.buf_out(outname) if is_last else L.buf_int(outname)
            fn(L, cur_ap, out_ap)
            cur_ap = out_ap
        in_maps = []
        for c in range(ncore):
            m = {}
            for nm, arr in L.ext_in.items():
                if arr is None:
                    m[nm] = bufs[nm][c]
                else:
                    m["w_" + nm] = arr
            in_maps.append(m)
        res = run_bass_kernel_spmd(L.p.nc, in_maps, core_ids=list(range(ncore)))
        for nm in L.ext_out:
            bufs[nm] = [res.results[c][nm] for c in range(ncore)]
    return np.stack(bufs["x8"], axis=0)


# version 03
# speedup vs baseline: 1.0010x; 1.0010x over previous
import contextlib
import math
import numpy as np

import concourse.bass as bass
import concourse.mybir as mybir
from concourse.bass_utils import run_bass_kernel_spmd

F32 = mybir.dt.float32
BF16 = mybir.dt.bfloat16
I32 = mybir.dt.int32
AF = mybir.ActivationFunctionType
ALU = mybir.AluOpType

T = 2048
D = 1024
NTT = T // 128
EPS = 1e-5


class P:

    def __init__(self):
        self.nc = bass.Bass("TRN2", target_bir_lowering=False)
        self._sems = {}
        self._uid = 0
        self.stack = contextlib.ExitStack()
        self.sem_prefix = ""

    def uid(self, pfx):
        self._uid += 1
        return "%s_%d" % (pfx, self._uid)

    def _sem(self, name):
        name = self.sem_prefix + name
        if name not in self._sems:
            self._sems[name] = [self.nc.alloc_semaphore(name), 0]
        return self._sems[name]

    def fence_all(self):
        for eng in (self.tensor, self.vector, self.scalar, self.gpsimd,
                    self.sync):
            for key, (h, cnt) in list(self._sems.items()):
                if key.startswith(self.sem_prefix) and cnt > 0:
                    eng.wait_ge(h, cnt)

    def inc(self, name, instr, k):
        s = self._sem(name)
        instr.then_inc(s[0], k)
        s[1] += k
        return s[1]

    def val(self, name):
        return self._sem(name)[1]

    def wait(self, eng, name, v=None):
        s = self._sem(name)
        eng.wait_ge(s[0], s[1] if v is None else v)

    def sb(self, pfx, shape, dtype):
        return self.stack.enter_context(
            self.nc.sbuf_tensor(self.uid(pfx), shape, dtype)
        ).ap()

    def ps(self, pfx, shape, dtype=F32):
        return self.stack.enter_context(
            self.nc.psum_tensor(self.uid(pfx), shape, dtype)
        ).ap()

    def dram(self, pfx, shape, dtype=F32, kind="Internal"):
        return self.nc.dram_tensor(self.uid(pfx), shape, dtype, kind=kind).ap()

    def vop(self, fn, *args, **kw):
        self.wait(self.vector, "V")
        return self.inc("V", fn(*args, **kw), 1)

    def aop(self, fn, *args, **kw):
        self.wait(self.scalar, "A")
        return self.inc("A", fn(*args, **kw), 1)

    def gop(self, fn, *args, **kw):
        self.wait(self.gpsimd, "G")
        return self.inc("G", fn(*args, **kw), 1)

    @property
    def tensor(self):
        return self.nc.tensor

    @property
    def vector(self):
        return self.nc.vector

    @property
    def scalar(self):
        return self.nc.scalar

    @property
    def gpsimd(self):
        return self.nc.gpsimd

    @property
    def sync(self):
        return self.nc.sync


def load_weight_bf16(p, dram_ap, sb_ap, sem="W"):
    instr = p.gpsimd.dma_start(out=sb_ap, in_=dram_ap)
    return p.inc(sem, instr, 16)


def emit_ln_fm(p, x_dram, out_fm, ident, nch=8, out_tm=None, gw=None, dbg=None,
               hook=None, wrep2=None, brep2=None):
    with contextlib.ExitStack() as _st:
        _old = p.stack
        p.stack = _st
        r = _emit_ln_fm_inner(p, x_dram, out_fm, ident, nch, out_tm, gw, dbg, hook,
                              wrep2, brep2)
        p.stack = _old
    return r


def _emit_ln_fm_inner(p, x_dram, out_fm, ident, nch, out_tm, gw, dbg=None, hook=None,
                      wrep2=None, brep2=None):
    Dp = nch * 128
    xt = [p.sb("lnx", [128, Dp], F32) for _ in range(2)]
    sq = p.sb("lnsq", [128, Dp], BF16)
    s1 = p.sb("lns1", [128, 1], F32)
    s2 = p.sb("lns2", [128, 1], F32)
    mean = p.sb("lnmean", [128, 1], F32)
    nm2 = p.sb("lnnm2", [128, 1], F32)
    var = p.sb("lnvar", [128, 1], F32)
    rstd = p.sb("lnrstd", [128, 1], F32)
    xn = p.sb("lnxn", [128, Dp], BF16)
    pt = [p.ps("lnps", [128, Dp], BF16) for _ in range(2)]
    last_consume = [[("V", 0)], [("V", 0)]]
    last_evac = [0, 0]
    s2_cons = 0
    nt = T // 128
    for tt in range(nt):
        b = tt % 2
        for (snm, scnt) in last_consume[b]:
            p.wait(p.sync, snm, scnt)
        ld = p.inc("L", p.sync.dma_start(out=xt[b], in_=x_dram[tt * 128:(tt + 1) * 128, :]), 16)
        p.wait(p.vector, "L", ld)
        p.vop(p.vector.reduce_sum, s1, xt[b], axis=mybir.AxisListType.X)
        p.wait(p.scalar, "L", ld)
        p.wait(p.scalar, "V", s2_cons)
        a1 = p.aop(p.scalar.activation, out=sq, in_=xt[b], func=AF.Square, accum_out=s2)
        p.vop(p.vector.tensor_scalar_mul, mean, s1, 1.0 / Dp)
        p.wait(p.vector, "A", a1)
        p.vop(p.vector.tensor_scalar, nm2, s1, mean, -1.0 / Dp, ALU.mult, ALU.mult)
        s2_cons = p.vop(p.vector.tensor_scalar, var, s2, 1.0 / Dp, nm2, ALU.mult, ALU.add)
        v1 = p.vop(p.vector.tensor_scalar_add, var, var, EPS)
        p.wait(p.scalar, "V", v1)
        p.aop(p.scalar.activation, out=rstd, in_=var, func=AF.Ln)
        a2 = p.aop(p.scalar.activation, out=rstd, in_=rstd, func=AF.Exp, scale=-0.5)
        if dbg is not None:
            p.wait(p.vector, "A", a2)
            for k, src in enumerate((s1, s2, mean, nm2, var, rstd)):
                p.vop(p.vector.tensor_copy, dbg[:, tt, k:k + 1], src)
        p.wait(p.vector, "A", a2)
        vap = p.vop(p.vector.tensor_scalar, xn, xt[b], mean, rstd, ALU.subtract, ALU.mult)
        if wrep2 is not None:
            vap = p.vop(p.vector.tensor_mul, xn, xn, wrep2)
        if brep2 is not None:
            vap = p.vop(p.vector.tensor_add, xn, xn, brep2)
        last_consume[b] = [("V", vap)]
        if hook is not None:
            last_consume[b] = last_consume[b] + list(hook(tt, xt[b], xn, mean, p.val("L")))
        if gw is not None:
            vap = p.vop(p.vector.tensor_tensor,
                        xn, xn, gw.broadcast_to((128, Dp)), ALU.mult)
        if out_tm is not None:
            vap = p.vop(p.vector.tensor_copy, out_tm[:, tt, :], xn)
        p.wait(p.tensor, "V", vap)
        p.wait(p.tensor, "V", last_evac[b])
        for k in range(nch):
            ins = p.tensor.transpose(pt[b][:, k * 128:(k + 1) * 128],
                                     xn[:, k * 128:(k + 1) * 128], ident)
        pe = p.inc("P", ins, 1)
        p.wait(p.vector, "P", pe)
        ev = p.vop(p.vector.tensor_copy,
                   out_fm[:, :, tt * 128:(tt + 1) * 128],
                   pt[b].rearrange("p (c f) -> p c f", f=128))
        last_evac[b] = ev
    return p.val("V")


def emit_mm_fm(p, w_sb, x_fm, kchunks, mchunks, evac, n_tile=512):
    with contextlib.ExitStack() as _st:
        _o = p.stack
        p.stack = _st
        nps = 2
        pst = [p.ps("mmps", [128, n_tile], F32) for _ in range(nps)]
        entry_v = p.val("V")
        last_read = [entry_v] * nps
        which = ["V"] * nps
        i = 0
        for mc in range(mchunks):
            for n0 in range(0, T, n_tile):
                b = i % nps
                i += 1
                p.wait(p.tensor, which[b], last_read[b])
                for kc in range(kchunks):
                    ins = p.tensor.matmul(
                        pst[b][:, 0:n_tile],
                        w_sb[:, kc, mc * 128:(mc + 1) * 128],
                        x_fm[:, kc, n0:n0 + n_tile],
                        start=(kc == 0), stop=(kc == kchunks - 1))
                pe = p.inc("P", ins, 1)
                sem_name, cnt = evac(mc, n0, pst[b], pe)
                which[b] = sem_name
                last_read[b] = cnt
        p.stack = _o


def emit_mm_tm(p, x_fm, w_sb, kchunks, dout, evac, psum_banks=2):
    with contextlib.ExitStack() as _st:
        _o = p.stack
        p.stack = _st
        n_tile = 512
        nps = psum_banks
        pst = [p.ps("mtps", [128, n_tile], F32) for _ in range(nps)]
        entry_v = p.val("V")
        last_read = [entry_v] * nps
        which = ["V"] * nps
        i = 0
        for tt in range(NTT):
            for n0 in range(0, dout, n_tile):
                b = i % nps
                i += 1
                p.wait(p.tensor, which[b], last_read[b])
                for kc in range(kchunks):
                    ins = p.tensor.matmul(
                        pst[b][:, 0:n_tile],
                        x_fm[:, kc, tt * 128:(tt + 1) * 128],
                        w_sb[:, kc, n0:n0 + n_tile],
                        start=(kc == 0), stop=(kc == kchunks - 1))
                pe = p.inc("P", ins, 1)
                sem_name, cnt = evac(tt, n0, pst[b], pe)
                which[b] = sem_name
                last_read[b] = cnt
        p.stack = _o


def emit_l1ffn(p, x_dram, out_dram, w_gate, w_up, w_down, ident):
    HID = 3072
    HC = HID // 128
    S127 = 1.0 / 127.0
    with contextlib.ExitStack() as st:
        old = p.stack
        p.stack = st

        h_fm = p.sb("hfm", [128, HC, T], BF16)

        with contextlib.ExitStack() as st1:
            p.stack = st1
            xn_fm = p.sb("xnfm", [128, 8, T], BF16)
            ln_done = emit_ln_fm(p, x_dram, xn_fm, ident)
            gtmp = p.sb("gtmp", [128, 512], F32)

            wg_r = w_gate.rearrange("(kc kp) n -> kp kc n", kp=128)
            wu_r = w_up.rearrange("(kc kp) n -> kp kc n", kp=128)
            wgi = [p.sb("wgi", [128, 8, 128], I32) for _ in range(2)]
            wui = [p.sb("wui", [128, 8, 128], I32) for _ in range(2)]
            wgb = [p.sb("wgb", [128, 8, 128], BF16) for _ in range(2)]
            wub = [p.sb("wub", [128, 8, 128], BF16) for _ in range(2)]
            wld = [0, 0]
            wcast = [ln_done, ln_done]
            wused = [0, 0]
            gtmp_cons = ln_done

            n_tile = 512
            pg = [p.ps("pg", [128, n_tile], F32) for _ in range(2)]
            pu = [p.ps("pu", [128, n_tile], F32) for _ in range(2)]
            last_read = [0, 0]

            def load_w(mc):
                wb = mc % 2
                p.wait(p.sync, "V", wcast[wb])
                p.inc("L", p.sync.dma_start(
                    out=wgi[wb], in_=wg_r[:, :, mc * 128:(mc + 1) * 128]), 16)
                wld[wb] = p.inc("L", p.sync.dma_start(
                    out=wui[wb], in_=wu_r[:, :, mc * 128:(mc + 1) * 128]), 16)

            def cast_w(mc):
                wb = mc % 2
                p.wait(p.vector, "L", wld[wb])
                p.wait(p.vector, "P", wused[wb])
                p.vop(p.vector.tensor_copy,
                      wgb[wb].rearrange("p a b -> p (a b)"),
                      wgi[wb].rearrange("p a b -> p (a b)"))
                wcast[wb] = p.vop(p.vector.tensor_copy,
                                  wub[wb].rearrange("p a b -> p (a b)"),
                                  wui[wb].rearrange("p a b -> p (a b)"))

            load_w(0)
            cast_w(0)
            i = 0
            for mc in range(HC):
                wb = mc % 2
                if mc + 1 < HC:
                    load_w(mc + 1)
                    cast_w(mc + 1)
                p.wait(p.tensor, "V", wcast[wb])
                for n0 in range(0, T, n_tile):
                    b = i % 2
                    i += 1
                    p.wait(p.tensor, "V", last_read[b])
                    for kc in range(8):
                        p.tensor.matmul(pg[b], wgb[wb][:, kc, :],
                                        xn_fm[:, kc, n0:n0 + n_tile],
                                        start=(kc == 0), stop=(kc == 7))
                    for kc in range(8):
                        ins = p.tensor.matmul(pu[b], wub[wb][:, kc, :],
                                              xn_fm[:, kc, n0:n0 + n_tile],
                                              start=(kc == 0), stop=(kc == 7))
                    pe = p.inc("P", ins, 1)
                    if n0 + n_tile == T:
                        wused[wb] = pe
                    p.wait(p.scalar, "P", pe)
                    p.wait(p.scalar, "V", gtmp_cons)
                    ac = p.aop(p.scalar.activation, out=gtmp, in_=pg[b],
                               func=AF.Silu, scale=S127)
                    p.wait(p.vector, "A", ac)
                    p.vop(p.vector.tensor_scalar_mul, gtmp, gtmp, S127 * S127)
                    cnt = p.vop(p.vector.tensor_mul,
                                h_fm[:, mc, n0:n0 + n_tile], gtmp, pu[b])
                    gtmp_cons = cnt
                    last_read[b] = cnt
            p.stack = st
        gate_done = p.val("V")

        wd = p.sb("wd", [128, HC, D], BF16)
        with contextlib.ExitStack() as st2:
            p.stack = st2
            wdi = p.sb("wdi", [128, 4, D], I32)
            wd_conv = gate_done
            for g4 in range(HC // 4):
                p.wait(p.sync, "V", wd_conv)
                ld = p.inc("L", p.sync.dma_start(
                    out=wdi,
                    in_=w_down.rearrange("(kc kp) n -> kp kc n", kp=128)[
                        :, g4 * 4:(g4 + 1) * 4, :]), 16)
                p.wait(p.vector, "L", ld)
                wd_conv = p.vop(p.vector.tensor_copy,
                                wd[:, g4 * 4:(g4 + 1) * 4, :], wdi)
            p.stack = st

        xres = [p.sb("xres", [128, D], F32) for _ in range(2)]
        ot = [p.sb("ot", [128, D], F32) for _ in range(2)]
        res_ld = [0] * NTT
        res_consumed = [wd_conv, wd_conv]
        st_done = [0, 0]

        def load_res(tt):
            b = tt % 2
            p.wait(p.sync, "V", res_consumed[b])
            res_ld[tt] = p.inc("L", p.sync.dma_start(
                out=xres[b], in_=x_dram[tt * 128:(tt + 1) * 128, :]), 16)

        load_res(0)
        load_res(1)

        def evac(tt, n0, ps_ap, pe_cnt):
            b = tt % 2
            p.wait(p.vector, "P", pe_cnt)
            p.wait(p.vector, "L", res_ld[tt])
            p.wait(p.vector, "S", st_done[b])
            cnt = p.vop(p.vector.tensor_add,
                        ot[b][:, n0:n0 + 512], xres[b][:, n0:n0 + 512], ps_ap)
            if n0 + 512 == D:
                res_consumed[b] = cnt
                p.wait(p.sync, "V", cnt)
                s = p.inc("S", p.sync.dma_start(
                    out=out_dram[tt * 128:(tt + 1) * 128, :], in_=ot[b]), 16)
                st_done[b] = s
                if tt + 2 < NTT:
                    load_res(tt + 2)
            return ("V", cnt)

        p.wait(p.tensor, "V", wd_conv)
        emit_mm_tm(p, h_fm, wd, HC, D, evac)
        p.wait(p.sync, "S")
        p.stack = old


def _dummy_out(p, x_dram, out_dram):
    tmp = p.sb("dmy", [128, D], F32)
    for tt in range(NTT):
        p.wait(p.sync, "S")
        p.inc("L", p.sync.dma_start(out=tmp, in_=x_dram[tt * 128:(tt + 1) * 128, :]), 16)
        p.wait(p.sync, "L")
        p.inc("S", p.sync.dma_start(out=out_dram[tt * 128:(tt + 1) * 128, :], in_=tmp), 16)
    p.wait(p.sync, "S")


def emit_attn(p, x_dram, out_dram, w, ident, ident_f32, dbg_phase=99):
    H = 16
    with contextlib.ExitStack() as st:
        old = p.stack
        p.stack = st
        xn_fm = p.sb("xnfm", [128, 8, T], BF16)
        q_fm = p.sb("qfm", [128, 8, T], BF16)
        k_fm = p.sb("kfm", [128, 8, T], BF16)
        v_tm = p.sb("vtm", [128, NTT, 1024], BF16)
        o_fm = p.sb("ofm", [128, 8, T], BF16)
        cext = p.sb("cext", [128, T], BF16)
        sext = p.sb("sext", [128, T], BF16)
        cmask = p.sb("cmask", [128, 128], F32)
        p.inc("W", p.gpsimd.dma_start(out=cext, in_=w["cext"]), 16)
        p.inc("W", p.gpsimd.dma_start(out=sext, in_=w["sext"]), 16)
        cw = p.inc("W", p.gpsimd.dma_start(out=cmask, in_=w["cmask"]), 16)

        ln_done = emit_ln_fm(p, x_dram, xn_fm, ident)
        if dbg_phase < 1:
            _dummy_out(p, x_dram, out_dram)
            p.stack = old
            return

        def build_eff(downT, upc, upe, eff, rot, gate_v):
            with contextlib.ExitStack() as s2:
                o2 = p.stack
                p.stack = s2
                dT = p.sb("dT", [128, 8, 1024], BF16)
                up2 = p.sb("up2", [128, 128], BF16)
                rtmp = p.sb("rtmp", [128, 32], BF16)
                pw = p.ps("pw", [128, 1024], F32)
                p.wait(p.gpsimd, "V", gate_v)
                p.inc("W", p.gpsimd.dma_start(
                    out=dT, in_=downT.rearrange("(kc kp) c -> kp kc c", kp=128)), 16)
                vz = p.vop(p.vector.memset, up2, 0.0)
                p.wait(p.gpsimd, "V", vz)
                if upc is None:
                    for half in (0, 64):
                        p.inc("W", p.gpsimd.dma_start(
                            out=up2[half:half + 64, half:half + 64], in_=upe), 16)
                else:
                    for half in (0, 64):
                        p.inc("W", p.gpsimd.dma_start(
                            out=up2[half:half + 64, half:half + 32], in_=upc), 16)
                        p.inc("W", p.gpsimd.dma_start(
                            out=rtmp[half:half + 64, :], in_=upe), 16)
                wle = p.val("W")
                p.wait(p.vector, "W", wle)
                if upc is not None:
                    for half in (0, 64):
                        ec = half + 32
                        ev2 = up2[half:half + 64, ec:ec + 32].rearrange(
                            "p (i two) -> p i two", two=2)
                        rv = rtmp[half:half + 64, :].rearrange(
                            "p (i two) -> p i two", two=2)
                        if rot:
                            p.vop(p.vector.tensor_scalar_mul,
                                  ev2[:, :, 0], rv[:, :, 1], -1.0)
                            p.vop(p.vector.tensor_copy, ev2[:, :, 1], rv[:, :, 0])
                        else:
                            p.vop(p.vector.tensor_copy,
                                  up2[half:half + 64, ec:ec + 32],
                                  rtmp[half:half + 64, :])
                wv = p.val("V")
                p.wait(p.tensor, "W", wle)
                p.wait(p.tensor, "V", wv)
                ev = 0
                for kcc in range(8):
                    p.wait(p.tensor, "V", ev)
                    for hp in range(H // 2):
                        ins = p.tensor.matmul(
                            pw[:, hp * 128:(hp + 1) * 128],
                            dT[:, hp, kcc * 128:(kcc + 1) * 128],
                            up2, start=True, stop=True)
                    pe = p.inc("P", ins, 1)
                    p.wait(p.vector, "P", pe)
                    ev = p.vop(p.vector.tensor_copy, eff[:, kcc, :], pw)
                p.stack = o2
            return ev

        def project_fm(eff, dst):
            def evac(mc, n0, ps_ap, pe_cnt):
                p.wait(p.vector, "P", pe_cnt)
                cnt = p.vop(p.vector.tensor_copy, dst[:, mc, n0:n0 + 512], ps_ap)
                return ("V", cnt)
            emit_mm_fm(p, eff, xn_fm, 8, 8, evac)

        with contextlib.ExitStack() as s3:
            o3 = p.stack
            p.stack = s3
            eff = p.sb("eff", [128, 8, 1024], BF16)
            rot_fm = o_fm
            gate = ln_done
            for (fam, dst) in (("q", q_fm), ("k", k_fm)):
                build_eff(w[fam + "dT"], w[fam + "_up_c"], w[fam + "_up_e"],
                          eff, rot=False, gate_v=gate)
                p.wait(p.tensor, "V")
                project_fm(eff, dst)
                build_eff(w[fam + "dT"], w[fam + "_up_c"], w[fam + "_up_e"],
                          eff, rot=True, gate_v=p.val("V"))
                p.wait(p.tensor, "V")
                project_fm(eff, rot_fm)
                p.wait(p.vector, "W", cw)
                for c in range(8):
                    p.vop(p.vector.tensor_tensor, rot_fm[:, c, :], rot_fm[:, c, :],
                          sext, ALU.mult)
                    p.vop(p.vector.tensor_tensor, dst[:, c, :], dst[:, c, :],
                          cext, ALU.mult)
                    p.vop(p.vector.tensor_add, dst[:, c, :], dst[:, c, :],
                          rot_fm[:, c, :])
                gate = p.val("V")
            build_eff(w["vdT"], None, w["v_up"], eff, rot=False, gate_v=gate)
            p.wait(p.tensor, "V")

            def evac_v(tt, n0, ps_ap, pe_cnt):
                p.wait(p.vector, "P", pe_cnt)
                cnt = p.vop(p.vector.tensor_copy, v_tm[:, tt, n0:n0 + 512], ps_ap)
                return ("V", cnt)
            emit_mm_tm(p, xn_fm, eff.rearrange("p a b -> p a b"), 8, 1024, evac_v)
            p.stack = o3
        qkv_done = p.val("V")
        if dbg_phase < 2:
            _dummy_out(p, x_dram, out_dram)
            p.stack = old
            return

        with contextlib.ExitStack() as s4:
            o4 = p.stack
            p.stack = s4
            ps_s = p.ps("pss", [128, T], F32)
            ps_pt = [p.ps("pspt", [128, 512], BF16) for _ in range(2)]
            ps_o = p.ps("pso", [64, 128], F32)
            p_sb = p.sb("psb", [128, T], BF16)
            pt_sb = [p.sb("ptsb", [128, 512], BF16) for _ in range(2)]
            rmax = p.sb("rmax", [128, 1], F32)
            nsm = p.sb("nsm", [128, 1], F32)
            lsum = p.sb("lsum", [128, 1], F32)
            rl = p.sb("rl", [128, 1], F32)
            s_free = ("V", qkv_done)
            o_free = ("V", 0)
            pt_free = [("V", 0), ("V", 0)]
            first = True
            pb = [0]

            def pe_base(b):
                if pb[0] != b:
                    p.wait(p.tensor, "P", p.val("P"))
                    pb[0] = b
            for h in range(min(H, getattr(emit_attn, "h_limit", H))):
                ch, base = h // 2, (h % 2) * 64
                for qt in range(min(NTT, getattr(emit_attn, "qt_limit", NTT))):
                    nk = (qt + 1) * 128
                    p.wait(p.tensor, *s_free)
                    if first:
                        p.wait(p.tensor, "V", qkv_done)
                        first = False
                    pe_base(base)
                    for n0 in range(0, nk, 512):
                        n1 = min(n0 + 512, nk)
                        ins = p.tensor.matmul(
                            ps_s[:, n0:n1],
                            q_fm[base:base + 64, ch, qt * 128:(qt + 1) * 128],
                            k_fm[base:base + 64, ch, n0:n1],
                            start=True, stop=True)
                    pe_s = p.inc("P", ins, 1)
                    p.wait(p.vector, "P", pe_s)
                    p.vop(p.vector.tensor_add, ps_s[:, nk - 128:nk],
                          ps_s[:, nk - 128:nk], cmask)
                    p.vop(p.vector.reduce_max, rmax, ps_s[:, 0:nk],
                          axis=mybir.AxisListType.X)
                    vm = p.vop(p.vector.tensor_scalar_mul, nsm, rmax, -0.125)
                    p.wait(p.scalar, "V", vm)
                    ae = p.aop(p.scalar.activation, out=p_sb[:, 0:nk],
                               in_=ps_s[:, 0:nk], func=AF.Exp, scale=0.125,
                               bias=nsm, accum_out=lsum)
                    s_free = ("A", ae)
                    p.wait(p.vector, "A", ae)
                    p.vop(p.vector.reciprocal, rl, lsum)
                    vp = p.vop(p.vector.tensor_scalar_mul, p_sb[:, 0:nk],
                               p_sb[:, 0:nk], rl)
                    nch = nk // 128
                    for g0 in range(0, nch, 4):
                        g1 = min(g0 + 4, nch)
                        bb = (g0 // 4) % 2
                        p.wait(p.tensor, *pt_free[bb])
                        p.wait(p.tensor, "V", vp)
                        pe_base(0)
                        for c in range(g0, g1):
                            ins = p.tensor.transpose(
                                ps_pt[bb][:, (c - g0) * 128:(c - g0 + 1) * 128],
                                p_sb[:, c * 128:(c + 1) * 128], ident)
                        pe_t = p.inc("P", ins, 1)
                        p.wait(p.vector, "P", pe_t)
                        ev = p.vop(p.vector.tensor_copy,
                                   pt_sb[bb][:, 0:(g1 - g0) * 128],
                                   ps_pt[bb][:, 0:(g1 - g0) * 128])
                        pt_free[bb] = ("V", ev)
                        p.wait(p.tensor, "V", ev)
                        if g0 == 0:
                            p.wait(p.tensor, *o_free)
                        for c in range(g0, g1):
                            ins = p.tensor.matmul(
                                ps_o,
                                v_tm[:, c, h * 64:(h + 1) * 64],
                                pt_sb[bb][:, (c - g0) * 128:(c - g0 + 1) * 128],
                                start=(c == 0), stop=(c == nch - 1))
                        pe_o = p.inc("P", ins, 1)
                    p.wait(p.vector, "P", pe_o)
                    vo = p.vop(p.vector.tensor_copy,
                               o_fm[base:base + 64, ch, qt * 128:(qt + 1) * 128],
                               ps_o)
                    o_free = ("V", vo)
            p.stack = o4
        attn_done = p.val("V")
        if dbg_phase < 3:
            _dummy_out(p, x_dram, out_dram)
            p.stack = old
            return

        with contextlib.ExitStack() as s5:
            o5 = p.stack
            p.stack = s5
            wc = p.sb("wc", [128, 8, 1024], BF16)
            p.wait(p.gpsimd, "V", attn_done)
            wcl = p.inc("W", p.gpsimd.dma_start(
                out=wc, in_=w["wc"].rearrange("(kc kp) n -> kp kc n", kp=128)), 16)
            p.wait(p.tensor, "W", wcl)
            xres = [p.sb("xres", [128, D], F32) for _ in range(2)]
            ot = [p.sb("ot", [128, D], F32) for _ in range(2)]
            res_ld = [0] * NTT
            res_consumed = [attn_done, attn_done]
            st_done = [0, 0]

            def load_res(tt):
                b = tt % 2
                p.wait(p.sync, "V", res_consumed[b])
                res_ld[tt] = p.inc("L", p.sync.dma_start(
                    out=xres[b], in_=x_dram[tt * 128:(tt + 1) * 128, :]), 16)

            load_res(0)
            load_res(1)

            def evac(tt, n0, ps_ap, pe_cnt):
                b = tt % 2
                p.wait(p.vector, "P", pe_cnt)
                p.wait(p.vector, "L", res_ld[tt])
                p.wait(p.vector, "S", st_done[b])
                cnt = p.vop(p.vector.tensor_add,
                            ot[b][:, n0:n0 + 512], xres[b][:, n0:n0 + 512], ps_ap)
                if n0 + 512 == D:
                    res_consumed[b] = cnt
                    p.wait(p.sync, "V", cnt)
                    s = p.inc("S", p.sync.dma_start(
                        out=out_dram[tt * 128:(tt + 1) * 128, :], in_=ot[b]), 16)
                    st_done[b] = s
                    if tt + 2 < NTT:
                        load_res(tt + 2)
                return ("V", cnt)

            p.wait(p.tensor, "V", attn_done)
            emit_mm_tm(p, o_fm, wc, 8, D, evac)
            p.wait(p.sync, "S")
            p.stack = o5
        p.stack = old


def emit_moe(p, x_dram, out_dram, gate_wt, expert_wt, ident, ident_f32):
    with contextlib.ExitStack() as st:
        old = p.stack
        p.stack = st
        xn_fm = p.sb("xnfm", [128, 8, T], BF16)
        acc = p.sb("acc", [128, NTT, D], F32)
        scores = p.sb("scores", [128, NTT, 8], F32)
        onehot = p.sb("onehot", [128, NTT, 8], F32)
        gwt = p.sb("gwt", [128, 8, 8], F32)
        srep = p.sb("srep", [128, 8], F32)
        onesk = p.sb("onesk", [128, 1], F32)
        onesr = p.sb("onesr", [1, 128], F32)
        srow = p.sb("srow", [1, 8], F32)
        xfs = p.sb("xfs", [128, 8, 128], F32)
        negmean = p.sb("negmean", [128, 1], F32)
        rm = p.sb("rm", [128, 1], F32)
        ptx = p.ps("ptx", [128, 1024], F32)
        psc = p.ps("psc", [128, 8], F32)
        psr = p.ps("psr", [128, 8], F32)

        gld = p.inc("L", p.sync.dma_start(
            out=gwt, in_=gate_wt.rearrange("(kc kp) e -> kp kc e", kp=128)), 16)
        p.vop(p.vector.memset, onesk, 1.0)
        p.vop(p.vector.memset, onesr, 1.0)
        p.wait(p.tensor, "L", gld)
        p.wait(p.tensor, "V")
        for kc in range(8):
            ins = p.tensor.matmul(psr[0:1, :], onesk, gwt[:, kc, :],
                                  start=(kc == 0), stop=(kc == 7))
        pe = p.inc("P", ins, 1)
        p.wait(p.vector, "P", pe)
        vs = p.vop(p.vector.tensor_copy, srow, psr[0:1, :])
        p.wait(p.tensor, "V", vs)
        ins = p.tensor.matmul(psr[:, :], onesr, srow, start=True, stop=True)
        pe = p.inc("P", ins, 1)
        p.wait(p.vector, "P", pe)
        p.vop(p.vector.tensor_copy, srep, psr)

        hook_state = {"ptx_free": ("V", 0), "pscore_free": ("V", 0)}

        def hook(tt, xt, xn, mean, ld):
            p.vop(p.vector.tensor_scalar_mul, negmean, mean, -1.0)
            p.wait(p.tensor, *hook_state["ptx_free"])
            for k in range(8):
                ins = p.tensor.transpose(ptx[:, k * 128:(k + 1) * 128],
                                         xt[:, k * 128:(k + 1) * 128], ident_f32)
            pe1 = p.inc("P", ins, 1)
            p.wait(p.vector, "P", pe1)
            ev = p.vop(p.vector.tensor_copy, xfs,
                       ptx.rearrange("p (c f) -> p c f", f=128))
            hook_state["ptx_free"] = ("V", ev)
            p.wait(p.tensor, "V", ev)
            for kc in range(8):
                ins = p.tensor.matmul(psc, xfs[:, kc, :], gwt[:, kc, :],
                                      start=(kc == 0), stop=(kc == 7))
            pe2 = p.inc("P", ins, 1)
            p.wait(p.vector, "P", pe2)
            p.vop(p.vector.scalar_tensor_tensor, scores[:, tt, :], srep, negmean,
                  psc, ALU.mult, ALU.add)
            p.vop(p.vector.reduce_max, rm, scores[:, tt, :], axis=mybir.AxisListType.X)
            p.vop(p.vector.tensor_single_scalar, onehot[:, tt, :], scores[:, tt, :],
                  rm, ALU.is_equal)
            p.vop(p.vector.tensor_copy, acc[:, tt, :], xn)
            va = p.vop(p.vector.tensor_add, acc[:, tt, :], acc[:, tt, :], xt)
            return [("V", va), ("P", pe1)]

        ln_done = emit_ln_fm(p, x_dram, xn_fm, ident, hook=hook)
        hook_done_v = p.val("V")

        with contextlib.ExitStack() as st2:
            p.stack = st2
            ew = [p.sb("ew", [128, 8, D], BF16) for _ in range(2)]
            pex = [p.ps("pex", [128, 512], F32) for _ in range(2)]
            ewr = expert_wt.rearrange("e (kc kp) f -> e kp kc f", kp=128)
            wld = [0, 0]
            wused = [0, 0]
            last_read = [0, 0]
            first_gate = [hook_done_v, hook_done_v]

            def load_e(e):
                wb = e % 2
                p.wait(p.gpsimd, "P", wused[wb])
                p.wait(p.gpsimd, "V", first_gate[wb])
                first_gate[wb] = 0
                wld[wb] = p.inc("W", p.gpsimd.dma_start(out=ew[wb], in_=ewr[e]), 16)

            load_e(0)
            load_e(1)
            i = 0
            for e in range(8):
                wb = e % 2
                p.wait(p.tensor, "W", wld[wb])
                for tt in range(NTT):
                    for n0 in (0, 512):
                        b = i % 2
                        i += 1
                        p.wait(p.tensor, "V", last_read[b])
                        for kc in range(8):
                            ins = p.tensor.matmul(
                                pex[b], xn_fm[:, kc, tt * 128:(tt + 1) * 128],
                                ew[wb][:, kc, n0:n0 + 512],
                                start=(kc == 0), stop=(kc == 7))
                        pe = p.inc("P", ins, 1)
                        if tt == NTT - 1 and n0 == 512:
                            wused[wb] = pe
                        p.wait(p.vector, "P", pe)
                        cnt = p.vop(p.vector.scalar_tensor_tensor,
                                    acc[:, tt, n0:n0 + 512], pex[b],
                                    onehot[:, tt, e:e + 1],
                                    acc[:, tt, n0:n0 + 512], ALU.mult, ALU.add)
                        last_read[b] = cnt
                if e + 2 < 8:
                    load_e(e + 2)
            p.stack = st

        acc_done = p.val("V")
        p.wait(p.sync, "V", acc_done)
        for tt in range(NTT):
            p.inc("S", p.sync.dma_start(
                out=out_dram[tt * 128:(tt + 1) * 128, :], in_=acc[:, tt, :]), 16)
        p.wait(p.sync, "S")
        p.stack = old


def emit_mamba(p, xn_fm, w, scr, evac_out, gate_v=0):
    EC = 16
    with contextlib.ExitStack() as st:
        old = p.stack
        p.stack = st
        convw = p.sb("convw", [128, EC, 4], F32)
        convb = p.sb("convb", [128, EC], F32)
        negdtb = p.sb("negdtb", [128, EC], F32)
        apos = p.sb("apos", [128, EC, 16], F32)
        dpar = p.sb("dpar", [128, EC], F32)
        selp = p.sb("selp", [16, T], BF16)
        seln = p.sb("seln", [16, T], BF16)
        p.wait(p.gpsimd, "V", gate_v)
        p.inc("W", p.gpsimd.dma_start(out=convw, in_=w["convw"]), 16)
        p.inc("W", p.gpsimd.dma_start(out=convb, in_=w["convb"]), 16)
        p.inc("W", p.gpsimd.dma_start(out=negdtb, in_=w["dtpb"]), 16)
        p.inc("W", p.gpsimd.dma_start(out=apos, in_=w["alog"]), 16)
        p.inc("W", p.gpsimd.dma_start(out=dpar, in_=w["dpar"]), 16)
        p.inc("W", p.gpsimd.dma_start(out=selp, in_=w["selp"]), 16)
        wsm = p.inc("W", p.gpsimd.dma_start(out=seln, in_=w["seln"]), 16)
        p.wait(p.vector, "W", wsm)
        p.vop(p.vector.tensor_scalar_mul, negdtb, negdtb, -1.0)
        p.wait(p.scalar, "W", wsm)
        p.aop(p.scalar.activation, out=apos.rearrange("p a b -> p (a b)"),
              in_=apos.rearrange("p a b -> p (a b)"), func=AF.Exp)

        stbuf = [p.sb("stg", [128, 512], BF16) for _ in range(2)]
        st_state = {"free": [0, 0], "i": 0}

        def stage_store(src_ap, dst_ap, pe_cnt=None):
            b = st_state["i"] % 2
            st_state["i"] += 1
            if pe_cnt is not None:
                p.wait(p.vector, "P", pe_cnt)
            p.wait(p.vector, "S", st_state["free"][b])
            cnt = p.vop(p.vector.tensor_copy, stbuf[b], src_ap)
            p.wait(p.sync, "V", cnt)
            s = p.inc("S", p.sync.dma_start(out=dst_ap, in_=stbuf[b]), 16)
            st_state["free"][b] = s
            return ("V", cnt)

        stx = contextlib.ExitStack()
        p.stack = stx
        xh = p.sb("xh", [128, EC, T], BF16)
        with contextlib.ExitStack() as s1:
            p.stack = s1
            inw = p.sb("inw", [128, 8, 4096], BF16)
            p.wait(p.gpsimd, "V")
            wl = p.inc("W", p.gpsimd.dma_start(
                out=inw, in_=w["inwT"].rearrange("(kc kp) n -> kp kc n", kp=128)), 16)
            p.wait(p.tensor, "W", wl)

            def evac1(mc, n0, ps_ap, pe_cnt):
                if mc < EC:
                    p.wait(p.vector, "P", pe_cnt)
                    cnt = p.vop(p.vector.tensor_copy, xh[:, mc, n0:n0 + 512], ps_ap)
                    return ("V", cnt)
                return stage_store(ps_ap, scr["z"][:, mc - EC, n0 // 128:n0 // 128 + 4, :]
                                   .rearrange("p a b -> p (a b)"), pe_cnt)

            emit_mm_fm(p, inw, xn_fm, 8, 32, evac1)
            p.stack = stx

        dbg_xh = getattr(emit_mamba, "dbg_xh", None)
        if dbg_xh is not None:
            p.wait(p.sync, "V")
            p.inc("S", p.sync.dma_start(out=dbg_xh, in_=xh), 16)
            p.wait(p.vector, "S")
            emit_mamba.dbg_xh = None
        acc = p.sb("cacc", [128, T], F32)
        sil2d = p.sb("sil2d", [128, T], BF16)
        for dc in range(EC):
            p.vop(p.vector.tensor_scalar_mul, acc, xh[:, dc, :],
                  convw[:, dc, 3:4])
            for s in (1, 2, 3):
                p.vop(p.vector.scalar_tensor_tensor, acc[:, s:T],
                      xh[:, dc, 0:T - s], convw[:, dc, 3 - s:4 - s],
                      acc[:, s:T], ALU.mult, ALU.add)
            vc = p.val("V")
            p.wait(p.scalar, "V", vc)
            av = p.aop(p.scalar.activation, out=sil2d, in_=acc, func=AF.Silu,
                       bias=convb[:, dc:dc + 1])
            p.wait(p.vector, "A", av)
            p.vop(p.vector.tensor_copy, xh[:, dc, :], sil2d)
        u_done = p.val("A")
        p.wait(p.vector, "A", u_done)
        dbg_u = getattr(emit_mamba, "dbg_u", None)
        if dbg_u is not None:
            p.wait(p.sync, "A", u_done)
            p.inc("S", p.sync.dma_start(out=dbg_u, in_=xh), 16)
            emit_mamba.dbg_u = None

        with contextlib.ExitStack() as s3:
            p.stack = s3
            xpw = p.sb("xpw", [128, EC, 96], BF16)
            dtpw = p.sb("dtpw", [64, T], BF16)
            dtr = p.sb("dtr", [64, T], BF16)
            bm = p.sb("bm", [16, T], BF16)
            cm = p.sb("cm", [16, T], BF16)
            sgt = p.sb("sgt", [128, 512], F32)
            pj = [p.ps("pj", [128, 512], F32) for _ in range(2)]
            p.wait(p.gpsimd, "V")
            p.inc("W", p.gpsimd.dma_start(
                out=xpw, in_=w["xpwT"].rearrange("(kc kp) n -> kp kc n", kp=128)), 16)
            wl = p.inc("W", p.gpsimd.dma_start(out=dtpw, in_=w["dtpwT"]), 16)
            p.wait(p.tensor, "W", wl)
            p.wait(p.tensor, "A", u_done)
            entry_v = p.val("V")
            bank_free = [("V", entry_v), ("V", entry_v)]
            ib = [0]

            def mm_group(lhsT_slices, rhs_slices, n0, nsz, M):
                b = ib[0] % 2
                ib[0] += 1
                p.wait(p.tensor, *bank_free[b])
                for ki, (lh, rh) in enumerate(zip(lhsT_slices, rhs_slices)):
                    ins = p.tensor.matmul(pj[b][0:M, 0:nsz], lh, rh,
                                          start=(ki == 0),
                                          stop=(ki == len(lhsT_slices) - 1))
                return b, p.inc("P", ins, 1)

            for n0 in range(0, T, 512):
                for (dst, c0, c1) in ((dtr, 0, 64), (bm, 64, 80), (cm, 80, 96)):
                    M = c1 - c0
                    b, pe = mm_group(
                        [xpw[:, kc, c0:c1] for kc in range(EC)],
                        [xh[:, kc, n0:n0 + 512] for kc in range(EC)],
                        n0, 512, M)
                    p.wait(p.vector, "P", pe)
                    cnt = p.vop(p.vector.tensor_copy, dst[0:M, n0:n0 + 512],
                                pj[b][0:M, 0:512])
                    bank_free[b] = ("V", cnt)
            for mc in range(EC):
                for n0 in range(0, T, 512):
                    b, pe = mm_group(
                        [dtpw[:, mc * 128:(mc + 1) * 128]],
                        [dtr[:, n0:n0 + 512]], n0, 512, 128)
                    p.wait(p.scalar, "P", pe)
                    ac = p.aop(p.scalar.activation, out=sgt, in_=pj[b][:, 0:512],
                               func=AF.Sigmoid, scale=-1.0,
                               bias=negdtb[:, mc:mc + 1])
                    bank_free[b] = ("A", ac)
                    b2 = st_state["i"] % 2
                    st_state["i"] += 1
                    p.wait(p.scalar, "S", st_state["free"][b2])
                    ac2 = p.aop(p.scalar.activation, out=stbuf[b2], in_=sgt,
                                func=AF.Ln)
                    p.wait(p.sync, "A", ac2)
                    s = p.inc("S", p.sync.dma_start(
                        out=scr["nd"][:, mc, n0 // 128:n0 // 128 + 4, :]
                        .rearrange("p a b -> p (a b)"), in_=stbuf[b2]), 16)
                    st_state["free"][b2] = s
            for n in range(16):
                for n0 in range(0, T, 512):
                    for (src, sel, dst) in ((bm, seln, scr["brep"]),
                                            (cm, selp, scr["crep"])):
                        b, pe = mm_group(
                            [sel[:, n * 128:(n + 1) * 128]],
                            [src[:, n0:n0 + 512]], n0, 512, 128)
                        bank_free[b] = stage_store(
                            pj[b][:, 0:512],
                            dst[:, n, n0 // 128:n0 // 128 + 4, :]
                            .rearrange("p a b -> p (a b)"), pe)
            p.stack = stx
        p3_done = p.val("S")

        with contextlib.ExitStack() as s4:
            p.stack = s4
            du = p.sb("du", [128, T], BF16)
            ndt = p.sb("ndt", [128, T], BF16)
            dA = p.sb("dA", [128, T], BF16)
            dBu = p.sb("dBu", [128, T], BF16)
            hh = p.sb("hh", [128, T], BF16)
            yac = p.sb("yac", [128, T], F32)
            zs = p.sb("zs", [128, T], BF16)
            bc = [p.sb("bcb", [128, T], BF16) for _ in range(2)]
            cc = [p.sb("ccb", [128, T], BF16) for _ in range(2)]
            bc_ld = [0, 0]
            bc_used = [0, 0]
            p.wait(p.sync, "S", p3_done)
            nd_ld = 0
            nd_used = 0

            def load_bc(dc, n):
                b = n % 2
                p.wait(p.sync, "V", bc_used[b])
                p.inc("L", p.sync.dma_start(
                    out=bc[b], in_=scr["brep"][:, n, :, :].rearrange("p a b -> p (a b)")), 16)
                bc_ld[b] = p.inc("L", p.sync.dma_start(
                    out=cc[b], in_=scr["crep"][:, n, :, :].rearrange("p a b -> p (a b)")), 16)

            for dc in range(EC):
                p.wait(p.sync, "V", nd_used)
                ndl = p.inc("L", p.sync.dma_start(
                    out=ndt, in_=scr["nd"][:, dc, :, :].rearrange("p a b -> p (a b)")), 16)
                p.wait(p.vector, "L", ndl)
                p.vop(p.vector.tensor_mul, du, xh[:, dc, :], ndt)
                load_bc(dc, 0)
                load_bc(dc, 1)
                for n in range(16):
                    b = n % 2
                    p.wait(p.scalar, "L", ndl)
                    p.wait(p.scalar, "V")
                    av = p.aop(p.scalar.activation, out=dA, in_=ndt, func=AF.Exp,
                               scale=apos[:, dc, n:n + 1])
                    p.wait(p.vector, "L", bc_ld[b])
                    p.vop(p.vector.tensor_mul, dBu, du, bc[b])
                    p.wait(p.vector, "A", av)
                    p.vop(p.vector.tensor_tensor_scan, hh, dA, dBu, 0.0,
                          ALU.mult, ALU.add)
                    p.vop(p.vector.tensor_mul, hh, hh, cc[b])
                    if n == 0:
                        cnt = p.vop(p.vector.tensor_copy, yac, hh)
                    else:
                        cnt = p.vop(p.vector.tensor_add, yac, yac, hh)
                    bc_used[b] = cnt
                    if n + 2 < 16:
                        load_bc(dc, n + 2)
                nd_used = p.val("V")
                p.vop(p.vector.scalar_tensor_tensor, yac, xh[:, dc, :],
                      dpar[:, dc:dc + 1], yac, ALU.mult, ALU.add)
                zl = p.inc("L", p.sync.dma_start(
                    out=zs, in_=scr["z"][:, dc, :, :].rearrange("p a b -> p (a b)")), 16)
                p.wait(p.scalar, "L", zl)
                p.wait(p.scalar, "V")
                az = p.aop(p.scalar.activation, out=zs, in_=zs, func=AF.Silu)
                p.wait(p.vector, "A", az)
                yv = p.vop(p.vector.tensor_mul, hh, yac, zs)
                p.wait(p.sync, "V", yv)
                p.inc("S", p.sync.dma_start(out=scr["y"][:, dc, :], in_=hh), 16)
            p.stack = stx
        stx.close()
        p.stack = st
        y_done = p.val("S")

        with contextlib.ExitStack() as s5:
            p.stack = s5
            opw = p.sb("opw", [128, EC, 1024], BF16)
            p.wait(p.gpsimd, "V")
            wl = p.inc("W", p.gpsimd.dma_start(
                out=opw, in_=w["opwT"].rearrange("(kc kp) n -> kp kc n", kp=128)), 16)
            yt = p.sb("yt", [128, EC, T], BF16)
            p.wait(p.sync, "S", y_done)
            p.wait(p.sync, "V")
            yl = p.inc("L", p.sync.dma_start(out=yt, in_=scr["y"]), 16)
            p.wait(p.tensor, "W", wl)
            p.wait(p.tensor, "L", yl)
            with contextlib.ExitStack() as s5b:
                p.stack = s5b
                n_tile = 512
                pst = [p.ps("mt", [128, n_tile], F32) for _ in range(2)]
                entry_v = p.val("V")
                last_read = [("V", entry_v), ("V", entry_v)]
                i = 0
                for tt in range(NTT):
                    for n0 in range(0, 1024, n_tile):
                        b = i % 2
                        i += 1
                        p.wait(p.tensor, *last_read[b])
                        for kc in range(EC):
                            ins = p.tensor.matmul(
                                pst[b], yt[:, kc, tt * 128:(tt + 1) * 128],
                                opw[:, kc, n0:n0 + n_tile],
                                start=(kc == 0), stop=(kc == EC - 1))
                        pe = p.inc("P", ins, 1)
                        last_read[b] = evac_out(tt, n0, pst[b], pe)
                p.stack = s5
            p.stack = st
        p.stack = old


def make_res_evac(p, x_dram, out_dram, gate_v):
    xres = [p.sb("xres", [128, D], F32) for _ in range(2)]
    ot = [p.sb("ot", [128, D], F32) for _ in range(2)]
    res_ld = [0] * NTT
    res_consumed = [gate_v, gate_v]
    st_done = [0, 0]

    def load_res(tt):
        b = tt % 2
        p.wait(p.sync, "V", res_consumed[b])
        res_ld[tt] = p.inc("L", p.sync.dma_start(
            out=xres[b], in_=x_dram[tt * 128:(tt + 1) * 128, :]), 16)

    load_res(0)
    load_res(1)

    def evac(tt, n0, ps_ap, pe_cnt):
        b = tt % 2
        p.wait(p.vector, "P", pe_cnt)
        p.wait(p.vector, "L", res_ld[tt])
        p.wait(p.vector, "S", st_done[b])
        cnt = p.vop(p.vector.tensor_add,
                    ot[b][:, n0:n0 + 512], xres[b][:, n0:n0 + 512], ps_ap)
        if n0 + 512 == D:
            res_consumed[b] = cnt
            p.wait(p.sync, "V", cnt)
            s = p.inc("S", p.sync.dma_start(
                out=out_dram[tt * 128:(tt + 1) * 128, :], in_=ot[b]), 16)
            st_done[b] = s
            if tt + 2 < NTT:
                load_res(tt + 2)
        return ("V", cnt)

    return evac


def emit_rowrep(p, row_sb, dst_sb, onesr, pswr):
    p.wait(p.tensor, "V")
    for n0 in range(0, D, 512):
        ins = p.tensor.matmul(pswr[:, n0 % 1024:n0 % 1024 + 512],
                              onesr, row_sb[0:1, n0:n0 + 512],
                              start=True, stop=True)
        pe = p.inc("P", ins, 1)
        p.wait(p.vector, "P", pe)
        p.vop(p.vector.tensor_copy, dst_sb[:, n0:n0 + 512],
              pswr[:, n0 % 1024:n0 % 1024 + 512])
    return p.val("V")


def emit_ln_tm(p, x_dram, out_dram, wrep=None, brep=None, add_dram=None):
    with contextlib.ExitStack() as _st:
        _old = p.stack
        p.stack = _st
        xt = [p.sb("lpx", [128, D], F32) for _ in range(2)]
        ad = [p.sb("lpa", [128, D], F32) for _ in range(2)]
        sq = p.sb("lpsq", [128, D], BF16)
        s1 = p.sb("lps1", [128, 1], F32)
        s2 = p.sb("lps2", [128, 1], F32)
        mean = p.sb("lpmean", [128, 1], F32)
        nm2 = p.sb("lpnm2", [128, 1], F32)
        var = p.sb("lpvar", [128, 1], F32)
        rstd = p.sb("lprstd", [128, 1], F32)
        ot = [p.sb("lpot", [128, D], F32) for _ in range(2)]
        cons = [0, 0]
        st_done = [0, 0]
        s2_cons = 0
        for tt in range(NTT):
            b = tt % 2
            p.wait(p.sync, "V", cons[b])
            ld = p.inc("L", p.sync.dma_start(
                out=xt[b], in_=x_dram[tt * 128:(tt + 1) * 128, :]), 16)
            if add_dram is not None:
                ld = p.inc("L", p.sync.dma_start(
                    out=ad[b], in_=add_dram[tt * 128:(tt + 1) * 128, :]), 16)
            p.wait(p.vector, "L", ld)
            p.vop(p.vector.reduce_sum, s1, xt[b], axis=mybir.AxisListType.X)
            p.wait(p.scalar, "L", ld)
            p.wait(p.scalar, "V", s2_cons)
            a1 = p.aop(p.scalar.activation, out=sq, in_=xt[b], func=AF.Square,
                       accum_out=s2)
            p.vop(p.vector.tensor_scalar_mul, mean, s1, 1.0 / D)
            p.wait(p.vector, "A", a1)
            p.vop(p.vector.tensor_scalar, nm2, s1, mean, -1.0 / D, ALU.mult, ALU.mult)
            s2_cons = p.vop(p.vector.tensor_scalar, var, s2, 1.0 / D, nm2,
                            ALU.mult, ALU.add)
            v1 = p.vop(p.vector.tensor_scalar_add, var, var, EPS)
            p.wait(p.scalar, "V", v1)
            p.aop(p.scalar.activation, out=rstd, in_=var, func=AF.Ln)
            a2 = p.aop(p.scalar.activation, out=rstd, in_=rstd, func=AF.Exp,
                       scale=-0.5)
            p.wait(p.vector, "A", a2)
            p.wait(p.vector, "S", st_done[b])
            cnt = p.vop(p.vector.tensor_scalar, ot[b], xt[b], mean, rstd,
                        ALU.subtract, ALU.mult)
            if wrep is not None:
                cnt = p.vop(p.vector.tensor_mul, ot[b], ot[b], wrep)
            if brep is not None:
                cnt = p.vop(p.vector.tensor_add, ot[b], ot[b], brep)
            if add_dram is not None:
                cnt = p.vop(p.vector.tensor_add, ot[b], ot[b], ad[b])
            cons[b] = cnt
            p.wait(p.sync, "V", cnt)
            st_done[b] = p.inc("S", p.sync.dma_start(
                out=out_dram[tt * 128:(tt + 1) * 128, :], in_=ot[b]), 16)
        p.wait(p.sync, "S")
        p.stack = _old


def emit_l1_attn_a(p, x_dram, out_dram, wl, wn, scr, ident):
    with contextlib.ExitStack() as st:
        old = p.stack
        p.stack = st
        onesr = p.sb("onesr", [1, 128], F32)
        wrow = p.sb("wrow", [1, D], F32)
        wrep = p.sb("wrep", [128, D], F32)
        brep = p.sb("brep2", [128, D], F32)
        pswr = p.ps("pswr", [128, 1024], F32)
        p.vop(p.vector.memset, onesr, 1.0)

        def load_reps(w_ap, b_ap):
            p.wait(p.sync, "V")
            lw = p.inc("L", p.sync.dma_start(out=wrow, in_=w_ap), 16)
            p.wait(p.tensor, "L", lw)
            v1 = emit_rowrep(p, wrow, wrep, onesr, pswr)
            p.wait(p.sync, "V", v1)
            lb = p.inc("L", p.sync.dma_start(out=wrow, in_=b_ap), 16)
            p.wait(p.tensor, "L", lb)
            return emit_rowrep(p, wrow, brep, onesr, pswr)

        emit_ln_tm(p, x_dram, scr["res0"])
        res = [scr["res0"], scr["res1"]]
        nlayers = getattr(emit_l1_attn_a, "nlayers", 4)
        for i in range(nlayers):
            rin, rout = res[i % 2], res[(i + 1) % 2]
            gv = load_reps(wn["norm_w"][i:i + 1, :], wn["norm_b"][i:i + 1, :])
            with contextlib.ExitStack() as sl:
                p.stack = sl
                xn_fm = p.sb("xnfm", [128, 8, T], BF16)
                p.wait(p.sync, "S")
                ln_done = emit_ln_fm(p, rin, xn_fm, ident, gw=None,
                                     wrep2=wrep, brep2=brep)
                evac = make_res_evac(p, rin, rout, ln_done)
                emit_mamba(p, xn_fm, wl[i], scr, evac, gate_v=ln_done)
                p.wait(p.sync, "S")
                p.stack = st
        gv = load_reps(wn["normf_w"], wn["normf_b"])
        p.wait(p.sync, "S")
        emit_ln_tm(p, res[nlayers % 2], out_dram, wrep=wrep, brep=brep,
                   add_dram=x_dram)
        p.stack = old


def emit_l1_attn_b(p, x_dram, out_dram, w, scr, ident):
    with contextlib.ExitStack() as st:
        old = p.stack
        p.stack = st
        xn_fm = p.sb("xnfm", [128, 8, T], BF16)
        ln_done = emit_ln_fm(p, x_dram, xn_fm, ident)
        evac = make_res_evac(p, x_dram, out_dram, ln_done)
        emit_mamba(p, xn_fm, w, scr, evac, gate_v=ln_done)
        p.wait(p.sync, "S")
        p.stack = old


def emit_slstm_sublayer(p, l, xa, xb, xnext, w, scr, ident):
    H = 4
    with contextlib.ExitStack() as st:
        old = p.stack
        p.stack = st
        onesr = p.sb("onesr", [1, 128], F32)
        wrow = p.sb("wrow", [1, D], F32)
        wrep = p.sb("wrep", [128, D], F32)
        pswr = p.ps("pswr", [128, 1024], F32)
        p.vop(p.vector.memset, onesr, 1.0)

        def load_rep(row_ap):
            p.wait(p.sync, "V")
            lw = p.inc("L", p.sync.dma_start(out=wrow, in_=row_ap), 16)
            p.wait(p.tensor, "L", lw)
            return emit_rowrep(p, wrow, wrep, onesr, pswr)

        y_fm = p.sb("yfm", [128, 8, T], BF16)

        with contextlib.ExitStack() as sa:
            p.stack = sa
            gv = load_rep(w["ln1w"][l:l + 1, :])
            xn_fm = p.sb("xnfm", [128, 8, T], BF16)
            ln_done = emit_ln_fm(p, xa, xn_fm, ident, wrep2=wrep)
            xc_fm = p.sb("xcfm", [128, 8, T], BF16)
            convw = p.sb("convw", [128, 8, 4], F32)
            convb = p.sb("convb", [128, 8], F32)
            p.wait(p.gpsimd, "V", ln_done)
            p.inc("W", p.gpsimd.dma_start(out=convw, in_=w["convw"][l]), 16)
            cwl = p.inc("W", p.gpsimd.dma_start(out=convb, in_=w["convb"][l]), 16)
            acc = p.sb("cacc", [128, T], F32)
            sil2d = p.sb("sil2d", [128, T], BF16)
            p.wait(p.vector, "W", cwl)
            p.wait(p.scalar, "W", cwl)
            for dc in range(8):
                p.vop(p.vector.tensor_scalar_mul, acc, xn_fm[:, dc, :],
                      convw[:, dc, 3:4])
                for s in (1, 2, 3):
                    p.vop(p.vector.scalar_tensor_tensor, acc[:, s:T],
                          xn_fm[:, dc, 0:T - s], convw[:, dc, 3 - s:4 - s],
                          acc[:, s:T], ALU.mult, ALU.add)
                vc = p.val("V")
                p.wait(p.scalar, "V", vc)
                av = p.aop(p.scalar.activation, out=sil2d, in_=acc, func=AF.Silu,
                           bias=convb[:, dc:dc + 1])
                p.wait(p.vector, "A", av)
                p.vop(p.vector.tensor_copy, xc_fm[:, dc, :], sil2d)
            wg_sb = p.sb("wg", [128, 8, 1024], BF16)
            brow = p.sb("brow", [1, 4096], F32)
            b_rep = p.sb("brep32", [32, 4096], BF16)
            p.wait(p.gpsimd, "V")
            p.inc("W", p.gpsimd.dma_start(
                out=wg_sb, in_=w["wg"][l].rearrange("h (kc kp) n -> kp (h kc) n", kp=128)), 16)
            bwl = p.inc("W", p.gpsimd.dma_start(out=brow, in_=w["cb"][l]), 16)
            ones32 = p.sb("ones32", [1, 32], F32)
            p.vop(p.vector.memset, ones32, 1.0)
            p.wait(p.tensor, "W", bwl)
            for n0 in range(0, 4096, 512):
                p.wait(p.tensor, "V")
                ins = p.tensor.matmul(pswr[0:32, n0 % 1024:n0 % 1024 + 512],
                                      ones32, brow[0:1, n0:n0 + 512],
                                      start=True, stop=True)
                pe = p.inc("P", ins, 1)
                p.wait(p.vector, "P", pe)
                p.vop(p.vector.tensor_copy, b_rep[:, n0:n0 + 512],
                      pswr[0:32, n0 % 1024:n0 % 1024 + 512])
            stg = [p.sb("stg", [32, 1024], BF16) for _ in range(2)]
            stg_free = [0, 0]
            pgt = [p.ps("pgt", [32, 1024], F32) for _ in range(2)]
            pgt_free = ["V", 0, "V", 0]
            gmm = p.val("V")
            p.wait(p.tensor, "V", gmm)
            si = 0
            for t0 in range(0, T, 32):
                tc, tof = t0 // 128, t0 % 128
                for h in range(H):
                    b = si % 2
                    p.wait(p.tensor, pgt_free[2 * b], pgt_free[2 * b + 1])
                    for g, src in ((0, xc_fm), (1, xn_fm), (2, xc_fm), (3, xn_fm)):
                        for kc in range(2):
                            ins = p.tensor.matmul(
                                pgt[b][:, g * 256:(g + 1) * 256],
                                src[:, 2 * h + kc, t0:t0 + 32],
                                wg_sb[:, 2 * h + kc, g * 256:(g + 1) * 256],
                                start=(kc == 0), stop=(kc == 1))
                    pe = p.inc("P", ins, 1)
                    p.wait(p.vector, "P", pe)
                    p.wait(p.vector, "S", stg_free[b])
                    cnt = p.vop(p.vector.tensor_add, stg[b], pgt[b],
                                b_rep[:, h * 1024:(h + 1) * 1024])
                    pgt_free[2 * b], pgt_free[2 * b + 1] = "V", cnt
                    p.wait(p.sync, "V", cnt)
                    s = p.inc("S", p.sync.dma_start(
                        out=scr["wx"][t0:t0 + 32, h * 1024:(h + 1) * 1024],
                        in_=stg[b]), 16)
                    stg_free[b] = s
                    si += 1
            p.stack = st
        ga_done_s = p.val("S")
        ga_done_v = p.val("V")

        with contextlib.ExitStack() as sb:
            p.stack = sb
            R_sb = p.sb("rk", [128, 8, 1024], BF16)
            p.wait(p.gpsimd, "V", ga_done_v)
            rl = p.inc("W", p.gpsimd.dma_start(
                out=R_sb, in_=w["rk"][l].rearrange("h (kc kp) n -> kp (h kc) n", kp=128)), 16)
            SL = 16
            slab = [p.sb("wxsl", [128, SL, 1024], BF16) for _ in range(2)]
            slab_ld = [0, 0]
            slab_free = [ga_done_v, ga_done_v]
            rawb = p.sb("rawb", [128, 1024], F32)
            sfo = p.sb("sfo", [128, 512], F32)
            ei = p.sb("ei", [128, 256], F32)
            tz = p.sb("tz", [128, 256], F32)
            t1 = p.sb("t1", [128, 256], F32)
            cst = p.sb("cst", [128, 256], F32)
            nst = p.sb("nst", [128, 256], F32)
            rn = p.sb("rn", [128, 256], F32)
            h_lm = p.sb("hlm", [128, 256], BF16)
            h0 = p.sb("h0", [128, 8], BF16)
            praw = p.ps("praw", [128, 1024], F32)
            ptr = p.ps("ptr", [128, 256], BF16)
            p.vop(p.vector.memset, cst, 0.0)
            p.vop(p.vector.memset, nst, 0.0)
            p.vop(p.vector.memset, h0, 0.0)
            wxr = scr["wx"].rearrange("(nt sl) (hh n) -> hh nt sl n", sl=SL, hh=4)

            def load_slab(si_):
                b = si_ % 2
                p.wait(p.sync, "V", slab_free[b])
                p.wait(p.sync, "S", ga_done_s)
                slab_ld[b] = p.inc("L", p.sync.dma_start(
                    out=slab[b][::32, :, :],
                    in_=wxr[:, si_, :, :]), 16)

            load_slab(0)
            load_slab(1)
            p.wait(p.tensor, "W", rl)
            prev_v = p.val("V")
            a_sfo = 0
            for t in range(T):
                sidx = t // SL
                s = t % SL
                b = sidx % 2
                p.wait(p.tensor, "V", prev_v)
                for h in range(H):
                    for nh in (0, 512):
                        for kc in range(2):
                            lh = h0[:, 2 * h + kc:2 * h + kc + 1] if t == 0 else \
                                y_fm[:, 2 * h + kc, t - 1:t]
                            ins = p.tensor.matmul(
                                praw[32 * h:32 * h + 1, nh:nh + 512], lh,
                                R_sb[:, 2 * h + kc, nh:nh + 512],
                                start=(kc == 0), stop=(kc == 1),
                                tile_position=(0, 32 * h))
                pe = p.inc("P", ins, 1)
                p.wait(p.vector, "P", pe)
                p.wait(p.vector, "L", slab_ld[b])
                nc_ = p.nc
                p.inc("V", nc_.vector.tensor_add(rawb, praw, slab[b][:, s, :]), 1)
                raw_v = p.val("V")
                if s == SL - 1:
                    slab_free[b] = raw_v
                    if (sidx + 2) * SL < T:
                        load_slab(sidx + 2)
                p.wait(p.scalar, "V", raw_v)
                p.inc("A", nc_.scalar.activation(out=ei, in_=rawb[:, 0:256], func=AF.Exp), 1)
                p.inc("A", nc_.scalar.activation(out=tz, in_=rawb[:, 256:512], func=AF.Tanh), 1)
                a_sfo = p.inc("A", nc_.scalar.activation(out=sfo, in_=rawb[:, 512:1024], func=AF.Sigmoid), 1)
                p.wait(p.vector, "A", a_sfo)
                nc_.vector.tensor_mul(t1, ei, tz)
                nc_.vector.tensor_mul(cst, cst, sfo[:, 0:256])
                nc_.vector.tensor_add(cst, cst, t1)
                nc_.vector.tensor_mul(nst, nst, sfo[:, 0:256])
                p.inc("V", nc_.vector.tensor_add(nst, nst, ei), 1)
                n_v = p.val("V")
                p.wait(p.scalar, "V", n_v)
                a_ln = p.inc("A", nc_.scalar.activation(out=rn, in_=nst, func=AF.Ln), 1)
                p.wait(p.scalar, "A", a_ln)
                a_rn = p.inc("A", nc_.scalar.activation(out=rn, in_=rn, func=AF.Exp, scale=-1.0), 1)
                p.wait(p.vector, "A", a_rn)
                nc_.vector.tensor_mul(t1, cst, rn)
                p.inc("V", nc_.vector.tensor_mul(h_lm, t1, sfo[:, 256:512]), 1)
                h_v = p.val("V")
                p.wait(p.tensor, "V", h_v)
                nc_.tensor.transpose(ptr[:, 0:128], h_lm[:, 0:128], ident)
                ins = nc_.tensor.transpose(ptr[:, 128:256], h_lm[:, 128:256], ident)
                pt_ = p.inc("P", ins, 1)
                p.wait(p.vector, "P", pt_)
                nc_.vector.tensor_copy(
                    y_fm[:, 0:8:2, t].unsqueeze(2),
                    ptr[:, 0:128].rearrange("p (l q) -> p l q", q=32)[:, :, 0:1])
                p.inc("V", nc_.vector.tensor_copy(
                    y_fm[:, 1:8:2, t].unsqueeze(2),
                    ptr[:, 128:256].rearrange("p (l q) -> p l q", q=32)[:, :, 0:1]), 1)
                prev_v = p.val("V")
            p.stack = st
        rec_done = p.val("V")

        with contextlib.ExitStack() as sc:
            p.stack = sc
            gnv = load_rep(w["gnw"][l:l + 1, :])
            ytile = p.sb("ytile", [128, 1024], BF16)
            sq = p.sb("gnsq", [128, 256], BF16)
            s1 = p.sb("gns1", [128, 1], F32)
            s2 = p.sb("gns2", [128, 1], F32)
            mean = p.sb("gnmean", [128, 1], F32)
            nm2 = p.sb("gnnm2", [128, 1], F32)
            var = p.sb("gnvar", [128, 1], F32)
            rstd = p.sb("gnrstd", [128, 1], F32)
            xat = [p.sb("xat", [128, D], F32) for _ in range(2)]
            ot = [p.sb("got", [128, D], F32) for _ in range(2)]
            ptm = p.ps("ptm", [128, 1024], BF16)
            xa_ld = [0, 0]
            st_done = [0, 0]
            cons = [rec_done, rec_done]
            ptm_free = rec_done
            for tt in range(NTT):
                bb = tt % 2
                p.wait(p.sync, "V", cons[bb])
                xa_ld[bb] = p.inc("L", p.sync.dma_start(
                    out=xat[bb], in_=xa[tt * 128:(tt + 1) * 128, :]), 16)
                p.wait(p.tensor, "V", ptm_free)
                p.wait(p.tensor, "V", rec_done)
                for k in range(8):
                    ins = p.tensor.transpose(ptm[:, k * 128:(k + 1) * 128],
                                             y_fm[:, k, tt * 128:(tt + 1) * 128],
                                             ident)
                pe = p.inc("P", ins, 1)
                p.wait(p.vector, "P", pe)
                ptm_free = p.vop(p.vector.tensor_copy, ytile, ptm)
                for h in range(H):
                    sl_ = ytile[:, h * 256:(h + 1) * 256]
                    p.vop(p.vector.reduce_sum, s1, sl_, axis=mybir.AxisListType.X)
                    p.wait(p.scalar, "V")
                    a1 = p.aop(p.scalar.activation, out=sq, in_=sl_, func=AF.Square,
                               accum_out=s2)
                    p.vop(p.vector.tensor_scalar_mul, mean, s1, 1.0 / 256)
                    p.wait(p.vector, "A", a1)
                    p.vop(p.vector.tensor_scalar, nm2, s1, mean, -1.0 / 256,
                          ALU.mult, ALU.mult)
                    p.vop(p.vector.tensor_scalar, var, s2, 1.0 / 256, nm2,
                          ALU.mult, ALU.add)
                    v1 = p.vop(p.vector.tensor_scalar_add, var, var, EPS)
                    p.wait(p.scalar, "V", v1)
                    p.aop(p.scalar.activation, out=rstd, in_=var, func=AF.Ln)
                    a2 = p.aop(p.scalar.activation, out=rstd, in_=rstd, func=AF.Exp,
                               scale=-0.5)
                    p.wait(p.vector, "A", a2)
                    p.vop(p.vector.tensor_scalar_sub, sl_, sl_, mean)
                    p.vop(p.vector.tensor_scalar_mul, sl_, sl_, rstd)
                p.wait(p.vector, "L", xa_ld[bb])
                p.wait(p.vector, "S", st_done[bb])
                p.vop(p.vector.tensor_mul, ot[bb], ytile, wrep)
                cnt = p.vop(p.vector.tensor_add, ot[bb], ot[bb], xat[bb])
                cons[bb] = cnt
                p.wait(p.sync, "V", cnt)
                st_done[bb] = p.inc("S", p.sync.dma_start(
                    out=xb[tt * 128:(tt + 1) * 128, :], in_=ot[bb]), 16)
            p.wait(p.sync, "S")
            p.stack = st
        gn_done = p.val("V")

        with contextlib.ExitStack() as sd:
            p.stack = sd
            lv = load_rep(w["ln2w"][l:l + 1, :])
            xn2 = p.sb("xn2", [128, 8, T], BF16)
            p.wait(p.sync, "S")
            ln2_done = emit_ln_fm(p, xb, xn2, ident, wrep2=wrep)
            ffup = p.sb("ffup", [128, 8, 2688], BF16)
            ffdn = p.sb("ffdn", [128, 11, 1024], BF16)
            p.wait(p.gpsimd, "V", ln2_done)
            p.inc("W", p.gpsimd.dma_start(
                out=ffup, in_=w["ffup"][l].rearrange("(kc kp) n -> kp kc n", kp=128)), 16)
            p.inc("W", p.gpsimd.dma_start(
                out=ffdn[:, 0:10, :],
                in_=w["ffdn"][l][0:1280, :].rearrange("(kc kp) n -> kp kc n", kp=128)), 16)
            fwl = p.inc("W", p.gpsimd.dma_start(
                out=ffdn[0:64, 10, :], in_=w["ffdn"][l][1280:1344, :]), 16)
            u2 = p.sb("u2", [128, 2688], F32)
            g2 = p.sb("g2", [128, 1344], F32)
            h2 = p.sb("h2", [128, 1344], BF16)
            h2f = [p.sb("h2f", [128, 11, 128], BF16) for _ in range(2)]
            pup = [p.ps("pup", [128, 512], F32) for _ in range(2)]
            ptf = p.ps("ptf", [128, 1408], BF16)
            evac = make_res_evac(p, xb, xnext, ln2_done)
            p.wait(p.tensor, "W", fwl)
            pup_free = ["V", ln2_done, "V", ln2_done]
            ptf_free = ln2_done
            h2f_used = [0, 0]
            pst = [p.ps("pdn", [128, 512], F32) for _ in range(2)]
            dn_free = ["V", ln2_done, "V", ln2_done]
            si = 0
            for tt in range(NTT):
                bb = tt % 2
                for j, n0 in enumerate(range(0, 2688, 512)):
                    nsz = min(512, 2688 - n0)
                    b = si % 2
                    si += 1
                    p.wait(p.tensor, pup_free[2 * b], pup_free[2 * b + 1])
                    for kc in range(8):
                        ins = p.tensor.matmul(
                            pup[b][:, 0:nsz],
                            xn2[:, kc, tt * 128:(tt + 1) * 128],
                            ffup[:, kc, n0:n0 + nsz],
                            start=(kc == 0), stop=(kc == 7))
                    pe = p.inc("P", ins, 1)
                    p.wait(p.vector, "P", pe)
                    cnt = p.vop(p.vector.tensor_copy, u2[:, n0:n0 + nsz],
                                pup[b][:, 0:nsz])
                    pup_free[2 * b], pup_free[2 * b + 1] = "V", cnt
                uv = p.val("V")
                p.wait(p.scalar, "V", uv)
                ag = p.aop(p.scalar.activation, out=g2, in_=u2[:, 0:1344],
                           func=AF.Gelu)
                p.wait(p.vector, "A", ag)
                hv = p.vop(p.vector.tensor_mul, h2, g2, u2[:, 1344:2688])
                p.wait(p.tensor, "V", hv)
                p.wait(p.tensor, "V", ptf_free)
                for c in range(11):
                    if c < 10:
                        ins = p.tensor.transpose(ptf[:, c * 128:(c + 1) * 128],
                                                 h2[:, c * 128:(c + 1) * 128], ident)
                    else:
                        ins = p.tensor.transpose(ptf[0:64, 1280:1408],
                                                 h2[:, 1280:1344], ident)
                pe = p.inc("P", ins, 1)
                p.wait(p.vector, "P", pe)
                p.wait(p.vector, "P", h2f_used[bb])
                ptf_free = p.vop(p.vector.tensor_copy,
                                 h2f[bb].rearrange("p c n -> p (c n)")[:, 0:1408],
                                 ptf.rearrange("p n -> p n"))
                p.wait(p.tensor, "V", ptf_free)
                for n0 in (0, 512):
                    b = si % 2
                    si += 1
                    p.wait(p.tensor, dn_free[2 * b], dn_free[2 * b + 1])
                    for kc in range(11):
                        kk = 128 if kc < 10 else 64
                        ins = p.tensor.matmul(
                            pst[b], h2f[bb][0:kk, kc, :],
                            ffdn[0:kk, kc, n0:n0 + 512],
                            start=(kc == 0), stop=(kc == 10))
                    pe = p.inc("P", ins, 1)
                    if n0 == 512:
                        h2f_used[bb] = pe
                    sem, cnt = evac(tt, n0, pst[b], pe)
                    dn_free[2 * b], dn_free[2 * b + 1] = sem, cnt
            p.wait(p.sync, "S")
            p.stack = st
        p.stack = old


def emit_xlstm(p, x_dram, out_dram, w, scr, ident, subs=(0, 1, 2, 3),
               do_pre=True, do_post=True):
    bufs = [scr["xa"], scr["xb"], scr["xc"]]
    if do_pre:
        emit_ln_tm(p, x_dram, bufs[0])
    cur = 0
    for l in subs:
        emit_slstm_sublayer(p, l, bufs[cur % 3], bufs[(cur + 1) % 3],
                            bufs[(cur + 2) % 3], w, scr, ident)
        cur += 2
    if do_post:
        with contextlib.ExitStack() as st:
            old = p.stack
            p.stack = st
            onesr = p.sb("onesr", [1, 128], F32)
            wrow = p.sb("wrow", [1, D], F32)
            wrep = p.sb("wrep", [128, D], F32)
            pswr = p.ps("pswr", [128, 1024], F32)
            p.vop(p.vector.memset, onesr, 1.0)
            p.wait(p.sync, "V")
            lw = p.inc("L", p.sync.dma_start(out=wrow, in_=w["postw"]), 16)
            p.wait(p.tensor, "L", lw)
            emit_rowrep(p, wrow, wrep, onesr, pswr)
            p.wait(p.sync, "S")
            emit_ln_tm(p, bufs[cur % 3], out_dram, wrep=wrep, brep=None,
                       add_dram=x_dram)
            p.stack = old


def _stage(inputs):
    import ml_dtypes
    g = lambda k: np.asarray(inputs[k])
    S = {}
    S["identb"] = np.eye(128, dtype=ml_dtypes.bfloat16)
    S["identf"] = np.eye(128, dtype=np.float32)
    DHE = 32
    inv = 1.0 / (10000.0 ** (np.arange(0, DHE, 2) / DHE))
    freqs = np.arange(T)[:, None] * inv[None, :]
    emb = np.concatenate([freqs, freqs], axis=1)
    cos, sin = np.cos(emb), np.sin(emb)
    C = np.ones((128, T), np.float32)
    Sx = np.zeros((128, T), np.float32)
    C[32:64] = cos.T; C[96:128] = cos.T
    Sx[32:64] = sin.T; Sx[96:128] = sin.T
    S["cext"] = C; S["sext"] = Sx
    S["cmask"] = np.triu(np.full((128, 128), -1e30, np.float32), 1)
    selp = np.zeros((16, T), np.float32)
    seln = np.zeros((16, T), np.float32)
    for n in range(16):
        selp[n, n * 128:(n + 1) * 128] = 1.0
        seln[n, n * 128:(n + 1) * 128] = -1.0
    S["selp"] = selp; S["seln"] = seln
    S["qdT"] = np.ascontiguousarray(g("L0_attn_a_Wq_down").T)
    S["kdT"] = np.ascontiguousarray(g("L0_attn_a_Wk_down").T)
    S["vdT"] = np.ascontiguousarray(g("L0_attn_a_Wv_down").T)
    for nm in ("q_up_c", "q_up_e", "k_up_c", "k_up_e", "v_up"):
        S[nm] = np.ascontiguousarray(g("L0_attn_a_W" + nm.replace("_up", "_up")))
    S["wc"] = g("L0_attn_a_Wc")
    S["moe_gwT"] = np.ascontiguousarray(g("L0_ffn_gate_w").T)
    S["moe_ewT"] = np.ascontiguousarray(np.transpose(g("L0_ffn_expert_w"), (0, 2, 1)))
    gperm = [0, 2, 1, 3]
    wi, wf, wz, wo = (g("L0_attn_b_wi"), g("L0_attn_b_wf"),
                      g("L0_attn_b_wz"), g("L0_attn_b_wo"))
    wgs = np.stack([wi, wz, wf, wo], axis=2)
    S["xl_wg"] = np.ascontiguousarray(wgs.transpose(0, 1, 4, 2, 3).reshape(4, 4, 256, 1024))
    S["xl_rk"] = np.ascontiguousarray(
        g("L0_attn_b_rker")[:, :, :, gperm, :].reshape(4, 4, 256, 1024))
    S["xl_cb"] = np.ascontiguousarray(
        g("L0_attn_b_cell_b")[:, :, gperm, :].reshape(4, 1, 4096))
    S["xl_convw"] = np.ascontiguousarray(
        g("L0_attn_b_conv_w").reshape(4, 8, 128, 4).transpose(0, 2, 1, 3))
    S["xl_convb"] = np.ascontiguousarray(
        g("L0_attn_b_conv_b").reshape(4, 8, 128).transpose(0, 2, 1))
    S["xl_ln1w"] = g("L0_attn_b_ln1_w")
    S["xl_ln2w"] = g("L0_attn_b_ln2_w")
    S["xl_gnw"] = g("L0_attn_b_gn_w")
    S["xl_postw"] = g("L0_attn_b_post_w")[None, :]
    S["xl_ffup"] = g("L0_attn_b_ff_up")
    S["xl_ffdn"] = g("L0_attn_b_ff_down")
    def fm2(v):
        return np.ascontiguousarray(v.reshape(16, 128).T)
    for i in range(4):
        S["ma%d_inwT" % i] = np.ascontiguousarray(g("L1_attn_a_in_proj_w")[i].T)
        S["ma%d_convw" % i] = np.ascontiguousarray(
            g("L1_attn_a_conv_w")[i].reshape(16, 128, 4).transpose(1, 0, 2))
        S["ma%d_convb" % i] = fm2(g("L1_attn_a_conv_b")[i])
        S["ma%d_xpwT" % i] = np.ascontiguousarray(g("L1_attn_a_x_proj_w")[i].T)
        S["ma%d_dtpwT" % i] = np.ascontiguousarray(g("L1_attn_a_dt_proj_w")[i].T)
        S["ma%d_dtpb" % i] = fm2(g("L1_attn_a_dt_proj_b")[i])
        S["ma%d_alog" % i] = np.ascontiguousarray(
            g("L1_attn_a_A_log")[i].reshape(16, 128, 16).transpose(1, 0, 2))
        S["ma%d_dpar" % i] = fm2(g("L1_attn_a_D")[i])
        S["ma%d_opwT" % i] = np.ascontiguousarray(g("L1_attn_a_out_proj_w")[i].T)
    S["ma_norm_w"] = g("L1_attn_a_norm_w")
    S["ma_norm_b"] = g("L1_attn_a_norm_b")
    S["ma_normf_w"] = g("L1_attn_a_normf_w")[None, :]
    S["ma_normf_b"] = g("L1_attn_a_normf_b")[None, :]
    S["mb_inwT"] = np.ascontiguousarray(g("L1_attn_b_in_proj_w").T)
    S["mb_convw"] = np.ascontiguousarray(
        g("L1_attn_b_conv_w")[:, 0, :].reshape(16, 128, 4).transpose(1, 0, 2))
    S["mb_convb"] = fm2(g("L1_attn_b_conv_b"))
    S["mb_xpwT"] = np.ascontiguousarray(g("L1_attn_b_x_proj_w").T)
    S["mb_dtpwT"] = np.ascontiguousarray(g("L1_attn_b_dt_proj_w").T)
    S["mb_dtpb"] = fm2(g("L1_attn_b_dt_proj_b"))
    S["mb_alog"] = np.ascontiguousarray(
        g("L1_attn_b_A_log").reshape(16, 128, 16).transpose(1, 0, 2))
    S["mb_dpar"] = fm2(g("L1_attn_b_D_param"))
    S["mb_opwT"] = np.ascontiguousarray(g("L1_attn_b_out_proj_w").T)
    S["f_wg"] = np.ascontiguousarray(g("L1_ffn_w_gate").T)
    S["f_wu"] = np.ascontiguousarray(g("L1_ffn_w_up").T)
    S["f_wd"] = np.ascontiguousarray(g("L1_ffn_w_down").T)
    return S


_NPDT = {np.dtype(np.float32): F32, np.dtype(np.int32): I32}


class Launch:

    def __init__(self, S):
        self.p = P()
        self.S = S
        self.ext_in = {}
        self.ext_out = {}
        self.cache = {}

    def W(self, name):
        if name not in self.cache:
            a = self.S[name]
            import ml_dtypes
            dt = _NPDT.get(a.dtype, BF16 if a.dtype == np.dtype(ml_dtypes.bfloat16) else F32)
            self.cache[name] = self.p.nc.dram_tensor(
                "w_" + name, list(a.shape), dt, kind="ExternalInput").ap()
            self.ext_in[name] = a
        return self.cache[name]

    def buf_in(self, name):
        if name not in self.cache:
            self.cache[name] = self.p.nc.dram_tensor(
                name, [T, D], F32, kind="ExternalInput").ap()
            self.ext_in[name] = None
        return self.cache[name]

    def buf_out(self, name):
        if name not in self.cache:
            self.cache[name] = self.p.nc.dram_tensor(
                name, [T, D], F32, kind="ExternalOutput").ap()
            self.ext_out[name] = True
        return self.cache[name]

    def buf_int(self, name):
        if name not in self.cache:
            self.cache[name] = self.p.dram(name, [T, D], F32)
        return self.cache[name]

    def idents(self):
        p = self.p
        identb = p.sb("identb", [128, 128], BF16)
        identf = p.sb("identf", [128, 128], F32)
        p.inc("L", p.sync.dma_start(out=identb, in_=self.W("identb")), 16)
        li = p.inc("L", p.sync.dma_start(out=identf, in_=self.W("identf")), 16)
        for e in (p.tensor, p.vector, p.scalar, p.gpsimd):
            p.wait(e, "L", li)
        return identb, identf


def _blk_attn(L, xin, xout):
    w = {nm: L.W(nm) for nm in ("qdT", "kdT", "vdT", "q_up_c", "q_up_e",
                                "k_up_c", "k_up_e", "v_up", "wc", "cext",
                                "sext", "cmask")}
    emit_attn(L.p, xin, xout, w, L.identb, L.identf)


def _blk_moe(L, xin, xout):
    emit_moe(L.p, xin, xout, L.W("moe_gwT"), L.W("moe_ewT"), L.identb, L.identf)


def _blk_xlstm(L, xin, xout):
    p = L.p
    w = {"ln1w": L.W("xl_ln1w"), "ln2w": L.W("xl_ln2w"), "gnw": L.W("xl_gnw"),
         "postw": L.W("xl_postw"), "convw": L.W("xl_convw"),
         "convb": L.W("xl_convb"), "wg": L.W("xl_wg"), "rk": L.W("xl_rk"),
         "cb": L.W("xl_cb"), "ffup": L.W("xl_ffup"), "ffdn": L.W("xl_ffdn")}
    scr = {"wx": p.dram("xl_wx", [T, 4096], BF16),
           "xa": L.buf_int("xl_xa"), "xb": L.buf_int("xl_xb"),
           "xc": L.buf_int("xl_xc")}
    emit_xlstm(p, xin, xout, w, scr, L.identb)


def _mamba_scr(p):
    return {
        "z": p.dram("scr_z", [128, 16, NTT, 128], BF16),
        "nd": p.dram("scr_nd", [128, 16, NTT, 128], BF16),
        "brep": p.dram("scr_br", [128, 16, NTT, 128], BF16),
        "crep": p.dram("scr_cr", [128, 16, NTT, 128], BF16),
        "y": p.dram("scr_y", [128, 16, T], BF16),
    }


def _blk_mamba_a(L, xin, xout):
    p = L.p
    wl = []
    for i in range(4):
        d = {k: L.W("ma%d_%s" % (i, k)) for k in
             ("inwT", "convw", "convb", "xpwT", "dtpwT", "dtpb", "alog",
              "dpar", "opwT")}
        d["selp"] = L.W("selp")
        d["seln"] = L.W("seln")
        wl.append(d)
    wn = {"norm_w": L.W("ma_norm_w"), "norm_b": L.W("ma_norm_b"),
          "normf_w": L.W("ma_normf_w"), "normf_b": L.W("ma_normf_b")}
    scr = _mamba_scr(p)
    scr["res0"] = L.buf_int("ma_res0")
    scr["res1"] = L.buf_int("ma_res1")
    emit_l1_attn_a(p, xin, xout, wl, wn, scr, L.identb)


def _blk_mamba_b(L, xin, xout):
    p = L.p
    w = {k: L.W("mb_" + k) for k in ("inwT", "convw", "convb", "xpwT",
                                     "dtpwT", "dtpb", "alog", "dpar", "opwT")}
    w["selp"] = L.W("selp")
    w["seln"] = L.W("seln")
    emit_l1_attn_b(p, xin, xout, w, _mamba_scr(p), L.identb)


def _blk_l1ffn(L, xin, xout):
    emit_l1ffn(L.p, xin, xout, L.W("f_wg"), L.W("f_wu"), L.W("f_wd"), L.identb)


_CHAIN = [
    ("x0", _blk_attn, "x1"),
    ("x1", _blk_moe, "x2"),
    ("x2", _blk_xlstm, "x3"),
    ("x3", _blk_moe, "x4"),
    ("x4", _blk_mamba_a, "x5"),
    ("x5", _blk_l1ffn, "x6"),
    ("x6", _blk_mamba_b, "x7"),
    ("x7", _blk_l1ffn, "x8"),
]

GROUPS = [[0, 1, 2, 3, 4, 5, 6, 7]]


def kernel(**inputs):
    S = _stage(inputs)
    x_full = np.asarray(inputs["L0_attn_a_x"], np.float32)
    ncore = 8
    bufs = {"x0": [np.ascontiguousarray(x_full[c]) for c in range(ncore)]}
    for group in GROUPS:
        L = Launch(S)
        names_in = _CHAIN[group[0]][0]
        names_out = _CHAIN[group[-1]][2]
        xin_ap = L.buf_in(names_in)
        L.identb, L.identf = L.idents()
        cur_ap = xin_ap
        first = True
        for bi in group:
            _, fn, outname = _CHAIN[bi]
            if not first:
                L.p.fence_all()
            L.p.sem_prefix = "b%d_" % bi
            first = False
            is_last = (bi == group[-1])
            out_ap = L.buf_out(outname) if is_last else L.buf_int(outname)
            fn(L, cur_ap, out_ap)
            cur_ap = out_ap
        L.p.sem_prefix = ""
        in_maps = []
        for c in range(ncore):
            m = {}
            for nm, arr in L.ext_in.items():
                if arr is None:
                    m[nm] = bufs[nm][c]
                else:
                    m["w_" + nm] = arr
            in_maps.append(m)
        res = run_bass_kernel_spmd(L.p.nc, in_maps, core_ids=list(range(ncore)))
        for nm in L.ext_out:
            bufs[nm] = [res.results[c][nm] for c in range(ncore)]
    return np.stack(bufs["x8"], axis=0)


# version 07 (final)
# speedup vs baseline: 1.1920x; 1.1908x over previous
import contextlib
import math
import numpy as np

import concourse.bass as bass
import concourse.mybir as mybir
from concourse.bass_utils import run_bass_kernel_spmd

F32 = mybir.dt.float32
BF16 = mybir.dt.bfloat16
I32 = mybir.dt.int32
AF = mybir.ActivationFunctionType
ALU = mybir.AluOpType

T = 2048
D = 1024
NTT = T // 128
EPS = 1e-5


class P:

    def __init__(self):
        self.nc = bass.Bass("TRN2", target_bir_lowering=False)
        self._sems = {}
        self._uid = 0
        self.stack = contextlib.ExitStack()
        self.sem_prefix = ""

    def uid(self, pfx):
        self._uid += 1
        return "%s_%d" % (pfx, self._uid)

    def _sem(self, name):
        name = self.sem_prefix + name
        if name not in self._sems:
            self._sems[name] = [self.nc.alloc_semaphore(name), 0]
        return self._sems[name]

    def fence_all(self):
        for eng in (self.tensor, self.vector, self.scalar, self.gpsimd,
                    self.sync):
            for key, (h, cnt) in list(self._sems.items()):
                if key.startswith(self.sem_prefix) and cnt > 0:
                    eng.wait_ge(h, cnt)

    def inc(self, name, instr, k):
        s = self._sem(name)
        instr.then_inc(s[0], k)
        s[1] += k
        return s[1]

    def val(self, name):
        return self._sem(name)[1]

    def wait(self, eng, name, v=None):
        s = self._sem(name)
        eng.wait_ge(s[0], s[1] if v is None else v)

    def sb(self, pfx, shape, dtype):
        return self.stack.enter_context(
            self.nc.sbuf_tensor(self.uid(pfx), shape, dtype)
        ).ap()

    def ps(self, pfx, shape, dtype=F32):
        return self.stack.enter_context(
            self.nc.psum_tensor(self.uid(pfx), shape, dtype)
        ).ap()

    def dram(self, pfx, shape, dtype=F32, kind="Internal"):
        return self.nc.dram_tensor(self.uid(pfx), shape, dtype, kind=kind).ap()

    def vop(self, fn, *args, **kw):
        self.wait(self.vector, "V")
        return self.inc("V", fn(*args, **kw), 1)

    def aop(self, fn, *args, **kw):
        self.wait(self.scalar, "A")
        return self.inc("A", fn(*args, **kw), 1)

    def gop(self, fn, *args, **kw):
        self.wait(self.gpsimd, "G")
        return self.inc("G", fn(*args, **kw), 1)

    @property
    def tensor(self):
        return self.nc.tensor

    @property
    def vector(self):
        return self.nc.vector

    @property
    def scalar(self):
        return self.nc.scalar

    @property
    def gpsimd(self):
        return self.nc.gpsimd

    @property
    def sync(self):
        return self.nc.sync


def load_weight_bf16(p, dram_ap, sb_ap, sem="W"):
    instr = p.gpsimd.dma_start(out=sb_ap, in_=dram_ap)
    return p.inc(sem, instr, 16)


def emit_ln_fm(p, x_dram, out_fm, ident, nch=8, out_tm=None, gw=None, dbg=None,
               hook=None, wrep2=None, brep2=None):
    with contextlib.ExitStack() as _st:
        _old = p.stack
        p.stack = _st
        r = _emit_ln_fm_inner(p, x_dram, out_fm, ident, nch, out_tm, gw, dbg, hook,
                              wrep2, brep2)
        p.stack = _old
    return r


def _emit_ln_fm_inner(p, x_dram, out_fm, ident, nch, out_tm, gw, dbg=None, hook=None,
                      wrep2=None, brep2=None):
    Dp = nch * 128
    xt = [p.sb("lnx", [128, Dp], F32) for _ in range(2)]
    sq = p.sb("lnsq", [128, Dp], BF16)
    s1 = p.sb("lns1", [128, 1], F32)
    s2 = p.sb("lns2", [128, 1], F32)
    mean = p.sb("lnmean", [128, 1], F32)
    nm2 = p.sb("lnnm2", [128, 1], F32)
    var = p.sb("lnvar", [128, 1], F32)
    rstd = p.sb("lnrstd", [128, 1], F32)
    xn = p.sb("lnxn", [128, Dp], BF16)
    pt = [p.ps("lnps", [128, Dp], BF16) for _ in range(2)]
    last_consume = [[("V", 0)], [("V", 0)]]
    last_evac = [0, 0]
    s2_cons = 0
    nt = T // 128
    for tt in range(nt):
        b = tt % 2
        for (snm, scnt) in last_consume[b]:
            p.wait(p.sync, snm, scnt)
        ld = p.inc("L", p.sync.dma_start(out=xt[b], in_=x_dram[tt * 128:(tt + 1) * 128, :]), 16)
        p.wait(p.vector, "L", ld)
        p.vop(p.vector.reduce_sum, s1, xt[b], axis=mybir.AxisListType.X)
        p.wait(p.scalar, "L", ld)
        p.wait(p.scalar, "V", s2_cons)
        a1 = p.aop(p.scalar.activation, out=sq, in_=xt[b], func=AF.Square, accum_out=s2)
        p.vop(p.vector.tensor_scalar_mul, mean, s1, 1.0 / Dp)
        p.wait(p.vector, "A", a1)
        p.vop(p.vector.tensor_scalar, nm2, s1, mean, -1.0 / Dp, ALU.mult, ALU.mult)
        s2_cons = p.vop(p.vector.tensor_scalar, var, s2, 1.0 / Dp, nm2, ALU.mult, ALU.add)
        v1 = p.vop(p.vector.tensor_scalar_add, var, var, EPS)
        p.wait(p.scalar, "V", v1)
        p.aop(p.scalar.activation, out=rstd, in_=var, func=AF.Ln)
        a2 = p.aop(p.scalar.activation, out=rstd, in_=rstd, func=AF.Exp, scale=-0.5)
        if dbg is not None:
            p.wait(p.vector, "A", a2)
            for k, src in enumerate((s1, s2, mean, nm2, var, rstd)):
                p.vop(p.vector.tensor_copy, dbg[:, tt, k:k + 1], src)
        p.wait(p.vector, "A", a2)
        vap = p.vop(p.vector.tensor_scalar, xn, xt[b], mean, rstd, ALU.subtract, ALU.mult)
        if wrep2 is not None:
            vap = p.vop(p.vector.tensor_mul, xn, xn, wrep2)
        if brep2 is not None:
            vap = p.vop(p.vector.tensor_add, xn, xn, brep2)
        last_consume[b] = [("V", vap)]
        if hook is not None:
            last_consume[b] = last_consume[b] + list(hook(tt, xt[b], xn, mean, p.val("L")))
        if gw is not None:
            vap = p.vop(p.vector.tensor_tensor,
                        xn, xn, gw.broadcast_to((128, Dp)), ALU.mult)
        if out_tm is not None:
            vap = p.vop(p.vector.tensor_copy, out_tm[:, tt, :], xn)
        p.wait(p.tensor, "V", vap)
        p.wait(p.tensor, "V", last_evac[b])
        for k in range(nch):
            ins = p.tensor.transpose(pt[b][:, k * 128:(k + 1) * 128],
                                     xn[:, k * 128:(k + 1) * 128], ident)
        pe = p.inc("P", ins, 1)
        p.wait(p.vector, "P", pe)
        ev = p.vop(p.vector.tensor_copy,
                   out_fm[:, :, tt * 128:(tt + 1) * 128],
                   pt[b].rearrange("p (c f) -> p c f", f=128))
        last_evac[b] = ev
    return p.val("V")


def emit_mm_fm(p, w_sb, x_fm, kchunks, mchunks, evac, n_tile=512):
    with contextlib.ExitStack() as _st:
        _o = p.stack
        p.stack = _st
        nps = 2
        pst = [p.ps("mmps", [128, n_tile], F32) for _ in range(nps)]
        entry_v = p.val("V")
        last_read = [entry_v] * nps
        which = ["V"] * nps
        i = 0
        for mc in range(mchunks):
            for n0 in range(0, T, n_tile):
                b = i % nps
                i += 1
                p.wait(p.tensor, which[b], last_read[b])
                for kc in range(kchunks):
                    ins = p.tensor.matmul(
                        pst[b][:, 0:n_tile],
                        w_sb[:, kc, mc * 128:(mc + 1) * 128],
                        x_fm[:, kc, n0:n0 + n_tile],
                        start=(kc == 0), stop=(kc == kchunks - 1))
                pe = p.inc("P", ins, 1)
                sem_name, cnt = evac(mc, n0, pst[b], pe)
                which[b] = sem_name
                last_read[b] = cnt
        p.stack = _o


def emit_mm_tm(p, x_fm, w_sb, kchunks, dout, evac, psum_banks=2):
    with contextlib.ExitStack() as _st:
        _o = p.stack
        p.stack = _st
        n_tile = 512
        nps = psum_banks
        pst = [p.ps("mtps", [128, n_tile], F32) for _ in range(nps)]
        entry_v = p.val("V")
        last_read = [entry_v] * nps
        which = ["V"] * nps
        i = 0
        for tt in range(NTT):
            for n0 in range(0, dout, n_tile):
                b = i % nps
                i += 1
                p.wait(p.tensor, which[b], last_read[b])
                for kc in range(kchunks):
                    ins = p.tensor.matmul(
                        pst[b][:, 0:n_tile],
                        x_fm[:, kc, tt * 128:(tt + 1) * 128],
                        w_sb[:, kc, n0:n0 + n_tile],
                        start=(kc == 0), stop=(kc == kchunks - 1))
                pe = p.inc("P", ins, 1)
                sem_name, cnt = evac(tt, n0, pst[b], pe)
                which[b] = sem_name
                last_read[b] = cnt
        p.stack = _o


def emit_l1ffn(p, x_dram, out_dram, w_gate, w_up, w_down, ident):
    HID = 3072
    HC = HID // 128
    S127 = 1.0 / 127.0
    with contextlib.ExitStack() as st:
        old = p.stack
        p.stack = st

        h_fm = p.sb("hfm", [128, HC, T], BF16)

        with contextlib.ExitStack() as st1:
            p.stack = st1
            xn_fm = p.sb("xnfm", [128, 8, T], BF16)
            ln_done = emit_ln_fm(p, x_dram, xn_fm, ident)
            gtmp = p.sb("gtmp", [128, 512], F32)

            wg_r = w_gate.rearrange("(kc kp) n -> kp kc n", kp=128)
            wu_r = w_up.rearrange("(kc kp) n -> kp kc n", kp=128)
            wgi = [p.sb("wgi", [128, 8, 128], I32) for _ in range(2)]
            wui = [p.sb("wui", [128, 8, 128], I32) for _ in range(2)]
            wgb = [p.sb("wgb", [128, 8, 128], BF16) for _ in range(2)]
            wub = [p.sb("wub", [128, 8, 128], BF16) for _ in range(2)]
            wld = [0, 0]
            wcast = [ln_done, ln_done]
            wused = [0, 0]
            gtmp_cons = ln_done

            n_tile = 512
            pg = [p.ps("pg", [128, n_tile], F32) for _ in range(2)]
            pu = [p.ps("pu", [128, n_tile], F32) for _ in range(2)]
            last_read = [0, 0]

            def load_w(mc):
                wb = mc % 2
                p.wait(p.sync, "V", wcast[wb])
                p.inc("L", p.sync.dma_start(
                    out=wgi[wb], in_=wg_r[:, :, mc * 128:(mc + 1) * 128]), 16)
                wld[wb] = p.inc("L", p.sync.dma_start(
                    out=wui[wb], in_=wu_r[:, :, mc * 128:(mc + 1) * 128]), 16)

            def cast_w(mc):
                wb = mc % 2
                p.wait(p.vector, "L", wld[wb])
                p.wait(p.vector, "P", wused[wb])
                p.vop(p.vector.tensor_copy,
                      wgb[wb].rearrange("p a b -> p (a b)"),
                      wgi[wb].rearrange("p a b -> p (a b)"))
                wcast[wb] = p.vop(p.vector.tensor_copy,
                                  wub[wb].rearrange("p a b -> p (a b)"),
                                  wui[wb].rearrange("p a b -> p (a b)"))

            load_w(0)
            cast_w(0)
            i = 0
            for mc in range(HC):
                wb = mc % 2
                if mc + 1 < HC:
                    load_w(mc + 1)
                    cast_w(mc + 1)
                p.wait(p.tensor, "V", wcast[wb])
                for n0 in range(0, T, n_tile):
                    b = i % 2
                    i += 1
                    p.wait(p.tensor, "V", last_read[b])
                    for kc in range(8):
                        p.tensor.matmul(pg[b], wgb[wb][:, kc, :],
                                        xn_fm[:, kc, n0:n0 + n_tile],
                                        start=(kc == 0), stop=(kc == 7))
                    for kc in range(8):
                        ins = p.tensor.matmul(pu[b], wub[wb][:, kc, :],
                                              xn_fm[:, kc, n0:n0 + n_tile],
                                              start=(kc == 0), stop=(kc == 7))
                    pe = p.inc("P", ins, 1)
                    if n0 + n_tile == T:
                        wused[wb] = pe
                    p.wait(p.scalar, "P", pe)
                    p.wait(p.scalar, "V", gtmp_cons)
                    ac = p.aop(p.scalar.activation, out=gtmp, in_=pg[b],
                               func=AF.Silu, scale=S127)
                    p.wait(p.vector, "A", ac)
                    p.vop(p.vector.tensor_scalar_mul, gtmp, gtmp, S127 * S127)
                    cnt = p.vop(p.vector.tensor_mul,
                                h_fm[:, mc, n0:n0 + n_tile], gtmp, pu[b])
                    gtmp_cons = cnt
                    last_read[b] = cnt
            p.stack = st
        gate_done = p.val("V")

        wd = p.sb("wd", [128, HC, D], BF16)
        with contextlib.ExitStack() as st2:
            p.stack = st2
            wdi = p.sb("wdi", [128, 4, D], I32)
            wd_conv = gate_done
            for g4 in range(HC // 4):
                p.wait(p.sync, "V", wd_conv)
                ld = p.inc("L", p.sync.dma_start(
                    out=wdi,
                    in_=w_down.rearrange("(kc kp) n -> kp kc n", kp=128)[
                        :, g4 * 4:(g4 + 1) * 4, :]), 16)
                p.wait(p.vector, "L", ld)
                wd_conv = p.vop(p.vector.tensor_copy,
                                wd[:, g4 * 4:(g4 + 1) * 4, :], wdi)
            p.stack = st

        xres = [p.sb("xres", [128, D], F32) for _ in range(2)]
        ot = [p.sb("ot", [128, D], F32) for _ in range(2)]
        res_ld = [0] * NTT
        res_consumed = [wd_conv, wd_conv]
        st_done = [0, 0]

        def load_res(tt):
            b = tt % 2
            p.wait(p.sync, "V", res_consumed[b])
            res_ld[tt] = p.inc("L", p.sync.dma_start(
                out=xres[b], in_=x_dram[tt * 128:(tt + 1) * 128, :]), 16)

        load_res(0)
        load_res(1)

        def evac(tt, n0, ps_ap, pe_cnt):
            b = tt % 2
            p.wait(p.vector, "P", pe_cnt)
            p.wait(p.vector, "L", res_ld[tt])
            p.wait(p.vector, "S", st_done[b])
            cnt = p.vop(p.vector.tensor_add,
                        ot[b][:, n0:n0 + 512], xres[b][:, n0:n0 + 512], ps_ap)
            if n0 + 512 == D:
                res_consumed[b] = cnt
                p.wait(p.sync, "V", cnt)
                s = p.inc("S", p.sync.dma_start(
                    out=out_dram[tt * 128:(tt + 1) * 128, :], in_=ot[b]), 16)
                st_done[b] = s
                if tt + 2 < NTT:
                    load_res(tt + 2)
            return ("V", cnt)

        p.wait(p.tensor, "V", wd_conv)
        emit_mm_tm(p, h_fm, wd, HC, D, evac)
        p.wait(p.sync, "S")
        p.stack = old


def _dummy_out(p, x_dram, out_dram):
    tmp = p.sb("dmy", [128, D], F32)
    for tt in range(NTT):
        p.wait(p.sync, "S")
        p.inc("L", p.sync.dma_start(out=tmp, in_=x_dram[tt * 128:(tt + 1) * 128, :]), 16)
        p.wait(p.sync, "L")
        p.inc("S", p.sync.dma_start(out=out_dram[tt * 128:(tt + 1) * 128, :], in_=tmp), 16)
    p.wait(p.sync, "S")


def emit_attn(p, x_dram, out_dram, w, ident, ident_f32, dbg_phase=99):
    H = 16
    with contextlib.ExitStack() as st:
        old = p.stack
        p.stack = st
        xn_fm = p.sb("xnfm", [128, 8, T], BF16)
        q_fm = p.sb("qfm", [128, 8, T], BF16)
        k_fm = p.sb("kfm", [128, 8, T], BF16)
        v_tm = p.sb("vtm", [128, NTT, 1024], BF16)
        o_fm = p.sb("ofm", [128, 8, T], BF16)
        cext = p.sb("cext", [128, T], BF16)
        sext = p.sb("sext", [128, T], BF16)
        cmask = p.sb("cmask", [128, 128], F32)
        p.inc("W", p.gpsimd.dma_start(out=cext, in_=w["cext"]), 16)
        p.inc("W", p.gpsimd.dma_start(out=sext, in_=w["sext"]), 16)
        cw = p.inc("W", p.gpsimd.dma_start(out=cmask, in_=w["cmask"]), 16)

        ln_done = emit_ln_fm(p, x_dram, xn_fm, ident)
        if dbg_phase < 1:
            _dummy_out(p, x_dram, out_dram)
            p.stack = old
            return

        def build_eff(downT, upc, upe, eff, rot, gate_v):
            with contextlib.ExitStack() as s2:
                o2 = p.stack
                p.stack = s2
                dT = p.sb("dT", [128, 8, 1024], BF16)
                up2 = p.sb("up2", [128, 128], BF16)
                rtmp = p.sb("rtmp", [128, 32], BF16)
                pw = p.ps("pw", [128, 1024], F32)
                p.wait(p.gpsimd, "V", gate_v)
                p.inc("W", p.gpsimd.dma_start(
                    out=dT, in_=downT.rearrange("(kc kp) c -> kp kc c", kp=128)), 16)
                vz = p.vop(p.vector.memset, up2, 0.0)
                p.wait(p.gpsimd, "V", vz)
                if upc is None:
                    for half in (0, 64):
                        p.inc("W", p.gpsimd.dma_start(
                            out=up2[half:half + 64, half:half + 64], in_=upe), 16)
                else:
                    for half in (0, 64):
                        p.inc("W", p.gpsimd.dma_start(
                            out=up2[half:half + 64, half:half + 32], in_=upc), 16)
                        p.inc("W", p.gpsimd.dma_start(
                            out=rtmp[half:half + 64, :], in_=upe), 16)
                wle = p.val("W")
                p.wait(p.vector, "W", wle)
                if upc is not None:
                    for half in (0, 64):
                        ec = half + 32
                        ev2 = up2[half:half + 64, ec:ec + 32].rearrange(
                            "p (i two) -> p i two", two=2)
                        rv = rtmp[half:half + 64, :].rearrange(
                            "p (i two) -> p i two", two=2)
                        if rot:
                            p.vop(p.vector.tensor_scalar_mul,
                                  ev2[:, :, 0], rv[:, :, 1], -1.0)
                            p.vop(p.vector.tensor_copy, ev2[:, :, 1], rv[:, :, 0])
                        else:
                            p.vop(p.vector.tensor_copy,
                                  up2[half:half + 64, ec:ec + 32],
                                  rtmp[half:half + 64, :])
                wv = p.val("V")
                p.wait(p.tensor, "W", wle)
                p.wait(p.tensor, "V", wv)
                ev = 0
                for kcc in range(8):
                    p.wait(p.tensor, "V", ev)
                    for hp in range(H // 2):
                        ins = p.tensor.matmul(
                            pw[:, hp * 128:(hp + 1) * 128],
                            dT[:, hp, kcc * 128:(kcc + 1) * 128],
                            up2, start=True, stop=True)
                    pe = p.inc("P", ins, 1)
                    p.wait(p.vector, "P", pe)
                    ev = p.vop(p.vector.tensor_copy, eff[:, kcc, :], pw)
                p.stack = o2
            return ev

        def project_fm(eff, dst):
            def evac(mc, n0, ps_ap, pe_cnt):
                p.wait(p.vector, "P", pe_cnt)
                cnt = p.vop(p.vector.tensor_copy, dst[:, mc, n0:n0 + 512], ps_ap)
                return ("V", cnt)
            emit_mm_fm(p, eff, xn_fm, 8, 8, evac)

        with contextlib.ExitStack() as s3:
            o3 = p.stack
            p.stack = s3
            eff = p.sb("eff", [128, 8, 1024], BF16)
            rot_fm = o_fm
            gate = ln_done
            for (fam, dst) in (("q", q_fm), ("k", k_fm)):
                build_eff(w[fam + "dT"], w[fam + "_up_c"], w[fam + "_up_e"],
                          eff, rot=False, gate_v=gate)
                p.wait(p.tensor, "V")
                project_fm(eff, dst)
                build_eff(w[fam + "dT"], w[fam + "_up_c"], w[fam + "_up_e"],
                          eff, rot=True, gate_v=p.val("V"))
                p.wait(p.tensor, "V")
                project_fm(eff, rot_fm)
                p.wait(p.vector, "W", cw)
                for c in range(8):
                    p.vop(p.vector.tensor_tensor, rot_fm[:, c, :], rot_fm[:, c, :],
                          sext, ALU.mult)
                    p.vop(p.vector.tensor_tensor, dst[:, c, :], dst[:, c, :],
                          cext, ALU.mult)
                    p.vop(p.vector.tensor_add, dst[:, c, :], dst[:, c, :],
                          rot_fm[:, c, :])
                gate = p.val("V")
            build_eff(w["vdT"], None, w["v_up"], eff, rot=False, gate_v=gate)
            p.wait(p.tensor, "V")

            def evac_v(tt, n0, ps_ap, pe_cnt):
                p.wait(p.vector, "P", pe_cnt)
                cnt = p.vop(p.vector.tensor_copy, v_tm[:, tt, n0:n0 + 512], ps_ap)
                return ("V", cnt)
            emit_mm_tm(p, xn_fm, eff.rearrange("p a b -> p a b"), 8, 1024, evac_v)
            p.stack = o3
        qkv_done = p.val("V")
        if dbg_phase < 2:
            _dummy_out(p, x_dram, out_dram)
            p.stack = old
            return

        with contextlib.ExitStack() as s4:
            o4 = p.stack
            p.stack = s4
            ps_s = p.ps("pss", [128, T], F32)
            ps_pt = [p.ps("pspt", [128, 512], BF16) for _ in range(2)]
            ps_o = p.ps("pso", [64, 128], F32)
            p_sb = p.sb("psb", [128, T], BF16)
            pt_sb = [p.sb("ptsb", [128, 512], BF16) for _ in range(2)]
            rmax = p.sb("rmax", [128, 1], F32)
            nsm = p.sb("nsm", [128, 1], F32)
            lsum = p.sb("lsum", [128, 1], F32)
            rl = p.sb("rl", [128, 1], F32)
            s_free = ("V", qkv_done)
            o_free = ("V", 0)
            pt_free = [("V", 0), ("V", 0)]
            first = True
            pb = [0]

            def pe_base(b):
                if pb[0] != b:
                    p.wait(p.tensor, "P", p.val("P"))
                    pb[0] = b
            for h in range(min(H, getattr(emit_attn, "h_limit", H))):
                ch, base = h // 2, (h % 2) * 64
                for qt in range(min(NTT, getattr(emit_attn, "qt_limit", NTT))):
                    nk = (qt + 1) * 128
                    p.wait(p.tensor, *s_free)
                    if first:
                        p.wait(p.tensor, "V", qkv_done)
                        first = False
                    pe_base(base)
                    for n0 in range(0, nk, 512):
                        n1 = min(n0 + 512, nk)
                        ins = p.tensor.matmul(
                            ps_s[:, n0:n1],
                            q_fm[base:base + 64, ch, qt * 128:(qt + 1) * 128],
                            k_fm[base:base + 64, ch, n0:n1],
                            start=True, stop=True)
                    pe_s = p.inc("P", ins, 1)
                    p.wait(p.vector, "P", pe_s)
                    p.vop(p.vector.tensor_add, ps_s[:, nk - 128:nk],
                          ps_s[:, nk - 128:nk], cmask)
                    p.vop(p.vector.reduce_max, rmax, ps_s[:, 0:nk],
                          axis=mybir.AxisListType.X)
                    vm = p.vop(p.vector.tensor_scalar_mul, nsm, rmax, -0.125)
                    p.wait(p.scalar, "V", vm)
                    ae = p.aop(p.scalar.activation, out=p_sb[:, 0:nk],
                               in_=ps_s[:, 0:nk], func=AF.Exp, scale=0.125,
                               bias=nsm, accum_out=lsum)
                    s_free = ("A", ae)
                    p.wait(p.vector, "A", ae)
                    p.vop(p.vector.reciprocal, rl, lsum)
                    vp = p.vop(p.vector.tensor_scalar_mul, p_sb[:, 0:nk],
                               p_sb[:, 0:nk], rl)
                    nch = nk // 128
                    for g0 in range(0, nch, 4):
                        g1 = min(g0 + 4, nch)
                        bb = (g0 // 4) % 2
                        p.wait(p.tensor, *pt_free[bb])
                        p.wait(p.tensor, "V", vp)
                        pe_base(0)
                        for c in range(g0, g1):
                            ins = p.tensor.transpose(
                                ps_pt[bb][:, (c - g0) * 128:(c - g0 + 1) * 128],
                                p_sb[:, c * 128:(c + 1) * 128], ident)
                        pe_t = p.inc("P", ins, 1)
                        p.wait(p.vector, "P", pe_t)
                        ev = p.vop(p.vector.tensor_copy,
                                   pt_sb[bb][:, 0:(g1 - g0) * 128],
                                   ps_pt[bb][:, 0:(g1 - g0) * 128])
                        pt_free[bb] = ("V", ev)
                        p.wait(p.tensor, "V", ev)
                        if g0 == 0:
                            p.wait(p.tensor, *o_free)
                        for c in range(g0, g1):
                            ins = p.tensor.matmul(
                                ps_o,
                                v_tm[:, c, h * 64:(h + 1) * 64],
                                pt_sb[bb][:, (c - g0) * 128:(c - g0 + 1) * 128],
                                start=(c == 0), stop=(c == nch - 1))
                        pe_o = p.inc("P", ins, 1)
                    p.wait(p.vector, "P", pe_o)
                    vo = p.vop(p.vector.tensor_copy,
                               o_fm[base:base + 64, ch, qt * 128:(qt + 1) * 128],
                               ps_o)
                    o_free = ("V", vo)
            p.stack = o4
        attn_done = p.val("V")
        if dbg_phase < 3:
            _dummy_out(p, x_dram, out_dram)
            p.stack = old
            return

        with contextlib.ExitStack() as s5:
            o5 = p.stack
            p.stack = s5
            wc = p.sb("wc", [128, 8, 1024], BF16)
            p.wait(p.gpsimd, "V", attn_done)
            wcl = p.inc("W", p.gpsimd.dma_start(
                out=wc, in_=w["wc"].rearrange("(kc kp) n -> kp kc n", kp=128)), 16)
            p.wait(p.tensor, "W", wcl)
            xres = [p.sb("xres", [128, D], F32) for _ in range(2)]
            ot = [p.sb("ot", [128, D], F32) for _ in range(2)]
            res_ld = [0] * NTT
            res_consumed = [attn_done, attn_done]
            st_done = [0, 0]

            def load_res(tt):
                b = tt % 2
                p.wait(p.sync, "V", res_consumed[b])
                res_ld[tt] = p.inc("L", p.sync.dma_start(
                    out=xres[b], in_=x_dram[tt * 128:(tt + 1) * 128, :]), 16)

            load_res(0)
            load_res(1)

            def evac(tt, n0, ps_ap, pe_cnt):
                b = tt % 2
                p.wait(p.vector, "P", pe_cnt)
                p.wait(p.vector, "L", res_ld[tt])
                p.wait(p.vector, "S", st_done[b])
                cnt = p.vop(p.vector.tensor_add,
                            ot[b][:, n0:n0 + 512], xres[b][:, n0:n0 + 512], ps_ap)
                if n0 + 512 == D:
                    res_consumed[b] = cnt
                    p.wait(p.sync, "V", cnt)
                    s = p.inc("S", p.sync.dma_start(
                        out=out_dram[tt * 128:(tt + 1) * 128, :], in_=ot[b]), 16)
                    st_done[b] = s
                    if tt + 2 < NTT:
                        load_res(tt + 2)
                return ("V", cnt)

            p.wait(p.tensor, "V", attn_done)
            emit_mm_tm(p, o_fm, wc, 8, D, evac)
            p.wait(p.sync, "S")
            p.stack = o5
        p.stack = old


def emit_moe(p, x_dram, out_dram, gate_wt, expert_wt, ident, ident_f32):
    with contextlib.ExitStack() as st:
        old = p.stack
        p.stack = st
        xn_fm = p.sb("xnfm", [128, 8, T], BF16)
        acc = p.sb("acc", [128, NTT, D], F32)
        scores = p.sb("scores", [128, NTT, 8], F32)
        onehot = p.sb("onehot", [128, NTT, 8], F32)
        gwt = p.sb("gwt", [128, 8, 8], F32)
        srep = p.sb("srep", [128, 8], F32)
        onesk = p.sb("onesk", [128, 1], F32)
        onesr = p.sb("onesr", [1, 128], F32)
        srow = p.sb("srow", [1, 8], F32)
        xfs = p.sb("xfs", [128, 8, 128], F32)
        negmean = p.sb("negmean", [128, 1], F32)
        rm = p.sb("rm", [128, 1], F32)
        ptx = p.ps("ptx", [128, 1024], F32)
        psc = p.ps("psc", [128, 8], F32)
        psr = p.ps("psr", [128, 8], F32)

        gld = p.inc("L", p.sync.dma_start(
            out=gwt, in_=gate_wt.rearrange("(kc kp) e -> kp kc e", kp=128)), 16)
        p.vop(p.vector.memset, onesk, 1.0)
        p.vop(p.vector.memset, onesr, 1.0)
        p.wait(p.tensor, "L", gld)
        p.wait(p.tensor, "V")
        for kc in range(8):
            ins = p.tensor.matmul(psr[0:1, :], onesk, gwt[:, kc, :],
                                  start=(kc == 0), stop=(kc == 7))
        pe = p.inc("P", ins, 1)
        p.wait(p.vector, "P", pe)
        vs = p.vop(p.vector.tensor_copy, srow, psr[0:1, :])
        p.wait(p.tensor, "V", vs)
        ins = p.tensor.matmul(psr[:, :], onesr, srow, start=True, stop=True)
        pe = p.inc("P", ins, 1)
        p.wait(p.vector, "P", pe)
        p.vop(p.vector.tensor_copy, srep, psr)

        hook_state = {"ptx_free": ("V", 0), "pscore_free": ("V", 0)}

        def hook(tt, xt, xn, mean, ld):
            p.vop(p.vector.tensor_scalar_mul, negmean, mean, -1.0)
            p.wait(p.tensor, *hook_state["ptx_free"])
            for k in range(8):
                ins = p.tensor.transpose(ptx[:, k * 128:(k + 1) * 128],
                                         xt[:, k * 128:(k + 1) * 128], ident_f32)
            pe1 = p.inc("P", ins, 1)
            p.wait(p.vector, "P", pe1)
            ev = p.vop(p.vector.tensor_copy, xfs,
                       ptx.rearrange("p (c f) -> p c f", f=128))
            hook_state["ptx_free"] = ("V", ev)
            p.wait(p.tensor, "V", ev)
            for kc in range(8):
                ins = p.tensor.matmul(psc, xfs[:, kc, :], gwt[:, kc, :],
                                      start=(kc == 0), stop=(kc == 7))
            pe2 = p.inc("P", ins, 1)
            p.wait(p.vector, "P", pe2)
            p.vop(p.vector.scalar_tensor_tensor, scores[:, tt, :], srep, negmean,
                  psc, ALU.mult, ALU.add)
            p.vop(p.vector.reduce_max, rm, scores[:, tt, :], axis=mybir.AxisListType.X)
            p.vop(p.vector.tensor_single_scalar, onehot[:, tt, :], scores[:, tt, :],
                  rm, ALU.is_equal)
            p.vop(p.vector.tensor_copy, acc[:, tt, :], xn)
            va = p.vop(p.vector.tensor_add, acc[:, tt, :], acc[:, tt, :], xt)
            return [("V", va), ("P", pe1)]

        ln_done = emit_ln_fm(p, x_dram, xn_fm, ident, hook=hook)
        hook_done_v = p.val("V")

        with contextlib.ExitStack() as st2:
            p.stack = st2
            ew = [p.sb("ew", [128, 8, D], BF16) for _ in range(2)]
            pex = [p.ps("pex", [128, 512], F32) for _ in range(2)]
            ewr = expert_wt.rearrange("e (kc kp) f -> e kp kc f", kp=128)
            wld = [0, 0]
            wused = [0, 0]
            last_read = [0, 0]
            first_gate = [hook_done_v, hook_done_v]

            def load_e(e):
                wb = e % 2
                p.wait(p.gpsimd, "P", wused[wb])
                p.wait(p.gpsimd, "V", first_gate[wb])
                first_gate[wb] = 0
                wld[wb] = p.inc("W", p.gpsimd.dma_start(out=ew[wb], in_=ewr[e]), 16)

            load_e(0)
            load_e(1)
            i = 0
            for e in range(8):
                wb = e % 2
                p.wait(p.tensor, "W", wld[wb])
                for tt in range(NTT):
                    for n0 in (0, 512):
                        b = i % 2
                        i += 1
                        p.wait(p.tensor, "V", last_read[b])
                        for kc in range(8):
                            ins = p.tensor.matmul(
                                pex[b], xn_fm[:, kc, tt * 128:(tt + 1) * 128],
                                ew[wb][:, kc, n0:n0 + 512],
                                start=(kc == 0), stop=(kc == 7))
                        pe = p.inc("P", ins, 1)
                        if tt == NTT - 1 and n0 == 512:
                            wused[wb] = pe
                        p.wait(p.vector, "P", pe)
                        cnt = p.vop(p.vector.scalar_tensor_tensor,
                                    acc[:, tt, n0:n0 + 512], pex[b],
                                    onehot[:, tt, e:e + 1],
                                    acc[:, tt, n0:n0 + 512], ALU.mult, ALU.add)
                        last_read[b] = cnt
                if e + 2 < 8:
                    load_e(e + 2)
            p.stack = st

        acc_done = p.val("V")
        p.wait(p.sync, "V", acc_done)
        for tt in range(NTT):
            p.inc("S", p.sync.dma_start(
                out=out_dram[tt * 128:(tt + 1) * 128, :], in_=acc[:, tt, :]), 16)
        p.wait(p.sync, "S")
        p.stack = old


def emit_mamba(p, xn_fm, w, scr, evac_out, gate_v=0):
    EC = 16
    with contextlib.ExitStack() as st:
        old = p.stack
        p.stack = st
        convw = p.sb("convw", [128, EC, 4], F32)
        convb = p.sb("convb", [128, EC], F32)
        negdtb = p.sb("negdtb", [128, EC], F32)
        apos = p.sb("apos", [128, EC, 16], F32)
        dpar = p.sb("dpar", [128, EC], F32)
        selp = p.sb("selp", [16, T], BF16)
        seln = p.sb("seln", [16, T], BF16)
        p.wait(p.gpsimd, "V", gate_v)
        p.inc("W", p.gpsimd.dma_start(out=convw, in_=w["convw"]), 16)
        p.inc("W", p.gpsimd.dma_start(out=convb, in_=w["convb"]), 16)
        p.inc("W", p.gpsimd.dma_start(out=negdtb, in_=w["dtpb"]), 16)
        p.inc("W", p.gpsimd.dma_start(out=apos, in_=w["alog"]), 16)
        p.inc("W", p.gpsimd.dma_start(out=dpar, in_=w["dpar"]), 16)
        p.inc("W", p.gpsimd.dma_start(out=selp, in_=w["selp"]), 16)
        wsm = p.inc("W", p.gpsimd.dma_start(out=seln, in_=w["seln"]), 16)
        p.wait(p.vector, "W", wsm)
        p.vop(p.vector.tensor_scalar_mul, negdtb, negdtb, -1.0)
        p.wait(p.scalar, "W", wsm)
        p.aop(p.scalar.activation, out=apos.rearrange("p a b -> p (a b)"),
              in_=apos.rearrange("p a b -> p (a b)"), func=AF.Exp)

        stbuf = [p.sb("stg", [128, 512], BF16) for _ in range(2)]
        st_state = {"free": [0, 0], "i": 0}

        def stage_store(src_ap, dst_ap, pe_cnt=None):
            b = st_state["i"] % 2
            st_state["i"] += 1
            if pe_cnt is not None:
                p.wait(p.vector, "P", pe_cnt)
            p.wait(p.vector, "S", st_state["free"][b])
            cnt = p.vop(p.vector.tensor_copy, stbuf[b], src_ap)
            p.wait(p.sync, "V", cnt)
            s = p.inc("S", p.sync.dma_start(out=dst_ap, in_=stbuf[b]), 16)
            st_state["free"][b] = s
            return ("V", cnt)

        stx = contextlib.ExitStack()
        p.stack = stx
        xh = p.sb("xh", [128, EC, T], BF16)
        with contextlib.ExitStack() as s1:
            p.stack = s1
            inw = p.sb("inw", [128, 8, 4096], BF16)
            p.wait(p.gpsimd, "V")
            wl = p.inc("W", p.gpsimd.dma_start(
                out=inw, in_=w["inwT"].rearrange("(kc kp) n -> kp kc n", kp=128)), 16)
            p.wait(p.tensor, "W", wl)

            def evac1(mc, n0, ps_ap, pe_cnt):
                if mc < EC:
                    p.wait(p.vector, "P", pe_cnt)
                    cnt = p.vop(p.vector.tensor_copy, xh[:, mc, n0:n0 + 512], ps_ap)
                    return ("V", cnt)
                return stage_store(ps_ap, scr["z"][:, mc - EC, n0 // 128:n0 // 128 + 4, :]
                                   .rearrange("p a b -> p (a b)"), pe_cnt)

            emit_mm_fm(p, inw, xn_fm, 8, 32, evac1)
            p.stack = stx

        dbg_xh = getattr(emit_mamba, "dbg_xh", None)
        if dbg_xh is not None:
            p.wait(p.sync, "V")
            p.inc("S", p.sync.dma_start(out=dbg_xh, in_=xh), 16)
            p.wait(p.vector, "S")
            emit_mamba.dbg_xh = None
        acc = p.sb("cacc", [128, T], F32)
        sil2d = p.sb("sil2d", [128, T], BF16)
        for dc in range(EC):
            p.vop(p.vector.tensor_scalar_mul, acc, xh[:, dc, :],
                  convw[:, dc, 3:4])
            for s in (1, 2, 3):
                p.vop(p.vector.scalar_tensor_tensor, acc[:, s:T],
                      xh[:, dc, 0:T - s], convw[:, dc, 3 - s:4 - s],
                      acc[:, s:T], ALU.mult, ALU.add)
            vc = p.val("V")
            p.wait(p.scalar, "V", vc)
            av = p.aop(p.scalar.activation, out=sil2d, in_=acc, func=AF.Silu,
                       bias=convb[:, dc:dc + 1])
            p.wait(p.vector, "A", av)
            p.vop(p.vector.tensor_copy, xh[:, dc, :], sil2d)
        u_done = p.val("A")
        p.wait(p.vector, "A", u_done)
        dbg_u = getattr(emit_mamba, "dbg_u", None)
        if dbg_u is not None:
            p.wait(p.sync, "A", u_done)
            p.inc("S", p.sync.dma_start(out=dbg_u, in_=xh), 16)
            emit_mamba.dbg_u = None

        with contextlib.ExitStack() as s3:
            p.stack = s3
            xpw = p.sb("xpw", [128, EC, 96], BF16)
            dtpw = p.sb("dtpw", [64, T], BF16)
            dtr = p.sb("dtr", [64, T], BF16)
            bm = p.sb("bm", [16, T], BF16)
            cm = p.sb("cm", [16, T], BF16)
            sgt = p.sb("sgt", [128, 512], F32)
            pj = [p.ps("pj", [128, 512], F32) for _ in range(2)]
            p.wait(p.gpsimd, "V")
            p.inc("W", p.gpsimd.dma_start(
                out=xpw, in_=w["xpwT"].rearrange("(kc kp) n -> kp kc n", kp=128)), 16)
            wl = p.inc("W", p.gpsimd.dma_start(out=dtpw, in_=w["dtpwT"]), 16)
            p.wait(p.tensor, "W", wl)
            p.wait(p.tensor, "A", u_done)
            entry_v = p.val("V")
            bank_free = [("V", entry_v), ("V", entry_v)]
            ib = [0]

            def mm_group(lhsT_slices, rhs_slices, n0, nsz, M):
                b = ib[0] % 2
                ib[0] += 1
                p.wait(p.tensor, *bank_free[b])
                for ki, (lh, rh) in enumerate(zip(lhsT_slices, rhs_slices)):
                    ins = p.tensor.matmul(pj[b][0:M, 0:nsz], lh, rh,
                                          start=(ki == 0),
                                          stop=(ki == len(lhsT_slices) - 1))
                return b, p.inc("P", ins, 1)

            for n0 in range(0, T, 512):
                for (dst, c0, c1) in ((dtr, 0, 64), (bm, 64, 80), (cm, 80, 96)):
                    M = c1 - c0
                    b, pe = mm_group(
                        [xpw[:, kc, c0:c1] for kc in range(EC)],
                        [xh[:, kc, n0:n0 + 512] for kc in range(EC)],
                        n0, 512, M)
                    p.wait(p.vector, "P", pe)
                    cnt = p.vop(p.vector.tensor_copy, dst[0:M, n0:n0 + 512],
                                pj[b][0:M, 0:512])
                    bank_free[b] = ("V", cnt)
            for mc in range(EC):
                for n0 in range(0, T, 512):
                    b, pe = mm_group(
                        [dtpw[:, mc * 128:(mc + 1) * 128]],
                        [dtr[:, n0:n0 + 512]], n0, 512, 128)
                    p.wait(p.scalar, "P", pe)
                    ac = p.aop(p.scalar.activation, out=sgt, in_=pj[b][:, 0:512],
                               func=AF.Sigmoid, scale=-1.0,
                               bias=negdtb[:, mc:mc + 1])
                    bank_free[b] = ("A", ac)
                    b2 = st_state["i"] % 2
                    st_state["i"] += 1
                    p.wait(p.scalar, "S", st_state["free"][b2])
                    ac2 = p.aop(p.scalar.activation, out=stbuf[b2], in_=sgt,
                                func=AF.Ln)
                    p.wait(p.sync, "A", ac2)
                    s = p.inc("S", p.sync.dma_start(
                        out=scr["nd"][:, mc, n0 // 128:n0 // 128 + 4, :]
                        .rearrange("p a b -> p (a b)"), in_=stbuf[b2]), 16)
                    st_state["free"][b2] = s
            for n in range(16):
                for n0 in range(0, T, 512):
                    for (src, sel, dst) in ((bm, seln, scr["brep"]),
                                            (cm, selp, scr["crep"])):
                        b, pe = mm_group(
                            [sel[:, n * 128:(n + 1) * 128]],
                            [src[:, n0:n0 + 512]], n0, 512, 128)
                        bank_free[b] = stage_store(
                            pj[b][:, 0:512],
                            dst[:, n, n0 // 128:n0 // 128 + 4, :]
                            .rearrange("p a b -> p (a b)"), pe)
            p.stack = stx
        p3_done = p.val("S")

        with contextlib.ExitStack() as s4:
            p.stack = s4
            du = p.sb("du", [128, T], BF16)
            ndt = p.sb("ndt", [128, T], BF16)
            dA = p.sb("dA", [128, T], BF16)
            dBu = p.sb("dBu", [128, T], BF16)
            hh = p.sb("hh", [128, T], BF16)
            hC = p.sb("hC", [128, T], BF16)
            yac = p.sb("yac", [128, T], F32)
            zs = p.sb("zs", [128, T], BF16)
            bc = [p.sb("bcb", [128, T], BF16) for _ in range(2)]
            cc = [p.sb("ccb", [128, T], BF16) for _ in range(2)]
            bc_ld = [0, 0]
            bc_used = [0, 0]
            p.wait(p.sync, "S", p3_done)
            nd_ld = 0
            nd_used = 0

            def load_bc(dc, n):
                b = n % 2
                p.wait(p.sync, "V", bc_used[b])
                p.inc("L", p.sync.dma_start(
                    out=bc[b], in_=scr["brep"][:, n, :, :].rearrange("p a b -> p (a b)")), 16)
                bc_ld[b] = p.inc("L", p.sync.dma_start(
                    out=cc[b], in_=scr["crep"][:, n, :, :].rearrange("p a b -> p (a b)")), 16)

            for dc in range(EC):
                p.wait(p.sync, "V", nd_used)
                ndl = p.inc("L", p.sync.dma_start(
                    out=ndt, in_=scr["nd"][:, dc, :, :].rearrange("p a b -> p (a b)")), 16)
                p.wait(p.vector, "L", ndl)
                p.vop(p.vector.tensor_mul, du, xh[:, dc, :], ndt)
                load_bc(dc, 0)
                load_bc(dc, 1)
                gC = 0
                yprev = 0
                for n in range(16):
                    b = n % 2
                    p.wait(p.scalar, "L", ndl)
                    p.wait(p.scalar, "V")
                    av = p.aop(p.scalar.activation, out=dA, in_=ndt, func=AF.Exp,
                               scale=apos[:, dc, n:n + 1])
                    p.wait(p.vector, "L", bc_ld[b])
                    p.vop(p.vector.tensor_mul, dBu, du, bc[b])
                    p.wait(p.vector, "A", av)
                    p.vop(p.vector.tensor_tensor_scan, hh, dA, dBu, 0.0,
                          ALU.mult, ALU.add)
                    p.vop(p.vector.tensor_mul, hh, hh, cc[b])
                    if n == 0:
                        cnt = p.vop(p.vector.tensor_copy, yac, hh)
                    else:
                        cnt = p.vop(p.vector.tensor_add, yac, yac, hh)
                    bc_used[b] = cnt
                    if n + 2 < 16:
                        load_bc(dc, n + 2)
                nd_used = p.val("V")
                p.vop(p.vector.scalar_tensor_tensor, yac, xh[:, dc, :],
                      dpar[:, dc:dc + 1], yac, ALU.mult, ALU.add)
                zl = p.inc("L", p.sync.dma_start(
                    out=zs, in_=scr["z"][:, dc, :, :].rearrange("p a b -> p (a b)")), 16)
                p.wait(p.scalar, "L", zl)
                p.wait(p.scalar, "V")
                az = p.aop(p.scalar.activation, out=zs, in_=zs, func=AF.Silu)
                p.wait(p.vector, "A", az)
                yv = p.vop(p.vector.tensor_mul, hh, yac, zs)
                p.wait(p.sync, "V", yv)
                p.inc("S", p.sync.dma_start(out=scr["y"][:, dc, :], in_=hh), 16)
            p.stack = stx
        stx.close()
        p.stack = st
        y_done = p.val("S")

        with contextlib.ExitStack() as s5:
            p.stack = s5
            opw = p.sb("opw", [128, EC, 1024], BF16)
            p.wait(p.gpsimd, "V")
            wl = p.inc("W", p.gpsimd.dma_start(
                out=opw, in_=w["opwT"].rearrange("(kc kp) n -> kp kc n", kp=128)), 16)
            yt = p.sb("yt", [128, EC, T], BF16)
            p.wait(p.sync, "S", y_done)
            p.wait(p.sync, "V")
            yl = p.inc("L", p.sync.dma_start(out=yt, in_=scr["y"]), 16)
            p.wait(p.tensor, "W", wl)
            p.wait(p.tensor, "L", yl)
            with contextlib.ExitStack() as s5b:
                p.stack = s5b
                n_tile = 512
                pst = [p.ps("mt", [128, n_tile], F32) for _ in range(2)]
                entry_v = p.val("V")
                last_read = [("V", entry_v), ("V", entry_v)]
                i = 0
                for tt in range(NTT):
                    for n0 in range(0, 1024, n_tile):
                        b = i % 2
                        i += 1
                        p.wait(p.tensor, *last_read[b])
                        for kc in range(EC):
                            ins = p.tensor.matmul(
                                pst[b], yt[:, kc, tt * 128:(tt + 1) * 128],
                                opw[:, kc, n0:n0 + n_tile],
                                start=(kc == 0), stop=(kc == EC - 1))
                        pe = p.inc("P", ins, 1)
                        last_read[b] = evac_out(tt, n0, pst[b], pe)
                p.stack = s5
            p.stack = st
        p.stack = old


def make_res_evac(p, x_dram, out_dram, gate_v):
    xres = [p.sb("xres", [128, D], F32) for _ in range(2)]
    ot = [p.sb("ot", [128, D], F32) for _ in range(2)]
    res_ld = [0] * NTT
    res_consumed = [gate_v, gate_v]
    st_done = [0, 0]

    def load_res(tt):
        b = tt % 2
        p.wait(p.sync, "V", res_consumed[b])
        res_ld[tt] = p.inc("L", p.sync.dma_start(
            out=xres[b], in_=x_dram[tt * 128:(tt + 1) * 128, :]), 16)

    load_res(0)
    load_res(1)

    def evac(tt, n0, ps_ap, pe_cnt):
        b = tt % 2
        p.wait(p.vector, "P", pe_cnt)
        p.wait(p.vector, "L", res_ld[tt])
        p.wait(p.vector, "S", st_done[b])
        cnt = p.vop(p.vector.tensor_add,
                    ot[b][:, n0:n0 + 512], xres[b][:, n0:n0 + 512], ps_ap)
        if n0 + 512 == D:
            res_consumed[b] = cnt
            p.wait(p.sync, "V", cnt)
            s = p.inc("S", p.sync.dma_start(
                out=out_dram[tt * 128:(tt + 1) * 128, :], in_=ot[b]), 16)
            st_done[b] = s
            if tt + 2 < NTT:
                load_res(tt + 2)
        return ("V", cnt)

    return evac


def emit_rowrep(p, row_sb, dst_sb, onesr, pswr):
    p.wait(p.tensor, "V")
    for n0 in range(0, D, 512):
        ins = p.tensor.matmul(pswr[:, n0 % 1024:n0 % 1024 + 512],
                              onesr, row_sb[0:1, n0:n0 + 512],
                              start=True, stop=True)
        pe = p.inc("P", ins, 1)
        p.wait(p.vector, "P", pe)
        p.vop(p.vector.tensor_copy, dst_sb[:, n0:n0 + 512],
              pswr[:, n0 % 1024:n0 % 1024 + 512])
    return p.val("V")


def emit_ln_tm(p, x_dram, out_dram, wrep=None, brep=None, add_dram=None):
    with contextlib.ExitStack() as _st:
        _old = p.stack
        p.stack = _st
        xt = [p.sb("lpx", [128, D], F32) for _ in range(2)]
        ad = [p.sb("lpa", [128, D], F32) for _ in range(2)]
        sq = p.sb("lpsq", [128, D], BF16)
        s1 = p.sb("lps1", [128, 1], F32)
        s2 = p.sb("lps2", [128, 1], F32)
        mean = p.sb("lpmean", [128, 1], F32)
        nm2 = p.sb("lpnm2", [128, 1], F32)
        var = p.sb("lpvar", [128, 1], F32)
        rstd = p.sb("lprstd", [128, 1], F32)
        ot = [p.sb("lpot", [128, D], F32) for _ in range(2)]
        cons = [0, 0]
        st_done = [0, 0]
        s2_cons = 0
        for tt in range(NTT):
            b = tt % 2
            p.wait(p.sync, "V", cons[b])
            ld = p.inc("L", p.sync.dma_start(
                out=xt[b], in_=x_dram[tt * 128:(tt + 1) * 128, :]), 16)
            if add_dram is not None:
                ld = p.inc("L", p.sync.dma_start(
                    out=ad[b], in_=add_dram[tt * 128:(tt + 1) * 128, :]), 16)
            p.wait(p.vector, "L", ld)
            p.vop(p.vector.reduce_sum, s1, xt[b], axis=mybir.AxisListType.X)
            p.wait(p.scalar, "L", ld)
            p.wait(p.scalar, "V", s2_cons)
            a1 = p.aop(p.scalar.activation, out=sq, in_=xt[b], func=AF.Square,
                       accum_out=s2)
            p.vop(p.vector.tensor_scalar_mul, mean, s1, 1.0 / D)
            p.wait(p.vector, "A", a1)
            p.vop(p.vector.tensor_scalar, nm2, s1, mean, -1.0 / D, ALU.mult, ALU.mult)
            s2_cons = p.vop(p.vector.tensor_scalar, var, s2, 1.0 / D, nm2,
                            ALU.mult, ALU.add)
            v1 = p.vop(p.vector.tensor_scalar_add, var, var, EPS)
            p.wait(p.scalar, "V", v1)
            p.aop(p.scalar.activation, out=rstd, in_=var, func=AF.Ln)
            a2 = p.aop(p.scalar.activation, out=rstd, in_=rstd, func=AF.Exp,
                       scale=-0.5)
            p.wait(p.vector, "A", a2)
            p.wait(p.vector, "S", st_done[b])
            cnt = p.vop(p.vector.tensor_scalar, ot[b], xt[b], mean, rstd,
                        ALU.subtract, ALU.mult)
            if wrep is not None:
                cnt = p.vop(p.vector.tensor_mul, ot[b], ot[b], wrep)
            if brep is not None:
                cnt = p.vop(p.vector.tensor_add, ot[b], ot[b], brep)
            if add_dram is not None:
                cnt = p.vop(p.vector.tensor_add, ot[b], ot[b], ad[b])
            cons[b] = cnt
            p.wait(p.sync, "V", cnt)
            st_done[b] = p.inc("S", p.sync.dma_start(
                out=out_dram[tt * 128:(tt + 1) * 128, :], in_=ot[b]), 16)
        p.wait(p.sync, "S")
        p.stack = _old


def emit_l1_attn_a(p, x_dram, out_dram, wl, wn, scr, ident):
    with contextlib.ExitStack() as st:
        old = p.stack
        p.stack = st
        onesr = p.sb("onesr", [1, 128], F32)
        wrow = p.sb("wrow", [1, D], F32)
        wrep = p.sb("wrep", [128, D], F32)
        brep = p.sb("brep2", [128, D], F32)
        pswr = p.ps("pswr", [128, 1024], F32)
        p.vop(p.vector.memset, onesr, 1.0)

        def load_reps(w_ap, b_ap):
            p.wait(p.sync, "V")
            lw = p.inc("L", p.sync.dma_start(out=wrow, in_=w_ap), 16)
            p.wait(p.tensor, "L", lw)
            v1 = emit_rowrep(p, wrow, wrep, onesr, pswr)
            p.wait(p.sync, "V", v1)
            lb = p.inc("L", p.sync.dma_start(out=wrow, in_=b_ap), 16)
            p.wait(p.tensor, "L", lb)
            return emit_rowrep(p, wrow, brep, onesr, pswr)

        emit_ln_tm(p, x_dram, scr["res0"])
        res = [scr["res0"], scr["res1"]]
        nlayers = getattr(emit_l1_attn_a, "nlayers", 4)
        for i in range(nlayers):
            rin, rout = res[i % 2], res[(i + 1) % 2]
            gv = load_reps(wn["norm_w"][i:i + 1, :], wn["norm_b"][i:i + 1, :])
            with contextlib.ExitStack() as sl:
                p.stack = sl
                xn_fm = p.sb("xnfm", [128, 8, T], BF16)
                p.wait(p.sync, "S")
                ln_done = emit_ln_fm(p, rin, xn_fm, ident, gw=None,
                                     wrep2=wrep, brep2=brep)
                evac = make_res_evac(p, rin, rout, ln_done)
                emit_mamba(p, xn_fm, wl[i], scr, evac, gate_v=ln_done)
                p.wait(p.sync, "S")
                p.stack = st
        gv = load_reps(wn["normf_w"], wn["normf_b"])
        p.wait(p.sync, "S")
        emit_ln_tm(p, res[nlayers % 2], out_dram, wrep=wrep, brep=brep,
                   add_dram=x_dram)
        p.stack = old


def emit_l1_attn_b(p, x_dram, out_dram, w, scr, ident):
    with contextlib.ExitStack() as st:
        old = p.stack
        p.stack = st
        xn_fm = p.sb("xnfm", [128, 8, T], BF16)
        ln_done = emit_ln_fm(p, x_dram, xn_fm, ident)
        evac = make_res_evac(p, x_dram, out_dram, ln_done)
        emit_mamba(p, xn_fm, w, scr, evac, gate_v=ln_done)
        p.wait(p.sync, "S")
        p.stack = old


def emit_slstm_sublayer(p, l, xa, xb, xnext, w, scr, ident):
    H = 4
    with contextlib.ExitStack() as st:
        old = p.stack
        p.stack = st
        onesr = p.sb("onesr", [1, 128], F32)
        wrow = p.sb("wrow", [1, D], F32)
        wrep = p.sb("wrep", [128, D], F32)
        pswr = p.ps("pswr", [128, 1024], F32)
        p.vop(p.vector.memset, onesr, 1.0)

        def load_rep(row_ap):
            p.wait(p.sync, "V")
            lw = p.inc("L", p.sync.dma_start(out=wrow, in_=row_ap), 16)
            p.wait(p.tensor, "L", lw)
            return emit_rowrep(p, wrow, wrep, onesr, pswr)

        y_fm = p.sb("yfm", [128, 8, T], BF16)

        with contextlib.ExitStack() as sa:
            p.stack = sa
            gv = load_rep(w["ln1w"][l:l + 1, :])
            xn_fm = p.sb("xnfm", [128, 8, T], BF16)
            ln_done = emit_ln_fm(p, xa, xn_fm, ident, wrep2=wrep)
            xc_fm = p.sb("xcfm", [128, 8, T], BF16)
            convw = p.sb("convw", [128, 8, 4], F32)
            convb = p.sb("convb", [128, 8], F32)
            p.wait(p.gpsimd, "V", ln_done)
            p.inc("W", p.gpsimd.dma_start(out=convw, in_=w["convw"][l]), 16)
            cwl = p.inc("W", p.gpsimd.dma_start(out=convb, in_=w["convb"][l]), 16)
            acc = p.sb("cacc", [128, T], F32)
            sil2d = p.sb("sil2d", [128, T], BF16)
            p.wait(p.vector, "W", cwl)
            p.wait(p.scalar, "W", cwl)
            for dc in range(8):
                p.vop(p.vector.tensor_scalar_mul, acc, xn_fm[:, dc, :],
                      convw[:, dc, 3:4])
                for s in (1, 2, 3):
                    p.vop(p.vector.scalar_tensor_tensor, acc[:, s:T],
                          xn_fm[:, dc, 0:T - s], convw[:, dc, 3 - s:4 - s],
                          acc[:, s:T], ALU.mult, ALU.add)
                vc = p.val("V")
                p.wait(p.scalar, "V", vc)
                av = p.aop(p.scalar.activation, out=sil2d, in_=acc, func=AF.Silu,
                           bias=convb[:, dc:dc + 1])
                p.wait(p.vector, "A", av)
                p.vop(p.vector.tensor_copy, xc_fm[:, dc, :], sil2d)
            wg_sb = p.sb("wg", [128, 8, 1024], BF16)
            brow = p.sb("brow", [1, 4096], F32)
            b_rep = p.sb("brep32", [32, 4096], BF16)
            p.wait(p.gpsimd, "V")
            p.inc("W", p.gpsimd.dma_start(
                out=wg_sb, in_=w["wg"][l].rearrange("h (kc kp) n -> kp (h kc) n", kp=128)), 16)
            bwl = p.inc("W", p.gpsimd.dma_start(out=brow, in_=w["cb"][l]), 16)
            ones32 = p.sb("ones32", [1, 32], F32)
            p.vop(p.vector.memset, ones32, 1.0)
            p.wait(p.tensor, "W", bwl)
            for n0 in range(0, 4096, 512):
                p.wait(p.tensor, "V")
                ins = p.tensor.matmul(pswr[0:32, n0 % 1024:n0 % 1024 + 512],
                                      ones32, brow[0:1, n0:n0 + 512],
                                      start=True, stop=True)
                pe = p.inc("P", ins, 1)
                p.wait(p.vector, "P", pe)
                p.vop(p.vector.tensor_copy, b_rep[:, n0:n0 + 512],
                      pswr[0:32, n0 % 1024:n0 % 1024 + 512])
            stg = [p.sb("stg", [32, 1024], BF16) for _ in range(2)]
            stg_free = [0, 0]
            pgt = [p.ps("pgt", [32, 1024], F32) for _ in range(2)]
            pgt_free = ["V", 0, "V", 0]
            gmm = p.val("V")
            p.wait(p.tensor, "V", gmm)
            si = 0
            for t0 in range(0, T, 32):
                tc, tof = t0 // 128, t0 % 128
                for h in range(H):
                    b = si % 2
                    p.wait(p.tensor, pgt_free[2 * b], pgt_free[2 * b + 1])
                    for g, src in ((0, xc_fm), (1, xn_fm), (2, xc_fm), (3, xn_fm)):
                        for kc in range(2):
                            ins = p.tensor.matmul(
                                pgt[b][:, g * 256:(g + 1) * 256],
                                src[:, 2 * h + kc, t0:t0 + 32],
                                wg_sb[:, 2 * h + kc, g * 256:(g + 1) * 256],
                                start=(kc == 0), stop=(kc == 1))
                    pe = p.inc("P", ins, 1)
                    p.wait(p.vector, "P", pe)
                    p.wait(p.vector, "S", stg_free[b])
                    cnt = p.vop(p.vector.tensor_add, stg[b], pgt[b],
                                b_rep[:, h * 1024:(h + 1) * 1024])
                    pgt_free[2 * b], pgt_free[2 * b + 1] = "V", cnt
                    p.wait(p.sync, "V", cnt)
                    s = p.inc("S", p.sync.dma_start(
                        out=scr["wx"][t0:t0 + 32, h * 1024:(h + 1) * 1024],
                        in_=stg[b]), 16)
                    stg_free[b] = s
                    si += 1
            p.stack = st
        ga_done_s = p.val("S")
        ga_done_v = p.val("V")

        with contextlib.ExitStack() as sb:
            p.stack = sb
            R_sb = p.sb("rk", [128, 8, 1024], BF16)
            p.wait(p.gpsimd, "V", ga_done_v)
            rl = p.inc("W", p.gpsimd.dma_start(
                out=R_sb, in_=w["rk"][l].rearrange("h (kc kp) n -> kp (h kc) n", kp=128)), 16)
            SL = 16
            slab = [p.sb("wxsl", [128, SL, 1024], BF16) for _ in range(2)]
            slab_ld = [0, 0]
            slab_free = [ga_done_v, ga_done_v]
            rawb = p.sb("rawb", [128, 1024], BF16)
            sfo = p.sb("sfo", [128, 512], BF16)
            ei = p.sb("ei", [128, 256], BF16)
            tz = p.sb("tz", [128, 256], BF16)
            t1 = p.sb("t1", [128, 256], BF16)
            cst = p.sb("cst", [128, 256], BF16)
            nst = p.sb("nst", [128, 256], BF16)
            rn = p.sb("rn", [128, 256], BF16)
            h_lm = p.sb("hlm", [128, 256], BF16)
            h0 = p.sb("h0", [128, 8], BF16)
            praw = p.ps("praw", [128, 1024], F32)
            ptr = p.ps("ptr", [128, 256], BF16)
            p.vop(p.vector.memset, cst, 0.0)
            p.vop(p.vector.memset, nst, 0.0)
            p.vop(p.vector.memset, h0, 0.0)
            wxr = scr["wx"].rearrange("(nt sl) (hh n) -> hh nt sl n", sl=SL, hh=4)

            def load_slab(si_):
                b = si_ % 2
                p.wait(p.sync, "V", slab_free[b])
                p.wait(p.sync, "S", ga_done_s)
                slab_ld[b] = p.inc("L", p.sync.dma_start(
                    out=slab[b][::32, :, :],
                    in_=wxr[:, si_, :, :]), 16)

            load_slab(0)
            load_slab(1)
            p.wait(p.tensor, "W", rl)
            prev_v = p.val("V")
            a_sfo = 0
            _lp = p.nc.allow_low_precision("bf16 slstm state chain (tolerance-checked)")
            _lp.__enter__()
            for t in range(T):
                sidx = t // SL
                s = t % SL
                b = sidx % 2
                p.wait(p.tensor, "V", prev_v)
                for h in range(H):
                    for nh in (0, 512):
                        for kc in range(2):
                            lh = h0[:, 2 * h + kc:2 * h + kc + 1] if t == 0 else \
                                y_fm[:, 2 * h + kc, t - 1:t]
                            ins = p.tensor.matmul(
                                praw[32 * h:32 * h + 1, nh:nh + 512], lh,
                                R_sb[:, 2 * h + kc, nh:nh + 512],
                                start=(kc == 0), stop=(kc == 1),
                                tile_position=(0, 32 * h))
                pe = p.inc("P", ins, 1)
                p.wait(p.vector, "P", pe)
                if s == 0:
                    p.wait(p.vector, "L", slab_ld[b])
                nc_ = p.nc
                p.inc("V", nc_.vector.tensor_add(rawb, praw, slab[b][:, s, :]), 1)
                raw_v = p.val("V")
                if s == SL - 1:
                    slab_free[b] = raw_v
                    if (sidx + 2) * SL < T:
                        load_slab(sidx + 2)
                p.wait(p.scalar, "V", raw_v)
                a_f = p.inc("A", nc_.scalar.activation(out=sfo[:, 0:256], in_=rawb[:, 512:768], func=AF.Sigmoid), 1)
                a_ei = p.inc("A", nc_.scalar.activation(out=ei, in_=rawb[:, 0:256], func=AF.Exp), 1)
                p.wait(p.vector, "A", a_ei)
                nc_.vector.tensor_mul(nst, nst, sfo[:, 0:256])
                nc_.vector.tensor_add(nst, nst, ei)
                nc_.vector.reciprocal(rn, nst)
                a_tz = p.inc("A", nc_.scalar.activation(out=tz, in_=rawb[:, 256:512], func=AF.Tanh), 1)
                a_sfo = p.inc("A", nc_.scalar.activation(out=sfo[:, 256:512], in_=rawb[:, 768:1024], func=AF.Sigmoid), 1)
                p.wait(p.vector, "A", a_tz)
                nc_.vector.tensor_mul(t1, ei, tz)
                nc_.vector.tensor_mul(cst, cst, sfo[:, 0:256])
                nc_.vector.tensor_add(cst, cst, t1)
                nc_.vector.tensor_mul(t1, cst, rn)
                p.wait(p.vector, "A", a_sfo)
                p.inc("V", nc_.vector.tensor_mul(h_lm, t1, sfo[:, 256:512]), 1)
                h_v = p.val("V")
                p.wait(p.tensor, "V", h_v)
                nc_.tensor.transpose(ptr[:, 0:128], h_lm[:, 0:128], ident)
                ins = nc_.tensor.transpose(ptr[:, 128:256], h_lm[:, 128:256], ident)
                pt_ = p.inc("P", ins, 1)
                p.wait(p.vector, "P", pt_)
                p.inc("V", nc_.vector.tensor_copy(
                    y_fm[:, :, t].rearrange("p (l k) -> p k l", k=2).unsqueeze(3),
                    ptr.rearrange("p (k l q) -> p k l q", k=2, q=32)[:, :, :, 0:1]), 1)
                prev_v = p.val("V")
            _lp.__exit__(None, None, None)
            p.stack = st
        rec_done = p.val("V")

        with contextlib.ExitStack() as sc:
            p.stack = sc
            gnv = load_rep(w["gnw"][l:l + 1, :])
            ytile = p.sb("ytile", [128, 1024], BF16)
            sq = p.sb("gnsq", [128, 256], BF16)
            s1 = p.sb("gns1", [128, 1], F32)
            s2 = p.sb("gns2", [128, 1], F32)
            mean = p.sb("gnmean", [128, 1], F32)
            nm2 = p.sb("gnnm2", [128, 1], F32)
            var = p.sb("gnvar", [128, 1], F32)
            rstd = p.sb("gnrstd", [128, 1], F32)
            xat = [p.sb("xat", [128, D], F32) for _ in range(2)]
            ot = [p.sb("got", [128, D], F32) for _ in range(2)]
            ptm = p.ps("ptm", [128, 1024], BF16)
            xa_ld = [0, 0]
            st_done = [0, 0]
            cons = [rec_done, rec_done]
            ptm_free = rec_done
            for tt in range(NTT):
                bb = tt % 2
                p.wait(p.sync, "V", cons[bb])
                xa_ld[bb] = p.inc("L", p.sync.dma_start(
                    out=xat[bb], in_=xa[tt * 128:(tt + 1) * 128, :]), 16)
                p.wait(p.tensor, "V", ptm_free)
                p.wait(p.tensor, "V", rec_done)
                for k in range(8):
                    ins = p.tensor.transpose(ptm[:, k * 128:(k + 1) * 128],
                                             y_fm[:, k, tt * 128:(tt + 1) * 128],
                                             ident)
                pe = p.inc("P", ins, 1)
                p.wait(p.vector, "P", pe)
                ptm_free = p.vop(p.vector.tensor_copy, ytile, ptm)
                for h in range(H):
                    sl_ = ytile[:, h * 256:(h + 1) * 256]
                    p.vop(p.vector.reduce_sum, s1, sl_, axis=mybir.AxisListType.X)
                    p.wait(p.scalar, "V")
                    a1 = p.aop(p.scalar.activation, out=sq, in_=sl_, func=AF.Square,
                               accum_out=s2)
                    p.vop(p.vector.tensor_scalar_mul, mean, s1, 1.0 / 256)
                    p.wait(p.vector, "A", a1)
                    p.vop(p.vector.tensor_scalar, nm2, s1, mean, -1.0 / 256,
                          ALU.mult, ALU.mult)
                    p.vop(p.vector.tensor_scalar, var, s2, 1.0 / 256, nm2,
                          ALU.mult, ALU.add)
                    v1 = p.vop(p.vector.tensor_scalar_add, var, var, EPS)
                    p.wait(p.scalar, "V", v1)
                    p.aop(p.scalar.activation, out=rstd, in_=var, func=AF.Ln)
                    a2 = p.aop(p.scalar.activation, out=rstd, in_=rstd, func=AF.Exp,
                               scale=-0.5)
                    p.wait(p.vector, "A", a2)
                    p.vop(p.vector.tensor_scalar_sub, sl_, sl_, mean)
                    p.vop(p.vector.tensor_scalar_mul, sl_, sl_, rstd)
                p.wait(p.vector, "L", xa_ld[bb])
                p.wait(p.vector, "S", st_done[bb])
                p.vop(p.vector.tensor_mul, ot[bb], ytile, wrep)
                cnt = p.vop(p.vector.tensor_add, ot[bb], ot[bb], xat[bb])
                cons[bb] = cnt
                p.wait(p.sync, "V", cnt)
                st_done[bb] = p.inc("S", p.sync.dma_start(
                    out=xb[tt * 128:(tt + 1) * 128, :], in_=ot[bb]), 16)
            p.wait(p.sync, "S")
            p.stack = st
        gn_done = p.val("V")

        with contextlib.ExitStack() as sd:
            p.stack = sd
            lv = load_rep(w["ln2w"][l:l + 1, :])
            xn2 = p.sb("xn2", [128, 8, T], BF16)
            p.wait(p.sync, "S")
            ln2_done = emit_ln_fm(p, xb, xn2, ident, wrep2=wrep)
            ffup = p.sb("ffup", [128, 8, 2688], BF16)
            ffdn = p.sb("ffdn", [128, 11, 1024], BF16)
            p.wait(p.gpsimd, "V", ln2_done)
            p.inc("W", p.gpsimd.dma_start(
                out=ffup, in_=w["ffup"][l].rearrange("(kc kp) n -> kp kc n", kp=128)), 16)
            p.inc("W", p.gpsimd.dma_start(
                out=ffdn[:, 0:10, :],
                in_=w["ffdn"][l][0:1280, :].rearrange("(kc kp) n -> kp kc n", kp=128)), 16)
            fwl = p.inc("W", p.gpsimd.dma_start(
                out=ffdn[0:64, 10, :], in_=w["ffdn"][l][1280:1344, :]), 16)
            u2 = p.sb("u2", [128, 2688], F32)
            g2 = p.sb("g2", [128, 1344], F32)
            h2 = p.sb("h2", [128, 1344], BF16)
            h2f = [p.sb("h2f", [128, 11, 128], BF16) for _ in range(2)]
            pup = [p.ps("pup", [128, 512], F32) for _ in range(2)]
            ptf = p.ps("ptf", [128, 1408], BF16)
            evac = make_res_evac(p, xb, xnext, ln2_done)
            p.wait(p.tensor, "W", fwl)
            pup_free = ["V", ln2_done, "V", ln2_done]
            ptf_free = ln2_done
            h2f_used = [0, 0]
            pst = [p.ps("pdn", [128, 512], F32) for _ in range(2)]
            dn_free = ["V", ln2_done, "V", ln2_done]
            si = 0
            for tt in range(NTT):
                bb = tt % 2
                for j, n0 in enumerate(range(0, 2688, 512)):
                    nsz = min(512, 2688 - n0)
                    b = si % 2
                    si += 1
                    p.wait(p.tensor, pup_free[2 * b], pup_free[2 * b + 1])
                    for kc in range(8):
                        ins = p.tensor.matmul(
                            pup[b][:, 0:nsz],
                            xn2[:, kc, tt * 128:(tt + 1) * 128],
                            ffup[:, kc, n0:n0 + nsz],
                            start=(kc == 0), stop=(kc == 7))
                    pe = p.inc("P", ins, 1)
                    p.wait(p.vector, "P", pe)
                    cnt = p.vop(p.vector.tensor_copy, u2[:, n0:n0 + nsz],
                                pup[b][:, 0:nsz])
                    pup_free[2 * b], pup_free[2 * b + 1] = "V", cnt
                uv = p.val("V")
                p.wait(p.scalar, "V", uv)
                ag = p.aop(p.scalar.activation, out=g2, in_=u2[:, 0:1344],
                           func=AF.Gelu)
                p.wait(p.vector, "A", ag)
                hv = p.vop(p.vector.tensor_mul, h2, g2, u2[:, 1344:2688])
                p.wait(p.tensor, "V", hv)
                p.wait(p.tensor, "V", ptf_free)
                for c in range(11):
                    if c < 10:
                        ins = p.tensor.transpose(ptf[:, c * 128:(c + 1) * 128],
                                                 h2[:, c * 128:(c + 1) * 128], ident)
                    else:
                        ins = p.tensor.transpose(ptf[0:64, 1280:1408],
                                                 h2[:, 1280:1344], ident)
                pe = p.inc("P", ins, 1)
                p.wait(p.vector, "P", pe)
                p.wait(p.vector, "P", h2f_used[bb])
                ptf_free = p.vop(p.vector.tensor_copy,
                                 h2f[bb].rearrange("p c n -> p (c n)")[:, 0:1408],
                                 ptf.rearrange("p n -> p n"))
                p.wait(p.tensor, "V", ptf_free)
                for n0 in (0, 512):
                    b = si % 2
                    si += 1
                    p.wait(p.tensor, dn_free[2 * b], dn_free[2 * b + 1])
                    for kc in range(11):
                        kk = 128 if kc < 10 else 64
                        ins = p.tensor.matmul(
                            pst[b], h2f[bb][0:kk, kc, :],
                            ffdn[0:kk, kc, n0:n0 + 512],
                            start=(kc == 0), stop=(kc == 10))
                    pe = p.inc("P", ins, 1)
                    if n0 == 512:
                        h2f_used[bb] = pe
                    sem, cnt = evac(tt, n0, pst[b], pe)
                    dn_free[2 * b], dn_free[2 * b + 1] = sem, cnt
            p.wait(p.sync, "S")
            p.stack = st
        p.stack = old


def emit_xlstm(p, x_dram, out_dram, w, scr, ident, subs=(0, 1, 2, 3),
               do_pre=True, do_post=True):
    bufs = [scr["xa"], scr["xb"], scr["xc"]]
    if do_pre:
        emit_ln_tm(p, x_dram, bufs[0])
    cur = 0
    for l in subs:
        emit_slstm_sublayer(p, l, bufs[cur % 3], bufs[(cur + 1) % 3],
                            bufs[(cur + 2) % 3], w, scr, ident)
        cur += 2
    if do_post:
        with contextlib.ExitStack() as st:
            old = p.stack
            p.stack = st
            onesr = p.sb("onesr", [1, 128], F32)
            wrow = p.sb("wrow", [1, D], F32)
            wrep = p.sb("wrep", [128, D], F32)
            pswr = p.ps("pswr", [128, 1024], F32)
            p.vop(p.vector.memset, onesr, 1.0)
            p.wait(p.sync, "V")
            lw = p.inc("L", p.sync.dma_start(out=wrow, in_=w["postw"]), 16)
            p.wait(p.tensor, "L", lw)
            emit_rowrep(p, wrow, wrep, onesr, pswr)
            p.wait(p.sync, "S")
            emit_ln_tm(p, bufs[cur % 3], out_dram, wrep=wrep, brep=None,
                       add_dram=x_dram)
            p.stack = old


def _stage(inputs):
    import ml_dtypes
    g = lambda k: np.asarray(inputs[k])
    S = {}
    S["identb"] = np.eye(128, dtype=ml_dtypes.bfloat16)
    S["identf"] = np.eye(128, dtype=np.float32)
    DHE = 32
    inv = 1.0 / (10000.0 ** (np.arange(0, DHE, 2) / DHE))
    freqs = np.arange(T)[:, None] * inv[None, :]
    emb = np.concatenate([freqs, freqs], axis=1)
    cos, sin = np.cos(emb), np.sin(emb)
    C = np.ones((128, T), np.float32)
    Sx = np.zeros((128, T), np.float32)
    C[32:64] = cos.T; C[96:128] = cos.T
    Sx[32:64] = sin.T; Sx[96:128] = sin.T
    S["cext"] = C; S["sext"] = Sx
    S["cmask"] = np.triu(np.full((128, 128), -1e30, np.float32), 1)
    selp = np.zeros((16, T), np.float32)
    seln = np.zeros((16, T), np.float32)
    for n in range(16):
        selp[n, n * 128:(n + 1) * 128] = 1.0
        seln[n, n * 128:(n + 1) * 128] = -1.0
    S["selp"] = selp; S["seln"] = seln
    S["qdT"] = np.ascontiguousarray(g("L0_attn_a_Wq_down").T)
    S["kdT"] = np.ascontiguousarray(g("L0_attn_a_Wk_down").T)
    S["vdT"] = np.ascontiguousarray(g("L0_attn_a_Wv_down").T)
    for nm in ("q_up_c", "q_up_e", "k_up_c", "k_up_e", "v_up"):
        S[nm] = np.ascontiguousarray(g("L0_attn_a_W" + nm.replace("_up", "_up")))
    S["wc"] = g("L0_attn_a_Wc")
    S["moe_gwT"] = np.ascontiguousarray(g("L0_ffn_gate_w").T)
    S["moe_ewT"] = np.ascontiguousarray(np.transpose(g("L0_ffn_expert_w"), (0, 2, 1)))
    gperm = [0, 2, 1, 3]
    wi, wf, wz, wo = (g("L0_attn_b_wi"), g("L0_attn_b_wf"),
                      g("L0_attn_b_wz"), g("L0_attn_b_wo"))
    wgs = np.stack([wi, wz, wf, wo], axis=2)
    S["xl_wg"] = np.ascontiguousarray(wgs.transpose(0, 1, 4, 2, 3).reshape(4, 4, 256, 1024))
    S["xl_rk"] = np.ascontiguousarray(
        g("L0_attn_b_rker")[:, :, :, gperm, :].reshape(4, 4, 256, 1024))
    S["xl_cb"] = np.ascontiguousarray(
        g("L0_attn_b_cell_b")[:, :, gperm, :].reshape(4, 1, 4096))
    S["xl_convw"] = np.ascontiguousarray(
        g("L0_attn_b_conv_w").reshape(4, 8, 128, 4).transpose(0, 2, 1, 3))
    S["xl_convb"] = np.ascontiguousarray(
        g("L0_attn_b_conv_b").reshape(4, 8, 128).transpose(0, 2, 1))
    S["xl_ln1w"] = g("L0_attn_b_ln1_w")
    S["xl_ln2w"] = g("L0_attn_b_ln2_w")
    S["xl_gnw"] = g("L0_attn_b_gn_w")
    S["xl_postw"] = g("L0_attn_b_post_w")[None, :]
    S["xl_ffup"] = g("L0_attn_b_ff_up")
    S["xl_ffdn"] = g("L0_attn_b_ff_down")
    def fm2(v):
        return np.ascontiguousarray(v.reshape(16, 128).T)
    for i in range(4):
        S["ma%d_inwT" % i] = np.ascontiguousarray(g("L1_attn_a_in_proj_w")[i].T)
        S["ma%d_convw" % i] = np.ascontiguousarray(
            g("L1_attn_a_conv_w")[i].reshape(16, 128, 4).transpose(1, 0, 2))
        S["ma%d_convb" % i] = fm2(g("L1_attn_a_conv_b")[i])
        S["ma%d_xpwT" % i] = np.ascontiguousarray(g("L1_attn_a_x_proj_w")[i].T)
        S["ma%d_dtpwT" % i] = np.ascontiguousarray(g("L1_attn_a_dt_proj_w")[i].T)
        S["ma%d_dtpb" % i] = fm2(g("L1_attn_a_dt_proj_b")[i])
        S["ma%d_alog" % i] = np.ascontiguousarray(
            g("L1_attn_a_A_log")[i].reshape(16, 128, 16).transpose(1, 0, 2))
        S["ma%d_dpar" % i] = fm2(g("L1_attn_a_D")[i])
        S["ma%d_opwT" % i] = np.ascontiguousarray(g("L1_attn_a_out_proj_w")[i].T)
    S["ma_norm_w"] = g("L1_attn_a_norm_w")
    S["ma_norm_b"] = g("L1_attn_a_norm_b")
    S["ma_normf_w"] = g("L1_attn_a_normf_w")[None, :]
    S["ma_normf_b"] = g("L1_attn_a_normf_b")[None, :]
    S["mb_inwT"] = np.ascontiguousarray(g("L1_attn_b_in_proj_w").T)
    S["mb_convw"] = np.ascontiguousarray(
        g("L1_attn_b_conv_w")[:, 0, :].reshape(16, 128, 4).transpose(1, 0, 2))
    S["mb_convb"] = fm2(g("L1_attn_b_conv_b"))
    S["mb_xpwT"] = np.ascontiguousarray(g("L1_attn_b_x_proj_w").T)
    S["mb_dtpwT"] = np.ascontiguousarray(g("L1_attn_b_dt_proj_w").T)
    S["mb_dtpb"] = fm2(g("L1_attn_b_dt_proj_b"))
    S["mb_alog"] = np.ascontiguousarray(
        g("L1_attn_b_A_log").reshape(16, 128, 16).transpose(1, 0, 2))
    S["mb_dpar"] = fm2(g("L1_attn_b_D_param"))
    S["mb_opwT"] = np.ascontiguousarray(g("L1_attn_b_out_proj_w").T)
    S["f_wg"] = np.ascontiguousarray(g("L1_ffn_w_gate").T)
    S["f_wu"] = np.ascontiguousarray(g("L1_ffn_w_up").T)
    S["f_wd"] = np.ascontiguousarray(g("L1_ffn_w_down").T)
    return S


_NPDT = {np.dtype(np.float32): F32, np.dtype(np.int32): I32}


class Launch:

    def __init__(self, S):
        self.p = P()
        self.S = S
        self.ext_in = {}
        self.ext_out = {}
        self.cache = {}

    def W(self, name):
        if name not in self.cache:
            a = self.S[name]
            import ml_dtypes
            dt = _NPDT.get(a.dtype, BF16 if a.dtype == np.dtype(ml_dtypes.bfloat16) else F32)
            self.cache[name] = self.p.nc.dram_tensor(
                "w_" + name, list(a.shape), dt, kind="ExternalInput").ap()
            self.ext_in[name] = a
        return self.cache[name]

    def buf_in(self, name):
        if name not in self.cache:
            self.cache[name] = self.p.nc.dram_tensor(
                name, [T, D], F32, kind="ExternalInput").ap()
            self.ext_in[name] = None
        return self.cache[name]

    def buf_out(self, name):
        if name not in self.cache:
            self.cache[name] = self.p.nc.dram_tensor(
                name, [T, D], F32, kind="ExternalOutput").ap()
            self.ext_out[name] = True
        return self.cache[name]

    def buf_int(self, name):
        if name not in self.cache:
            self.cache[name] = self.p.dram(name, [T, D], F32)
        return self.cache[name]

    def idents(self):
        p = self.p
        identb = p.sb("identb", [128, 128], BF16)
        identf = p.sb("identf", [128, 128], F32)
        p.inc("L", p.sync.dma_start(out=identb, in_=self.W("identb")), 16)
        li = p.inc("L", p.sync.dma_start(out=identf, in_=self.W("identf")), 16)
        for e in (p.tensor, p.vector, p.scalar, p.gpsimd):
            p.wait(e, "L", li)
        return identb, identf


def _blk_attn(L, xin, xout):
    w = {nm: L.W(nm) for nm in ("qdT", "kdT", "vdT", "q_up_c", "q_up_e",
                                "k_up_c", "k_up_e", "v_up", "wc", "cext",
                                "sext", "cmask")}
    emit_attn(L.p, xin, xout, w, L.identb, L.identf)


def _blk_moe(L, xin, xout):
    emit_moe(L.p, xin, xout, L.W("moe_gwT"), L.W("moe_ewT"), L.identb, L.identf)


def _blk_xlstm(L, xin, xout):
    p = L.p
    w = {"ln1w": L.W("xl_ln1w"), "ln2w": L.W("xl_ln2w"), "gnw": L.W("xl_gnw"),
         "postw": L.W("xl_postw"), "convw": L.W("xl_convw"),
         "convb": L.W("xl_convb"), "wg": L.W("xl_wg"), "rk": L.W("xl_rk"),
         "cb": L.W("xl_cb"), "ffup": L.W("xl_ffup"), "ffdn": L.W("xl_ffdn")}
    scr = {"wx": p.dram("xl_wx", [T, 4096], BF16),
           "xa": L.buf_int("xl_xa"), "xb": L.buf_int("xl_xb"),
           "xc": L.buf_int("xl_xc")}
    emit_xlstm(p, xin, xout, w, scr, L.identb)


def _mamba_scr(p):
    return {
        "z": p.dram("scr_z", [128, 16, NTT, 128], BF16),
        "nd": p.dram("scr_nd", [128, 16, NTT, 128], BF16),
        "brep": p.dram("scr_br", [128, 16, NTT, 128], BF16),
        "crep": p.dram("scr_cr", [128, 16, NTT, 128], BF16),
        "y": p.dram("scr_y", [128, 16, T], BF16),
    }


def _blk_mamba_a(L, xin, xout):
    p = L.p
    wl = []
    for i in range(4):
        d = {k: L.W("ma%d_%s" % (i, k)) for k in
             ("inwT", "convw", "convb", "xpwT", "dtpwT", "dtpb", "alog",
              "dpar", "opwT")}
        d["selp"] = L.W("selp")
        d["seln"] = L.W("seln")
        wl.append(d)
    wn = {"norm_w": L.W("ma_norm_w"), "norm_b": L.W("ma_norm_b"),
          "normf_w": L.W("ma_normf_w"), "normf_b": L.W("ma_normf_b")}
    scr = _mamba_scr(p)
    scr["res0"] = L.buf_int("ma_res0")
    scr["res1"] = L.buf_int("ma_res1")
    emit_l1_attn_a(p, xin, xout, wl, wn, scr, L.identb)


def _blk_mamba_b(L, xin, xout):
    p = L.p
    w = {k: L.W("mb_" + k) for k in ("inwT", "convw", "convb", "xpwT",
                                     "dtpwT", "dtpb", "alog", "dpar", "opwT")}
    w["selp"] = L.W("selp")
    w["seln"] = L.W("seln")
    emit_l1_attn_b(p, xin, xout, w, _mamba_scr(p), L.identb)


def _blk_l1ffn(L, xin, xout):
    emit_l1ffn(L.p, xin, xout, L.W("f_wg"), L.W("f_wu"), L.W("f_wd"), L.identb)


_CHAIN = [
    ("x0", _blk_attn, "x1"),
    ("x1", _blk_moe, "x2"),
    ("x2", _blk_xlstm, "x3"),
    ("x3", _blk_moe, "x4"),
    ("x4", _blk_mamba_a, "x5"),
    ("x5", _blk_l1ffn, "x6"),
    ("x6", _blk_mamba_b, "x7"),
    ("x7", _blk_l1ffn, "x8"),
]

GROUPS = [[0, 1, 2, 3, 4, 5, 6, 7]]


def kernel(**inputs):
    S = _stage(inputs)
    x_full = np.asarray(inputs["L0_attn_a_x"], np.float32)
    ncore = 8
    bufs = {"x0": [np.ascontiguousarray(x_full[c]) for c in range(ncore)]}
    for group in GROUPS:
        L = Launch(S)
        names_in = _CHAIN[group[0]][0]
        names_out = _CHAIN[group[-1]][2]
        xin_ap = L.buf_in(names_in)
        L.identb, L.identf = L.idents()
        cur_ap = xin_ap
        first = True
        for bi in group:
            _, fn, outname = _CHAIN[bi]
            if not first:
                L.p.fence_all()
            L.p.sem_prefix = "b%d_" % bi
            first = False
            is_last = (bi == group[-1])
            out_ap = L.buf_out(outname) if is_last else L.buf_int(outname)
            fn(L, cur_ap, out_ap)
            cur_ap = out_ap
        L.p.sem_prefix = ""
        in_maps = []
        for c in range(ncore):
            m = {}
            for nm, arr in L.ext_in.items():
                if arr is None:
                    m[nm] = bufs[nm][c]
                else:
                    m["w_" + nm] = arr
            in_maps.append(m)
        res = run_bass_kernel_spmd(L.p.nc, in_maps, core_ids=list(range(ncore)))
        for nm in L.ext_out:
            bufs[nm] = [res.results[c][nm] for c in range(ncore)]
    return np.stack(bufs["x8"], axis=0)
